# version 09
# baseline (speedup 1.0000x reference)
import numpy as np
import ml_dtypes
from contextlib import ExitStack
import concourse.bass as bass
import concourse.mybir as mybir
from concourse.bass_utils import run_bass_kernel_spmd

F32 = mybir.dt.float32
BF16 = mybir.dt.bfloat16
U8 = mybir.dt.uint8
AF = mybir.ActivationFunctionType
OP = mybir.AluOpType
AX = mybir.AxisListType

D = 2048
L = 2048
DIN = 4096
H = 64
DH = 64
NST = 128
G = 8
GN = G * NST
CONV = DIN + 2 * GN
NPROJ = DIN + H + 2 * GN + DIN
DFF = 8192
NK = D // 128
NCH = 16
CK = 128
MAGIC = 12582912.0
BIGM = 30000.0

ENGINES = ("sync", "tensor", "vector", "scalar", "gpsimd")

N_PHASES = 7
DEBUG = False


class Sched:
    def __init__(self):
        self.q = {e: [] for e in ENGINES}
        self.cnt = {}
        self.names = []

    def sem(self, *names):
        for n in names:
            self.cnt[n] = 0
            self.names.append(n)

    def w(self, eng, s, v=None):
        v = self.cnt[s] if v is None else v
        if v > 0:
            self.q[eng].append(("w", s, v))

    def o(self, eng, fn, inc=None, n=1):
        self.q[eng].append(("o", fn, inc, n))
        if inc is not None:
            self.cnt[inc] += n
            return self.cnt[inc]
        return None

    def barrier(self):
        snap = [(s, self.cnt[s]) for s in self.names if self.cnt[s] > 0]
        for e in ENGINES:
            for s, v in snap:
                self.q[e].append(("w", s, v))


class DGroup:

    def __init__(self, p, base, n=6):
        self.p = p
        self.names = [f"{base}{i}" for i in range(n)]
        p.sem(*self.names)
        self.i = 0
        self.waited = {s: 0 for s in self.names}

    def issue(self, eng, out_ap, in_ap, **kw):
        s = self.names[self.i % len(self.names)]
        self.i += 1
        if self.p.cnt[s] > self.waited[s]:
            self.p.w(eng, s, self.p.cnt[s])
            self.waited[s] = self.p.cnt[s]
        self.p.o(eng,
                 lambda e, o=out_ap, i=in_ap, kw=kw: e.dma_start(out=o, in_=i, **kw),
                 s, 16)
        return s, self.p.cnt[s]

    def wait_last(self, eng):
        s = self.names[(self.i - 1) % len(self.names)]
        self.p.w(eng, s, self.p.cnt[s])

    def waitall(self, eng, issuer=False):
        for s in self.names:
            self.p.w(eng, s, self.p.cnt[s])
            if issuer:
                self.waited[s] = max(self.waited[s], self.p.cnt[s])


def _sz(dt):
    return {F32: 4, BF16: 2}[dt]


def build(n_phases=N_PHASES, debug=False):
    nc = bass.Bass("TRN2", target_bir_lowering=False)

    xT = nc.dram_tensor("xT", [D, L], F32, kind="ExternalInput").ap()
    w_in = nc.dram_tensor("w_in", [D, NPROJ], F32, kind="ExternalInput").ap()
    w1t = nc.dram_tensor("w1t", [D, DFF], F32, kind="ExternalInput").ap()
    w2t = nc.dram_tensor("w2t", [DFF, D], F32, kind="ExternalInput").ap()
    wot = nc.dram_tensor("wot", [DIN, D], F32, kind="ExternalInput").ap()
    convw = nc.dram_tensor("convw", [CONV, 4], F32, kind="ExternalInput").ap()
    convb = nc.dram_tensor("convb", [CONV], F32, kind="ExternalInput").ap()
    convnw = nc.dram_tensor("convnw", [CONV], F32, kind="ExternalInput").ap()
    normw = nc.dram_tensor("normw", [D], F32, kind="ExternalInput").ap()
    outnw = nc.dram_tensor("outnw", [DIN], F32, kind="ExternalInput").ap()
    alog = nc.dram_tensor("alog", [H], F32, kind="ExternalInput").ap()
    dtb = nc.dram_tensor("dtb", [H], F32, kind="ExternalInput").ap()
    dtwt = nc.dram_tensor("dtwt", [H, H], F32, kind="ExternalInput").ap()
    outT = nc.dram_tensor("outT", [D, L], F32, kind="ExternalOutput").ap()

    r_s = nc.dram_tensor("r_s", [NK, 128, L], F32, kind="Internal").ap()
    craw_s = nc.dram_tensor("craw_s", [CONV // 128, 128, L], BF16, kind="Internal").ap()
    gate_s = nc.dram_tensor("gate_s", [DIN // 128, 128, L], BF16, kind="Internal").ap()
    a_s = nc.dram_tensor("a_s", [DFF // 128, 128, L], BF16, kind="Internal").ap()
    yg_s = nc.dram_tensor("yg_s", [NCH, DIN // 128, 128, CK], BF16, kind="Internal").ap()
    out1_s = nc.dram_tensor("out1_s", [NK, 128, L], F32, kind="Internal").ap()
    w2bf_s = nc.dram_tensor("w2bf_s", [NK, 128, 64 * 128], BF16, kind="Internal").ap()

    dbg = {}
    if debug:
        dbg["dbg_xn"] = nc.dram_tensor("dbg_xn", [D, L], BF16, kind="ExternalOutput").ap()
        dbg["dbg_r"] = nc.dram_tensor("dbg_r", [D, L], F32, kind="ExternalOutput").ap()
        dbg["dbg_craw"] = nc.dram_tensor("dbg_craw", [48, 128, L], BF16, kind="ExternalOutput").ap()
        dbg["dbg_gate"] = nc.dram_tensor("dbg_gate", [32, 128, L], BF16, kind="ExternalOutput").ap()
        dbg["dbg_adt"] = nc.dram_tensor("dbg_adt", [64, L], F32, kind="ExternalOutput").ap()
        dbg["dbg_crms"] = nc.dram_tensor("dbg_crms", [128, L], BF16, kind="ExternalOutput").ap()
        dbg["dbg_a"] = nc.dram_tensor("dbg_a", [DFF // 128, 128, L], BF16, kind="ExternalOutput").ap()
        dbg["dbg_yg"] = nc.dram_tensor("dbg_yg", [NCH, 128, DIN], BF16, kind="ExternalOutput").ap()
        dbg["dbg_y"] = nc.dram_tensor("dbg_y", [NCH, 128, DIN], BF16, kind="ExternalOutput").ap()
        dbg["dbg_acs"] = nc.dram_tensor("dbg_acs", [64, L], F32, kind="ExternalOutput").ap()
        dbg["dbg_h"] = nc.dram_tensor("dbg_h", [128, H * DH], F32, kind="ExternalOutput").ap()
        dbg["dbg_lex"] = nc.dram_tensor("dbg_lex", [8, 128, 128], BF16, kind="ExternalOutput").ap()
        dbg["dbg_mcb"] = nc.dram_tensor("dbg_mcb", [8, 128, 128], BF16, kind="ExternalOutput").ap()
        dbg["dbg_xtok"] = nc.dram_tensor("dbg_xtok", [128, DIN], BF16, kind="ExternalOutput").ap()
        dbg["dbg_cbs"] = nc.dram_tensor("dbg_cbs", [128, 128], BF16, kind="ExternalOutput").ap()

    ones_col = nc.alloc_sbuf_tensor("ones_col", [128, 1], BF16)
    ones_row = nc.alloc_sbuf_tensor("ones_row", [1, 128], F32)
    onesb = nc.alloc_sbuf_tensor("onesb", [128, 128], BF16)
    ones64 = nc.alloc_sbuf_tensor("ones64", [64, 128], BF16)
    ones64f = nc.alloc_sbuf_tensor("ones64f", [64, 128], F32)
    id64 = nc.alloc_sbuf_tensor("id64", [64, 64], BF16)
    nid64 = nc.alloc_sbuf_tensor("nid64", [64, 64], BF16)
    idb = nc.alloc_sbuf_tensor("idb", [128, 128], BF16)
    c1m = nc.alloc_sbuf_tensor("c1m", [128, 128], BF16)
    c2m = nc.alloc_sbuf_tensor("c2m", [128, 128], BF16)
    cst = nc.alloc_sbuf_tensor("cst", [128, 4], F32)
    normw_kp = nc.alloc_sbuf_tensor("normw_kp", [128, NK], F32)
    convw_kp = nc.alloc_sbuf_tensor("convw_kp", [128, 192], F32)
    convb_kp = nc.alloc_sbuf_tensor("convb_kp", [128, 48], F32)
    convnw_kp = nc.alloc_sbuf_tensor("convnw_kp", [128, 48], F32)
    outnw_kp = nc.alloc_sbuf_tensor("outnw_kp", [128, 32], F32)
    norm_slab = nc.alloc_sbuf_tensor("norm_slab", [128, D], BF16)
    outnw_slab = nc.alloc_sbuf_tensor("outnw_slab", [128, DIN], BF16)
    alog_sb = nc.alloc_sbuf_tensor("alog_sb", [64, 1], F32)
    dtb_sb = nc.alloc_sbuf_tensor("dtb_sb", [64, 1], F32)
    dtw_sb = nc.alloc_sbuf_tensor("dtw_sb", [64, 64], F32)
    negea = nc.alloc_sbuf_tensor("negea", [64, 1], F32)

    AB = 196096
    arena = nc.alloc_sbuf_tensor("arena", [128, AB], U8)

    def V(off, elems, dt, p=128):
        assert off % 64 == 0 and off + elems * _sz(dt) <= AB, (off, elems, dt)
        return arena[0:p, off:off + elems * _sz(dt)].bitcast(dt)

    SC = 256
    cact = [V(0, 48 * SC, BF16)]
    gact = [V(24576, 32 * SC, BF16)]
    xtok = V(40960, DIN, BF16)
    xdec = V(49152, DIN, BF16)
    btok = V(57344, GN, BF16)
    ytok = V(59392, DIN, BF16)
    lexp = V(67584, 512, BF16)
    mcbq = [V(68608, 512, BF16), V(69632, 512, BF16)]
    cbs = [V(70656, 128, BF16), V(70912, 128, BF16)]
    eas = V(71168, 512, BF16)
    cdl_t = V(72192, 128, BF16, p=64)
    ysc = V(72448, 512, BF16)
    yrms = V(73472, 16, F32)
    awk = V(73536, 64, F32, p=64)
    htmp = V(73792, 512, F32)
    crmsb = V(75840, L, BF16)
    hi_t = V(79936, CK, BF16, p=64)
    lo_t = V(80192, CK, BF16, p=64)
    exp_t = V(80448, CK, BF16, p=64)
    dec_t = V(80704, CK, BF16, p=64)
    acs = V(80960, L, F32, p=64)
    h32 = V(89152, H * DH, F32)
    hbf = V(105536, H * DH, BF16)
    a_dt = V(89152, L, F32, p=64)
    wk1 = V(97344, 512, F32, p=64)
    wk2 = V(99392, 512, F32, p=64)
    wk3 = V(101440, 512, F32, p=64)
    xn = [V(113728 + k * 4096, D, BF16) for k in range(NK)]
    w1b = [V(179264, D, BF16), V(183360, D, BF16)]
    qf = V(187456, 512, F32)
    qt = V(189504, 512, F32)
    astage = [V(191552, L, BF16)]
    exp_tok = V(195648, 64, BF16)
    cdx = V(195776, 64, BF16)
    s_b = V(0, L, F32)
    ms_b = V(8192, L, F32)
    xstage = [V(16384, L, F32), V(24576, L, F32)]
    rstage = [V(32768, L, F32), V(40960, L, F32)]
    xbf_t = [V(49152, L, BF16), V(53248, L, BF16)]
    sq_t = [V(57344, L, BF16), V(61440, L, BF16)]
    rowm = V(65536, L, F32, p=1)
    rowe = V(73728, L, F32, p=1)
    rows_ = V(81920, L, F32, p=1)
    rowt = V(90112, L, F32, p=1)
    rmsf_b = V(0, L, F32)
    wcast = [V(8192, D, BF16), V(12288, D, BF16), V(98304, D, BF16)]
    wf = [V(16384, D, BF16), V(20480, D, BF16), V(102400, D, BF16)]
    cslab = V(24576, 2051, BF16)
    ct0 = V(28736, L, BF16)
    ct1 = V(32832, L, BF16)
    ct2 = V(36928, L, BF16)
    csq = V(41024, L, BF16)
    crawt = V(45120, L, BF16)
    gtmp = V(28736, L, F32)
    rowcv = V(49216, L, F32, p=1)
    rowcv2 = V(57408, L, F32, p=1)
    atile = [V(k * 2048, 1024, BF16) for k in range(64)]
    ygt = [V(k * 4096, L, BF16) for k in range(32)]
    w2slab = [V(131072, 64 * 128, BF16), V(147456, 64 * 128, BF16)]
    woslab = [V(131072, 32 * 128, BF16), V(139264, 32 * 128, BF16)]
    ochunk = V(163840, 2048, F32)
    rchunk = V(172032, 2048, F32)

    p = Sched()
    p.sem("tE", "vE", "sE", "gE")
    gS = DGroup(p, "dS")
    gG = DGroup(p, "dG")
    gR = DGroup(p, "dR")
    gD = DGroup(p, "dD")

    if n_phases >= 1:
        gS.issue("sync", normw_kp[:, :], normw.rearrange("(k p) -> p k", p=128),
                 allow_slow_non_contiguous=True)
        gS.issue("sync", convw_kp[:, :].rearrange("p (k j) -> p k j", j=4),
                 convw.rearrange("(k p) j -> p k j", p=128),
                 allow_slow_non_contiguous=True)
        gS.issue("sync", convb_kp[:, :], convb.rearrange("(k p) -> p k", p=128),
                 allow_slow_non_contiguous=True)
        gS.issue("sync", convnw_kp[:, :], convnw.rearrange("(k p) -> p k", p=128),
                 allow_slow_non_contiguous=True)
        gS.issue("sync", outnw_kp[:, :], outnw.rearrange("(k p) -> p k", p=128),
                 allow_slow_non_contiguous=True)
        gS.issue("sync", alog_sb[0:64, :], alog.rearrange("(h o) -> h o", o=1))
        gS.issue("sync", dtb_sb[0:64, :], dtb.rearrange("(h o) -> h o", o=1))
        gS.issue("sync", dtw_sb[0:64, :], dtwt)

        def g_consts_a(e):
            e.memset(ones_col[:, :], 1.0)
            e.memset(onesb[:, :], 1.0)
            e.memset(ones64[0:64, :], 1.0)
            e.memset(ones64f[0:64, :], 1.0)
            e.memset(ones_row[:, :], 1.0)
            e.memset(cst[:, 0:1], 1e-5)
            e.memset(cst[:, 1:2], 1e-6)
            e.memset(cst[:, 2:3], 1.0)
            e.memset(id64[0:64, :], 1.0)
            e.memset(idb[:, :], 1.0)
            e.memset(nid64[0:64, :], -1.0)
            e.memset(c1m[:, :], 1.0)
            return e.memset(c2m[:, :], -BIGM)
        p.o("gpsimd", g_consts_a, "gE")
        p.w("gpsimd", "gE")

        def g_consts_b(e):
            e.affine_select(out=id64[0:64, :], in_=id64[0:64, :], pattern=[[-1, 64]],
                            compare_op=OP.is_equal, fill=0.0, base=0, channel_multiplier=1)
            e.affine_select(out=idb[:, :], in_=idb[:, :], pattern=[[-1, 128]],
                            compare_op=OP.is_equal, fill=0.0, base=0, channel_multiplier=1)
            e.affine_select(out=nid64[0:64, :], in_=nid64[0:64, :], pattern=[[-1, 64]],
                            compare_op=OP.is_equal, fill=0.0, base=0, channel_multiplier=1)
            e.affine_select(out=c1m[:, :], in_=c1m[:, :], pattern=[[1, 128]],
                            compare_op=OP.is_ge, fill=0.0, base=-1, channel_multiplier=-1)
            return e.affine_select(out=c2m[:, :], in_=c2m[:, :], pattern=[[-1, 128]],
                                   compare_op=OP.is_ge, fill=0.0, base=0, channel_multiplier=1)
        p.o("gpsimd", g_consts_b, "gE")

        gS.waitall("scalar")
        p.o("scalar", lambda e: e.activation(negea[0:64, :], alog_sb[0:64, :], AF.Exp),
            "sE")
        gS.waitall("vector")
        p.w("vector", "gE")
        p.w("vector", "sE")

        def v_slabs(e):
            for k in range(NK):
                e.tensor_scalar(norm_slab[:, k * 128:(k + 1) * 128], onesb[:, :],
                                normw_kp[:, k:k + 1], None, OP.mult)
            for k in range(32):
                e.tensor_scalar(outnw_slab[:, k * 128:(k + 1) * 128], onesb[:, :],
                                outnw_kp[:, k:k + 1], None, OP.mult)
            return e.tensor_scalar(negea[0:64, :], negea[0:64, :], -1.0, None, OP.mult)
        p.o("vector", v_slabs, "vE")

    if n_phases >= 6:
        for m in range(NK):
            gG.issue("gpsimd",
                     w2bf_s[m].rearrange("p (k c) -> p k c", c=128),
                     w2t[:, m * 128:(m + 1) * 128].rearrange("(k p) c -> p k c",
                                                             p=128))

    pb = [nc.alloc_psum_tensor(f"pb{i}", [128, 512], F32) for i in range(8)]

    if n_phases >= 2:
        p.w("tensor", "gE")
        for k in range(NK):
            if k >= 2:
                p.w("sync", "sE", p.cnt["sE"] - 1)
            sv = gS.issue("sync", xstage[k % 2][:, :], xT[k * 128:(k + 1) * 128, :])
            p.w("scalar", *sv)
            if k >= 2:
                p.w("scalar", "tE", p.cnt["tE"] - 1)

            def s_cast(e, k=k):
                e.activation(xbf_t[k % 2][:, :], xstage[k % 2][:, :], AF.Copy)
                return e.activation(sq_t[k % 2][:, :], xstage[k % 2][:, :], AF.Square)
            p.o("scalar", s_cast, "sE")
            p.w("tensor", "sE")

            def t_stats(e, k=k):
                for tb in range(4):
                    e.matmul(pb[tb][0:1, :], ones_col[:, :],
                             xbf_t[k % 2][:, tb * 512:(tb + 1) * 512],
                             start=(k == 0), stop=(k == NK - 1))
                last = None
                for tb in range(4):
                    last = e.matmul(pb[4 + tb][0:1, :], ones_col[:, :],
                                    sq_t[k % 2][:, tb * 512:(tb + 1) * 512],
                                    start=(k == 0), stop=(k == NK - 1))
                return last
            p.o("tensor", t_stats, "tE")

        def vch(fn):
            p.w("vector", "vE")
            p.o("vector", fn, "vE")

        def sch(fn):
            p.w("scalar", "vE")
            p.o("scalar", fn, "sE")
            p.w("vector", "sE")

        p.w("vector", "tE")

        def v_fin1(e):
            last = None
            for tb in range(4):
                e.tensor_scalar(rowm[0:1, tb * 512:(tb + 1) * 512], pb[tb][0:1, :],
                                1.0 / D, None, OP.mult)
                last = e.tensor_scalar(rowe[0:1, tb * 512:(tb + 1) * 512],
                                       pb[4 + tb][0:1, :], 1.0 / D, None, OP.mult)
            return last
        p.o("vector", v_fin1, "vE")
        vch(lambda e: e.tensor_tensor(rowt[0:1, :], rowm[0:1, :], rowm[0:1, :], OP.mult))
        vch(lambda e: e.tensor_tensor(rowe[0:1, :], rowe[0:1, :], rowt[0:1, :], OP.subtract))
        sch(lambda e: e.activation(rowt[0:1, :], rowe[0:1, :], AF.Sqrt, bias=cst[0:1, 0:1]))
        vch(lambda e: e.reciprocal(rows_[0:1, :], rowt[0:1, :]))
        vch(lambda e: e.tensor_tensor(rowt[0:1, :], rowe[0:1, :], rows_[0:1, :], OP.mult))
        vch(lambda e: e.tensor_tensor(rowt[0:1, :], rowt[0:1, :], rows_[0:1, :], OP.mult))
        sch(lambda e: e.activation(rowt[0:1, :], rowt[0:1, :], AF.Sqrt, bias=cst[0:1, 1:2]))
        vch(lambda e: e.reciprocal(rowe[0:1, :], rowt[0:1, :]))
        vch(lambda e: e.tensor_tensor(rowm[0:1, :], rowm[0:1, :], rows_[0:1, :], OP.mult))

        for row, dst in ((rows_, s_b), (rowm, ms_b)):
            p.w("tensor", "vE")

            def t_bc(e, row=row):
                last = None
                for tb in range(4):
                    last = e.matmul(pb[tb][:, :], ones_row[:, :],
                                    row[0:1, tb * 512:(tb + 1) * 512],
                                    start=True, stop=True)
                return last
            p.o("tensor", t_bc, "tE")
            p.w("vector", "tE")

            def v_bc(e, dst=dst):
                last = None
                for tb in range(4):
                    last = e.tensor_copy(out=dst[:, tb * 512:(tb + 1) * 512],
                                         in_=pb[tb][:, :])
                return last
            p.o("vector", v_bc, "vE")

        xcons = []
        rsems = []
        p.w("sync", "vE")
        p.w("sync", "sE")
        for k in range(NK):
            if k >= 2:
                p.w("sync", "vE", xcons[k - 2])
            sv = gS.issue("sync", xstage[k % 2][:, :], xT[k * 128:(k + 1) * 128, :])
            p.w("vector", *sv)
            if k >= 2:
                for pr in rsems[k - 2]:
                    p.w("vector", *pr)

            def v_ln1(e, k=k):
                return e.tensor_tensor(rstage[k % 2][:, :], xstage[k % 2][:, :],
                                       s_b[:, :], OP.mult)
            p.o("vector", v_ln1, "vE")
            p.w("vector", "vE")

            def v_ln2a(e, k=k):
                return e.tensor_tensor(xn[k][:, :], rstage[k % 2][:, :], ms_b[:, :],
                                       OP.subtract)
            p.o("vector", v_ln2a, "vE")
            p.w("vector", "vE")

            def v_ln2b(e, k=k):
                return e.tensor_tensor(rstage[k % 2][:, :], rstage[k % 2][:, :],
                                       xstage[k % 2][:, :], OP.add)
            xcons.append(p.o("vector", v_ln2b, "vE"))
            p.w("vector", "vE")

            def v_ln3(e, k=k):
                return e.tensor_tensor(rstage[k % 2][:, :], rstage[k % 2][:, :],
                                       ms_b[:, :], OP.subtract)
            p.o("vector", v_ln3, "vE")
            p.w("sync", "vE")
            prs = []
            if debug:
                gD.issue("sync", dbg["dbg_xn"][k * 128:(k + 1) * 128, :], xn[k][:, :])
                prs.append(gR.issue("sync", dbg["dbg_r"][k * 128:(k + 1) * 128, :],
                                    rstage[k % 2][:, :]))
            prs.append(gR.issue("sync", r_s[k], rstage[k % 2][:, :]))
            rsems.append(prs)
        gR.waitall("sync")
        gD.waitall("sync")

    if n_phases >= 3:
        NM = 81
        p.w("vector", "sE")
        p.w("vector", "tE")
        p.w("gpsimd", "vE")
        gR.waitall("vector")
        gD.waitall("vector")
        gW = DGroup(p, "dW")

        p.w("tensor", "vE")

        def t_bcf(e):
            last = None
            for tb in range(4):
                last = e.matmul(pb[tb][:, :], ones_row[:, :],
                                rowe[0:1, tb * 512:(tb + 1) * 512],
                                start=True, stop=True)
            return last
        p.o("tensor", t_bcf, "tE")
        p.w("vector", "tE")

        def v_bcf(e):
            last = None
            for tb in range(4):
                last = e.tensor_copy(out=rmsf_b[:, tb * 512:(tb + 1) * 512],
                                     in_=pb[tb][:, :])
            return last
        p.o("vector", v_bcf, "vE")
        p.w("vector", "vE")
        p.o("vector", lambda e: e.memset(cslab[:, 0:3], 0.0), "vE")

        wsem, vfold, tmm, vevac = [], [], [], []
        statc = {}
        statt = {}
        cstore, gstore = [], []
        nslab64 = norm_slab[:, :].rearrange("p (k c) -> p k c", c=128)[:, :, 0:64]

        for m in range(NM):
            c0 = m * 128
            w = 64 if m == NM - 1 else 128
            if m >= 3:
                p.w("gpsimd", "vE", vfold[m - 3])
            src_ap = w_in[:, c0:c0 + w].rearrange("(k p) c -> p k c", p=128)
            dst_ap = wcast[m % 3][:, 0:NK * w].rearrange("p (k c) -> p k c", c=w)
            wsem.append(gG.issue("gpsimd", dst_ap, src_ap))

            p.w("vector", *wsem[m])
            if m >= 3:
                p.w("vector", "tE", tmm[m - 3])

            def v_fold(e, m=m, w=w):
                if w == 128:
                    return e.tensor_tensor(wf[m % 3][:, :], wcast[m % 3][:, :],
                                           norm_slab[:, :], OP.mult)
                return e.tensor_tensor(
                    wf[m % 3][:, 0:NK * w].rearrange("p (k c) -> p k c", c=w),
                    wcast[m % 3][:, 0:NK * w].rearrange("p (k c) -> p k c", c=w),
                    nslab64, OP.mult)
            vfold.append(p.o("vector", v_fold, "vE"))

            p.w("tensor", "vE", vfold[m])
            if m >= 1:
                p.w("tensor", "vE", vevac[m - 1])

            def t_mm(e, m=m, w=w):
                last = None
                for k in range(NK):
                    for tb in range(4):
                        last = e.matmul(pb[tb][0:w, :],
                                        wf[m % 3][:, k * w:(k + 1) * w],
                                        xn[k][:, tb * 512:(tb + 1) * 512],
                                        start=(k == 0), stop=(k == NK - 1))
                return last
            tmm.append(p.o("tensor", t_mm, "tE"))
            if 1 <= m <= 48:
                ms = m - 1
                p.w("tensor", "sE", statc[ms])

                def t_stat(e, ms=ms):
                    last = None
                    for tb in range(4):
                        last = e.matmul(pb[4 + tb][0:1, :], ones_col[:, :],
                                        csq[:, tb * 512:(tb + 1) * 512],
                                        start=(ms == 0), stop=(ms == 47))
                    return last
                statt[ms] = p.o("tensor", t_stat, "tE")

            p.w("vector", "tE", tmm[m])
            if m < 48:
                def v_ev(e, m=m):
                    last = None
                    for tb in range(4):
                        last = e.tensor_tensor(
                            cslab[:, 3 + tb * 512:3 + (tb + 1) * 512],
                            pb[tb][:, :], rmsf_b[:, tb * 512:(tb + 1) * 512], OP.mult)
                    return last
                vevac.append(p.o("vector", v_ev, "vE"))

                p.w("vector", "vE")
                p.o("vector", lambda e, m=m: e.tensor_scalar(
                    ct0[:, :], cslab[:, 0:L], convw_kp[:, 4 * m:4 * m + 1],
                    None, OP.mult), "vE")
                p.w("vector", "vE")
                if m >= 1:
                    for pr in cstore[m - 1]:
                        p.w("vector", *pr)
                p.o("vector", lambda e, m=m: e.scalar_tensor_tensor(
                    ct1[:, :], cslab[:, 1:1 + L], convw_kp[:, 4 * m + 1:4 * m + 2],
                    ct0[:, :], OP.mult, OP.add), "vE")
                p.w("vector", "vE")
                p.o("vector", lambda e, m=m: e.scalar_tensor_tensor(
                    ct2[:, :], cslab[:, 2:2 + L], convw_kp[:, 4 * m + 2:4 * m + 3],
                    ct1[:, :], OP.mult, OP.add), "vE")
                p.w("vector", "vE")
                if m >= 1:
                    p.w("vector", "sE", statc[m - 1])
                p.o("vector", lambda e, m=m: e.scalar_tensor_tensor(
                    ct0[:, :], cslab[:, 3:3 + L], convw_kp[:, 4 * m + 3:4 * m + 4],
                    ct2[:, :], OP.mult, OP.add), "vE")
                p.w("vector", "vE")
                p.o("vector", lambda e, m=m: e.tensor_scalar(
                    crawt[:, :], ct0[:, :], convb_kp[:, m:m + 1], None, OP.add), "vE")

                p.w("scalar", "vE")
                if m >= 1:
                    p.w("scalar", "tE", statt[m - 1])
                statc[m] = p.o("scalar", lambda e: e.activation(
                    csq[:, :], crawt[:, :], AF.Square), "sE")
                p.w("vector", "vE")
                p.o("vector", lambda e, m=m: e.tensor_scalar(
                    ct1[:, :], crawt[:, :], convnw_kp[:, m:m + 1], None, OP.mult), "vE")
                p.w("sync", "vE")
                prs = [gW.issue("sync", craw_s[m], ct1[:, :])]
                if debug:
                    prs.append(gD.issue("sync", dbg["dbg_craw"][m], ct1[:, :]))
                cstore.append(prs)
            elif m < 80:
                mg = m - 48
                p.w("vector", "vE", p.cnt["vE"])
                if mg >= 1:
                    p.w("vector", "sE", gsil[mg - 1])
                else:
                    for pr in cstore[-1]:
                        p.w("vector", *pr)

                def v_evg(e):
                    last = None
                    for tb in range(4):
                        last = e.tensor_tensor(
                            gtmp[:, tb * 512:(tb + 1) * 512],
                            pb[tb][:, :], rmsf_b[:, tb * 512:(tb + 1) * 512], OP.mult)
                    return last
                vevac.append(p.o("vector", v_evg, "vE"))
                p.w("scalar", "vE")
                p.w("scalar", "sE", p.cnt["sE"])
                if mg >= 1:
                    for pr in gstore[mg - 1]:
                        p.w("scalar", *pr)
                elif cstore:
                    for pr in cstore[-1]:
                        p.w("scalar", *pr)
                if mg == 0:
                    gsil = []
                gsil.append(p.o("scalar",
                                lambda e: e.activation(crawt[:, :], gtmp[:, :], AF.Silu),
                                "sE"))
                p.w("sync", "sE")
                prs = [gW.issue("sync", gate_s[mg], crawt[:, :])]
                if debug:
                    prs.append(gD.issue("sync", dbg["dbg_gate"][mg], crawt[:, :]))
                gstore.append(prs)
            else:
                def v_eva(e):
                    last = None
                    for tb in range(4):
                        last = e.tensor_tensor(
                            a_dt[0:64, tb * 512:(tb + 1) * 512],
                            pb[tb][0:64, :], rmsf_b[0:64, tb * 512:(tb + 1) * 512],
                            OP.mult)
                    return last
                vevac.append(p.o("vector", v_eva, "vE"))
                if debug:
                    p.w("sync", "vE")
                    gD.issue("sync", dbg["dbg_adt"], a_dt[0:64, :])

        p.w("vector", "tE")

        def v_cst1(e):
            last = None
            for tb in range(4):
                last = e.tensor_scalar(rowcv[0:1, tb * 512:(tb + 1) * 512],
                                       pb[4 + tb][0:1, :], 1.0 / CONV, None, OP.mult)
            return last
        p.o("vector", v_cst1, "vE")
        p.w("scalar", "vE")
        p.o("scalar", lambda e: e.activation(rowcv[0:1, :], rowcv[0:1, :], AF.Sqrt,
                                             bias=cst[0:1, 1:2]), "sE")
        p.w("vector", "sE")
        p.o("vector", lambda e: e.reciprocal(rowcv2[0:1, :], rowcv[0:1, :]), "vE")
        p.w("tensor", "vE")

        def t_bcc(e):
            last = None
            for tb in range(4):
                last = e.matmul(pb[4 + tb][:, :], ones_row[:, :],
                                rowcv2[0:1, tb * 512:(tb + 1) * 512],
                                start=True, stop=True)
            return last
        p.o("tensor", t_bcc, "tE")
        p.w("vector", "tE")

        def v_bcc(e):
            last = None
            for tb in range(4):
                last = e.tensor_copy(out=crmsb[:, tb * 512:(tb + 1) * 512],
                                     in_=pb[4 + tb][:, :])
            return last
        p.o("vector", v_bcc, "vE")
        if debug:
            p.w("sync", "vE")
            gD.issue("sync", dbg["dbg_crms"], crmsb[:, :])
        gW.waitall("sync")
        gD.waitall("sync")

    if n_phases >= 4:
        p.w("tensor", "vE")
        p.w("vector", "tE")
        p.w("vector", "sE")
        p.w("scalar", "vE")
        p.w("scalar", "tE")
        gW.waitall("vector")
        gW.waitall("scalar")
        gG.waitall("gpsimd", issuer=True)
        p.w("gpsimd", "vE")
        p.w("gpsimd", "tE")
        gR.waitall("vector")
        gR.waitall("scalar")
        gR.waitall("tensor")
        gD.waitall("vector")
        gD.waitall("scalar")
        p.w("sync", "vE")
        p.w("sync", "tE")
        p.w("sync", "sE")
        gT = DGroup(p, "dX")

        st3 = {"w1sem": [], "tmm": [], "vq": [], "astore": [], "agel": []}

        def ph3_unit(m):
            if m >= 2:
                p.w("gpsimd", "tE", st3["tmm"][m - 2])
            src_ap = w1t[:, m * 128:(m + 1) * 128].rearrange("(k p) c -> p k c", p=128)
            dst_ap = w1b[m % 2][:, :].rearrange("p (k c) -> p k c", c=128)
            st3["w1sem"].append(gG.issue("gpsimd", dst_ap, src_ap))
            p.w("tensor", *st3["w1sem"][m])
            if m >= 1:
                p.w("tensor", "vE", st3["vq"][m - 1])

            def t_mm(e, m=m):
                last = None
                for k in range(NK):
                    for tb in range(4):
                        last = e.matmul(pb[tb][:, :],
                                        w1b[m % 2][:, k * 128:(k + 1) * 128],
                                        xn[k][:, tb * 512:(tb + 1) * 512],
                                        start=(k == 0), stop=(k == NK - 1))
                return last
            st3["tmm"].append(p.o("tensor", t_mm, "tE"))
            p.w("vector", "tE", st3["tmm"][m])
            ag = None
            for tb in range(4):
                p.w("vector", "vE")
                if ag is not None:
                    p.w("vector", "sE", ag)
                elif st3["agel"]:
                    p.w("vector", "sE", st3["agel"][-1])
                p.o("vector", lambda e, tb=tb: e.tensor_scalar(
                    qf[:, :], pb[tb][:, :], 10.0, -32768.49, OP.mult, OP.max), "vE")
                p.w("vector", "vE")
                p.o("vector", lambda e: e.tensor_scalar(
                    qt[:, :], qf[:, :], 32767.49, MAGIC, OP.min, OP.add), "vE")
                p.w("vector", "vE")
                vq = p.o("vector", lambda e: e.tensor_scalar(
                    qf[:, :], qt[:, :], -MAGIC, None, OP.add), "vE")
                p.w("scalar", "vE", vq)
                if tb == 0 and m >= 1:
                    for pr in st3["astore"][m - 1]:
                        p.w("scalar", *pr)
                ag = p.o("scalar", lambda e, tb=tb: e.activation(
                    astage[0][:, tb * 512:(tb + 1) * 512], qf[:, :],
                    AF.Gelu_apprx_tanh, scale=0.1), "sE")
            st3["vq"].append(vq)
            st3["agel"].append(ag)
            p.w("sync", "sE", ag)
            st3["astore"].append([gW.issue("sync", a_s[m], astage[0][:, :])])

        if n_phases >= 5:
            st4 = {"sccons": {}, "scl": {}, "silc": {},
                   "hbfc": [0] * 8, "xdec": [0] * 8, "states": [0] * 8,
                   "yadd": [0] * 8, "ysc": [0] * 8, "htmp": [0] * 8,
                   "hadd": [0] * 8, "cbse": [0, 0], "mcbq": [0, 0],
                   "ydq": [0, 0], "lexq": 0, "ygstore": [], "prev_vyg": 0,
                   "tlast": 0, "vacs": 0, "easv": 0}
            ytok3 = ytok[:, :].rearrange("p (m t) -> p m t", t=128)

            def ph4_prelude():
                relc = {}
                for cb in range(4):
                    bank = pb[6 + (cb % 2)]
                    if cb >= 2:
                        p.w("tensor", "sE", relc[cb - 2])

                    def t_dt(e, cb=cb, bank=bank):
                        return e.matmul(bank[0:64, :], dtw_sb[0:64, :],
                                        a_dt[0:64, cb * 512:(cb + 1) * 512],
                                        start=True, stop=True)
                    tdt = p.o("tensor", t_dt, "tE")
                    p.w("scalar", "tE", tdt)
                    p.w("scalar", "vE", p.cnt["vE"])
                    p.o("scalar", lambda e, bank=bank: e.activation(
                        wk1[0:64, :], bank[0:64, :], AF.Abs, bias=dtb_sb[0:64, :]),
                        "sE")
                    p.w("scalar", "sE")
                    p.o("scalar", lambda e: e.activation(
                        wk2[0:64, :], wk1[0:64, :], AF.Exp, scale=-1.0), "sE")
                    p.w("scalar", "sE")
                    p.o("scalar", lambda e: e.activation(
                        wk3[0:64, :], wk2[0:64, :], AF.Ln, bias=cst[0:64, 2:3]), "sE")
                    p.w("scalar", "sE")
                    relc[cb] = p.o("scalar", lambda e, bank=bank: e.activation(
                        wk1[0:64, :], bank[0:64, :], AF.Relu, bias=dtb_sb[0:64, :]),
                        "sE")
                    p.w("vector", "sE", relc[cb])
                    p.o("vector", lambda e: e.tensor_tensor(
                        wk2[0:64, :], wk3[0:64, :], wk1[0:64, :], OP.add), "vE")
                    p.w("vector", "vE")
                    p.o("vector", lambda e, cb=cb: e.tensor_scalar(
                        a_dt[0:64, cb * 512:(cb + 1) * 512], wk2[0:64, :],
                        negea[0:64, 0:1], None, OP.mult), "vE")
                p.w("vector", "vE")
                for c in range(NCH):
                    p.o("vector", lambda e, c=c: e.tensor_tensor_scan(
                        acs[0:64, c * CK:(c + 1) * CK], ones64f[0:64, :],
                        a_dt[0:64, c * CK:(c + 1) * CK], 0.0, OP.mult, OP.add), "vE")
                st4["vacs"] = p.cnt["vE"]
                p.w("vector", "vE")
                p.w("vector", "sE")
                p.o("vector", lambda e: e.memset(h32[:, :], 0.0), "vE")
                p.w("vector", "vE")
                p.o("vector", lambda e: e.memset(hbf[:, :], 0.0), "vE")
                st4["hbfc"] = [p.cnt["vE"]] * 8
                if debug:
                    p.w("sync", "vE")
                    gD.issue("sync", dbg["dbg_acs"], acs[0:64, :])

            def ph4_chunk(c):
                sc, off = c // 2, (c % 2) * CK
                yield
                if c % 2 == 0:
                    if sc >= 1:
                        for s_, v_ in st4["sccons"][sc - 1]:
                            p.w("sync", s_, v_)
                    NSUB = 4
                    TW = 48 // NSUB
                    subs = []
                    for i in range(NSUB):
                        subs.append(gS.issue(
                            "sync",
                            cact[0][:, i * TW * SC:(i + 1) * TW * SC].rearrange(
                                "p (m t) -> p m t", t=SC),
                            craw_s[i * TW:(i + 1) * TW, :,
                                   sc * SC:(sc + 1) * SC].transpose([1, 0, 2])))
                    st4["scl"][sc] = [None,
                                      gS.issue("sync",
                                               gact[0][:, :].rearrange(
                                                   "p (m t) -> p m t", t=SC),
                                               gate_s[:, :, sc * SC:(sc + 1) * SC]
                                               .transpose([1, 0, 2]))]
                    sil = None
                    for i in range(NSUB):
                        p.w("vector", *subs[i])
                        p.w("vector", "vE")
                        vcv = p.o("vector", lambda e, sc=sc, i=i, TW=TW:
                                  e.tensor_tensor(
                            cact[0][:, i * TW * SC:(i + 1) * TW * SC].rearrange(
                                "p (m t) -> p m t", t=SC),
                            cact[0][:, i * TW * SC:(i + 1) * TW * SC].rearrange(
                                "p (m t) -> p m t", t=SC),
                            crmsb[:, sc * SC:(sc + 1) * SC].unsqueeze(1)
                            .broadcast_to([128, TW, SC]), OP.mult), "vE")
                        p.w("scalar", "vE", vcv)
                        sil = p.o("scalar", lambda e, i=i, TW=TW: e.activation(
                            cact[0][:, i * TW * SC:(i + 1) * TW * SC],
                            cact[0][:, i * TW * SC:(i + 1) * TW * SC], AF.Silu),
                            "sE")
                    st4["silc"][sc] = sil

                def csl(m, width=CK):
                    return cact[0][:, m * SC + off:m * SC + off + width]

                acs_c = acs[0:64, c * CK:(c + 1) * CK]
                p.w("scalar", "tE", st4["tlast"])
                p.w("scalar", "sE")
                p.o("scalar", lambda e, acs_c=acs_c: e.activation(
                    exp_t[0:64, :], acs_c, AF.Exp), "sE")
                p.o("scalar", lambda e, acs_c=acs_c, c=c: e.activation(
                    dec_t[0:64, :], acs_c, AF.Exp, scale=-1.0,
                    bias=acs[0:64, c * CK + CK - 1:c * CK + CK]), "sE")
                sa = p.o("scalar", lambda e, c=c: e.activation(
                    awk[0:64, 0:1], acs[0:64, c * CK + CK - 1:c * CK + CK], AF.Exp),
                    "sE")
                p.w("vector", "tE", st4["tlast"])
                p.w("vector", "vE")
                p.o("vector", lambda e, acs_c=acs_c: e.tensor_copy(
                    out=hi_t[0:64, :], in_=acs_c), "vE")
                p.w("vector", "vE")
                p.o("vector", lambda e, acs_c=acs_c: e.tensor_tensor(
                    lo_t[0:64, :], acs_c, hi_t[0:64, :], OP.subtract), "vE")
                p.w("vector", "sE", sa)
                vcdl = p.o("vector", lambda e: e.tensor_scalar(
                    cdl_t[0:64, :], ones64[0:64, :], awk[0:64, 0:1], None, OP.mult),
                    "vE")
                satl = p.cnt["sE"]
                vtl = p.cnt["vE"]
                p.w("tensor", "vE", vtl)

                pT = pb[6][:, :].bitcast(BF16)
                p.w("tensor", "sE", satl)
                if c >= 1:
                    p.w("tensor", "vE", st4["prev_vyg"])
                if c >= 1:
                    for pr in st4["ygstore"][c - 1]:
                        p.w("vector", *pr)
                srcs = ([csl(m) for m in range(32)] + [csl(32 + g) for g in range(8)]
                        + [exp_t[0:64, :], cdl_t[0:64, :]])
                dsts = ([xtok[:, m * 128:(m + 1) * 128] for m in range(32)] +
                        [btok[:, g * 128:(g + 1) * 128] for g in range(8)] +
                        [exp_tok[:, :], cdx[:, :]])
                tevb = {0: p.cnt["sE"], 1: p.cnt["sE"]}
                for bi, b0 in enumerate(range(0, 42, 8)):
                    nb = min(8, 42 - b0)
                    pTb = pb[6][:, :].bitcast(BF16) if bi % 2 == 0 else \
                        pb[4][:, :].bitcast(BF16)
                    p.w("tensor", "sE", tevb[bi % 2])

                    def t_trb(e, b0=b0, nb=nb, srcs=srcs, pT=pTb):
                        last = None
                        for j in range(nb):
                            s_ = srcs[b0 + j]
                            if s_.shape[0] == 64:
                                last = e.transpose(pT[:, j * 128:j * 128 + 64],
                                                   s_, id64[0:64, :])
                            else:
                                last = e.transpose(pT[:, j * 128:(j + 1) * 128],
                                                   s_, idb[:, :])
                        return last
                    ttb = p.o("tensor", t_trb, "tE")
                    p.w("scalar", "tE", ttb)
                    if b0 == 0:
                        if c >= 1:
                            p.w("scalar", "tE", st4["states"][7])
                            p.w("scalar", "vE", st4["prev_vyg"])

                    def s_trev(e, b0=b0, nb=nb, dsts=dsts, pT=pTb):
                        last = None
                        for j in range(nb):
                            d_ = dsts[b0 + j]
                            w_ = d_.shape[-1]
                            last = e.activation(d_, pT[:, j * 128:j * 128 + w_],
                                                AF.Copy)
                        return last
                    tevb[bi % 2] = p.o("scalar", s_trev, "sE")
                tevc = max(tevb.values())
                p.w("tensor", "sE", tevc)
                p.w("vector", "sE", tevc)
                yield

                for g in range(8):
                    gsl = slice(g * 512, (g + 1) * 512)
                    selg = id64[0:64, 8 * g:8 * g + 8].unsqueeze(2).broadcast_to(
                        [64, 8, 64])
                    bAP, cAP = csl(32 + g), csl(40 + g)
                    if g >= 1:
                        p.w("tensor", "vE", st4["hadd"][g - 1])

                    def t_dec(e, selg=selg):
                        return e.matmul(pb[6][:, :], dec_t[0:64, :], selg,
                                        start=True, stop=True)
                    tdec = p.o("tensor", t_dec, "tE")
                    if g >= 1:
                        p.w("tensor", "vE", st4["ysc"][g - 1])

                    def t_cb(e, bAP=bAP, cAP=cAP):
                        return e.matmul(pb[7][:, 0:128], bAP, cAP,
                                        start=True, stop=True)
                    tcb = p.o("tensor", t_cb, "tE")
                    p.w("vector", "tE", tdec)
                    p.w("vector", "vE")
                    st4["xdec"][g] = p.o("vector", lambda e, gsl=gsl: e.tensor_tensor(
                        xdec[:, gsl], xtok[:, gsl], pb[6][:, :], OP.mult), "vE")
                    for q in range(2):
                        p.w("tensor", "sE", st4["lexq"])
                        hi = hi_t[0:64, :]
                        lo = lo_t[0:64, :]

                        def t_lm(e, g=g, q=q, hi=hi, lo=lo):
                            for j in range(4):
                                h = 8 * g + 4 * q + j
                                selh = id64[0:64, h:h + 1].broadcast_to([64, 128])
                                nselh = nid64[0:64, h:h + 1].broadcast_to([64, 128])
                                sl = pb[4][:, j * 128:(j + 1) * 128]
                                e.matmul(sl, selh, hi, start=(j == 0), stop=False)
                                e.matmul(sl, selh, lo, start=False, stop=False)
                                e.matmul(sl, hi, nselh, start=False, stop=False)
                                e.matmul(sl, lo, nselh, start=False, stop=False)
                            c2r = c2m[:, :].unsqueeze(1).broadcast_to([128, 4, 128])
                            return e.matmul(
                                pb[4][:, 0:512].rearrange("p (j t) -> p j t", t=128),
                                c1m[:, :], c2r, start=False, stop=True)
                        tlq = p.o("tensor", t_lm, "tE")
                        p.w("scalar", "tE", tlq)
                        p.w("scalar", "vE", st4["mcbq"][q])
                        st4["lexq"] = p.o("scalar", lambda e: e.activation(
                            lexp[:, :], pb[4][:, 0:512], AF.Exp), "sE")
                        p.w("vector", "sE", st4["lexq"])
                        p.w("vector", "tE", max(st4["ydq"][q], tcb))
                        cbr = pb[7][:, 0:128].unsqueeze(1).broadcast_to([128, 4, 128])
                        st4["mcbq"][q] = p.o("vector",
                                             lambda e, q=q, cbr=cbr: e.tensor_tensor(
                            mcbq[q][:, :].rearrange("p (j t) -> p j t", t=128),
                            lexp[:, :].rearrange("p (j t) -> p j t", t=128),
                            cbr, OP.mult), "vE")
                        p.w("tensor", "vE", st4["mcbq"][q])
                        if q == 0 and g >= 1:
                            p.w("tensor", "vE", st4["yadd"][g - 1])

                        def t_yd(e, g=g, q=q):
                            last = None
                            for j in range(4):
                                h8 = 4 * q + j
                                last = e.matmul(
                                    pb[5][:, h8 * 64:(h8 + 1) * 64],
                                    mcbq[q][:, j * 128:(j + 1) * 128],
                                    xtok[:, g * 512 + h8 * 64:g * 512 + (h8 + 1) * 64],
                                    start=True, stop=True)
                            return last
                        st4["ydq"][q] = p.o("tensor", t_yd, "tE")
                    p.w("tensor", "vE", st4["xdec"][g])

                    def t_st(e, g=g, gsl=gsl):
                        return e.matmul(pb[6][:, :], btok[:, g * 128:(g + 1) * 128],
                                        xdec[:, gsl], start=True, stop=True)
                    st4["states"][g] = p.o("tensor", t_st, "tE")
                    p.w("tensor", "vE", max(st4["hbfc"][g], st4["mcbq"][1]))

                    def t_yc(e, cAP=cAP, gsl=gsl):
                        return e.matmul(pb[7][:, :], cAP, hbf[:, gsl],
                                        start=True, stop=True)
                    tyc = p.o("tensor", t_yc, "tE")
                    p.w("vector", "vE")
                    cdr = cdx[:, 8 * g:8 * g + 8].unsqueeze(2).broadcast_to(
                        [128, 8, 64])
                    st4["htmp"][g] = p.o("vector", lambda e, gsl=gsl, cdr=cdr:
                                         e.tensor_tensor(
                        htmp[:, :].rearrange("p (j d) -> p j d", d=64),
                        h32[:, gsl].rearrange("p (j d) -> p j d", d=64),
                        cdr, OP.mult), "vE")
                    p.w("vector", "tE", tyc)
                    p.w("vector", "vE")
                    exr = exp_tok[:, 8 * g:8 * g + 8].unsqueeze(2).broadcast_to(
                        [128, 8, 64])
                    st4["ysc"][g] = p.o("vector", lambda e, exr=exr: e.tensor_tensor(
                        ysc[:, :].rearrange("p (j d) -> p j d", d=64),
                        pb[7][:, :].rearrange("p (j d) -> p j d", d=64),
                        exr, OP.mult), "vE")
                    p.w("vector", "tE", st4["ydq"][1])
                    p.w("vector", "vE")
                    st4["yadd"][g] = p.o("vector", lambda e, gsl=gsl: e.tensor_tensor(
                        ytok[:, gsl], pb[5][:, :], ysc[:, :], OP.add), "vE")
                    p.w("vector", "tE", st4["states"][g])
                    p.w("vector", "vE")
                    st4["hadd"][g] = p.o("vector", lambda e, gsl=gsl: e.tensor_tensor(
                        h32[:, gsl], htmp[:, :], pb[6][:, :], OP.add), "vE")
                    p.w("vector", "tE", tyc)
                    p.w("vector", "vE")
                    st4["hbfc"][g] = p.o("vector", lambda e, gsl=gsl: e.tensor_copy(
                        out=hbf[:, gsl], in_=h32[:, gsl]), "vE")
                    yield

                ydump = []
                if debug:
                    p.w("sync", "vE", st4["yadd"][7])
                    ydump.append(gD.issue("sync", dbg["dbg_y"][c], ytok[:, :]))
                p.w("scalar", "vE", st4["yadd"][7])
                p.w("scalar", "tE", st4["states"][7])
                ssq = p.o("scalar", lambda e: e.activation(xdec[:, :], ytok[:, :],
                                                           AF.Square), "sE")
                p.w("vector", "sE", ssq)
                p.o("vector", lambda e: e.reduce_sum(yrms[:, 0:1], xdec[:, :],
                                                     axis=AX.X), "vE")
                p.w("vector", "vE")
                p.o("vector", lambda e: e.tensor_scalar(
                    yrms[:, 1:2], yrms[:, 0:1], 1.0 / DIN, 1e-6, OP.mult, OP.add),
                    "vE")
                p.w("scalar", "vE", p.cnt["vE"])
                p.o("scalar", lambda e: e.activation(yrms[:, 2:3], yrms[:, 1:2],
                                                     AF.Ln), "sE")
                p.w("scalar", "sE")
                p.o("scalar", lambda e: e.activation(yrms[:, 3:4], yrms[:, 2:3],
                                                     AF.Exp, scale=-0.5), "sE")
                p.w("scalar", "sE")
                syn = p.o("scalar", lambda e: e.activation(
                    xdec[:, :], ytok[:, :], AF.Copy, scale=yrms[:, 3:4]), "sE")
                tevb = {0: syn, 1: syn}
                for bi, b0 in enumerate(range(0, 32, 8)):
                    pTb = pb[6][:, :].bitcast(BF16) if bi % 2 == 0 else \
                        pb[4][:, :].bitcast(BF16)
                    p.w("tensor", "sE", tevb[bi % 2])
                    if b0 == 0:
                        p.w("tensor", "vE", st4["hadd"][7])

                    def t_trb(e, b0=b0, pT=pTb):
                        last = None
                        for j in range(8):
                            last = e.transpose(pT[:, j * 128:(j + 1) * 128],
                                               xdec[:, (b0 + j) * 128:(b0 + j + 1) * 128],
                                               idb[:, :])
                        return last
                    ttb = p.o("tensor", t_trb, "tE")
                    p.w("scalar", "tE", ttb)
                    if b0 == 0:
                        p.w("scalar", "vE", st4["hadd"][7])

                    def s_trev(e, b0=b0, pT=pTb):
                        last = None
                        for j in range(8):
                            last = e.activation(xtok[:, (b0 + j) * 128:(b0 + j + 1) * 128],
                                                pT[:, j * 128:(j + 1) * 128], AF.Copy)
                        return last
                    tevb[bi % 2] = p.o("scalar", s_trev, "sE")
                tevc = max(tevb.values())
                p.w("vector", "sE", tevc)
                p.w("vector", *st4["scl"][sc][1])
                for pr in ydump:
                    p.w("vector", *pr)
                p.w("vector", "sE", syn)
                p.w("vector", "vE")
                vyg = p.o("vector", lambda e, off=off: e.tensor_tensor(
                    ytok3, xtok[:, :].rearrange("p (m t) -> p m t", t=128),
                    gact[0][:, :].rearrange("p (m t) -> p m t", t=SC)[:, :,
                                                                     off:off + 128],
                    OP.mult), "vE")
                st4["prev_vyg"] = vyg
                p.w("sync", "vE", vyg)
                prs = [gW.issue("sync", yg_s[c].transpose([1, 0, 2]),
                                ytok[:, :].rearrange("p (m t) -> p m t", t=128))]
                if debug:
                    prs.append(gD.issue("sync", dbg["dbg_yg"][c], ytok[:, :]))
                st4["ygstore"].append(prs)
                st4["tlast"] = p.cnt["tE"]
                if c % 2 == 1:
                    st4["sccons"][sc] = [("tE", p.cnt["tE"]), ("vE", p.cnt["vE"]),
                                         ("sE", p.cnt["sE"])]

            ph4_prelude()
            gens = [ph4_chunk(c) for c in range(NCH)]
            todo = [s for g_ in gens for s in [g_]]

        NU = DFF // 128
        gi = 0
        for m in range(NU):
            ph3_unit(m)
            if n_phases >= 5:
                for _ in range(3):
                    if gi < len(todo):
                        try:
                            next(todo[gi])
                        except StopIteration:
                            gi += 1
        if n_phases >= 5:
            while gi < len(todo):
                try:
                    next(todo[gi])
                except StopIteration:
                    gi += 1
        gW.waitall("sync")
        gD.waitall("sync")

    if n_phases >= 6:
        p.barrier()
        ablock = arena[:, 0:131072].bitcast(BF16).rearrange("p (k t) -> p k t", t=1024)
        o1sem = []
        for tb2 in range(2):
            if tb2 == 1:
                p.w("sync", "vE", p.cnt["vE"])
                p.w("sync", "tE", p.cnt["tE"])
            asem = gS.issue("sync", ablock,
                            a_s[:, :, tb2 * 1024:(tb2 + 1) * 1024].transpose([1, 0, 2]))
            w2sem, t5, v5 = [], [], []
            for m in range(NK):
                if m >= 2:
                    p.w("sync", "tE", t5[m - 2])
                elif tb2 == 1:
                    p.w("sync", "tE")
                w2sem.append(gS.issue("sync", w2slab[m % 2][:, :], w2bf_s[m]))
                if m >= 1:
                    p.w("sync", "vE", v5[m - 1])
                rse = gS.issue("sync", rchunk[:, 0:1024],
                               r_s[m][:, tb2 * 1024:(tb2 + 1) * 1024])
                p.w("tensor", *w2sem[m])
                p.w("tensor", *asem)
                if m >= 1:
                    p.w("tensor", "vE", v5[m - 1])

                def t_mm(e, m=m):
                    last = None
                    for k in range(64):
                        for tb in range(2):
                            last = e.matmul(
                                pb[2 * (m % 2) + tb][:, :],
                                w2slab[m % 2][:, k * 128:(k + 1) * 128],
                                atile[k][:, tb * 512:(tb + 1) * 512],
                                start=(k == 0), stop=(k == 63))
                    return last
                t5.append(p.o("tensor", t_mm, "tE"))
                p.w("vector", "tE", t5[m])
                p.w("vector", *rse)
                if o1sem:
                    p.w("vector", *o1sem[-1])
                p.w("vector", "vE")

                def v_ev(e, m=m):
                    last = None
                    for tb in range(2):
                        last = e.tensor_tensor(
                            ochunk[:, tb * 512:(tb + 1) * 512],
                            pb[2 * (m % 2) + tb][:, :],
                            rchunk[:, tb * 512:(tb + 1) * 512], OP.add)
                    return last
                v5.append(p.o("vector", v_ev, "vE"))
                p.w("sync", "vE", v5[m])
                o1sem.append(gW.issue("sync",
                                      out1_s[m][:, tb2 * 1024:(tb2 + 1) * 1024],
                                      ochunk[:, 0:1024]))
        gW.waitall("sync")

    if n_phases >= 7:
        p.barrier()
        for kf in range(32):
            gS.issue("sync",
                     ygt[kf][:, :].rearrange("p (c t) -> p c t", t=128),
                     yg_s[:, kf].transpose([1, 0, 2]))
        wosem, t6, v6, f6, ostores = [], [], [], [], []
        for m in range(NK):
            if m >= 2:
                p.w("gpsimd", "vE", f6[m - 2])
                p.w("gpsimd", "tE", t6[m - 2])
            src_ap = wot[:, m * 128:(m + 1) * 128].rearrange("(k p) c -> p k c", p=128)
            dst_ap = woslab[m % 2][:, :].rearrange("p (k c) -> p k c", c=128)
            wosem.append(gG.issue("gpsimd", dst_ap, src_ap))
            p.w("vector", *wosem[m])
            if m >= 2:
                p.w("vector", "tE", t6[m - 2])

            def v_fold(e, m=m):
                return e.tensor_tensor(woslab[m % 2][:, :], woslab[m % 2][:, :],
                                       outnw_slab[:, :], OP.mult)
            f6.append(p.o("vector", v_fold, "vE"))
            p.w("tensor", "vE", f6[m])
            if m == 0:
                gS.waitall("tensor")
            if m >= 1:
                p.w("tensor", "vE", v6[m - 1])

            def t_mm(e, m=m):
                last = None
                for k in range(32):
                    for tb in range(4):
                        last = e.matmul(pb[tb][:, :],
                                        woslab[m % 2][:, k * 128:(k + 1) * 128],
                                        ygt[k][:, tb * 512:(tb + 1) * 512],
                                        start=(k == 0), stop=(k == 31))
                return last
            t6.append(p.o("tensor", t_mm, "tE"))
            if m >= 1:
                p.w("sync", "vE", v6[m - 1])
            ose = gS.issue("sync", rchunk[:, 0:1024], out1_s[m][:, 0:1024])
            ose2 = gS.issue("sync", rchunk[:, 1024:2048], out1_s[m][:, 1024:2048])
            p.w("vector", "tE", t6[m])
            p.w("vector", *ose)
            p.w("vector", *ose2)
            if ostores:
                p.w("vector", *ostores[-1])
            p.w("vector", "vE")

            def v_fin(e, m=m):
                last = None
                for tb in range(4):
                    last = e.tensor_tensor(ochunk[:, tb * 512:(tb + 1) * 512],
                                           pb[tb][:, :],
                                           rchunk[:, tb * 512:(tb + 1) * 512], OP.add)
                return last
            v6.append(p.o("vector", v_fin, "vE"))
            p.w("sync", "vE", v6[m])
            ostores.append(gW.issue("sync", outT[m * 128:(m + 1) * 128, :],
                                    ochunk[:, :]))
        gW.waitall("sync")
        gW.waitall("vector")

    with ExitStack() as st:
        sems = {n: st.enter_context(nc.semaphore(n)) for n in p.names}
        block = st.enter_context(nc.Block())
        for ename in ENGINES:
            def mk(en):
                def go(eng):
                    for it in p.q[en]:
                        if it[0] == "w":
                            eng.wait_ge(sems[it[1]], it[2])
                        else:
                            inst = it[1](eng)
                            if it[2] is not None:
                                assert inst is not None, f"op returned None on {en}"
                                inst.then_inc(sems[it[2]], it[3])
                return go
            getattr(block, ename)(mk(ename))
    return nc


def _prep_inputs(L0_attn_x, L0_attn_norm_w, L0_attn_in_proj_w, L0_attn_conv_w,
                 L0_attn_conv_b, L0_attn_conv_norm_w, L0_attn_A_log, L0_attn_dt_w,
                 L0_attn_dt_b, L0_attn_out_norm_w, L0_attn_out_proj_w,
                 L0_ffn_W1, L0_ffn_W2):
    f = np.float32
    x = np.asarray(L0_attn_x, f)
    w_in = np.asarray(L0_attn_in_proj_w, f)
    perm = np.concatenate([
        np.arange(0, DIN),
        np.arange(DIN + H, DIN + H + GN),
        np.arange(DIN + H + GN, DIN + H + 2 * GN),
        np.arange(DIN + H + 2 * GN, NPROJ),
        np.arange(DIN, DIN + H),
    ])
    shared = {
        "w_in": np.ascontiguousarray(w_in[perm].T),
        "w1t": np.ascontiguousarray(np.asarray(L0_ffn_W1, f).T),
        "w2t": np.ascontiguousarray(np.asarray(L0_ffn_W2, f).T),
        "wot": np.ascontiguousarray(np.asarray(L0_attn_out_proj_w, f).T),
        "convw": np.ascontiguousarray(np.asarray(L0_attn_conv_w, f).reshape(CONV, 4)),
        "convb": np.ascontiguousarray(np.asarray(L0_attn_conv_b, f)),
        "convnw": np.ascontiguousarray(np.asarray(L0_attn_conv_norm_w, f)),
        "normw": np.ascontiguousarray(np.asarray(L0_attn_norm_w, f)),
        "outnw": np.ascontiguousarray(np.asarray(L0_attn_out_norm_w, f)),
        "alog": np.ascontiguousarray(np.asarray(L0_attn_A_log, f)),
        "dtb": np.ascontiguousarray(np.asarray(L0_attn_dt_b, f)),
        "dtwt": np.ascontiguousarray(np.asarray(L0_attn_dt_w, f).T),
    }
    in_maps = []
    for c in range(8):
        m = dict(shared)
        m["xT"] = np.ascontiguousarray(x[c].T)
        in_maps.append(m)
    return in_maps


def kernel(**inputs):
    in_maps = _prep_inputs(**inputs)
    nc = build(N_PHASES, DEBUG)
    res = run_bass_kernel_spmd(nc, in_maps, core_ids=list(range(8)))
    out = np.stack([np.asarray(r["outT"]).T for r in res.results], axis=0)
    return np.ascontiguousarray(out.astype(np.float32))


# version 10
# speedup vs baseline: 1.0127x; 1.0127x over previous
import numpy as np
import ml_dtypes
from contextlib import ExitStack
import concourse.bass as bass
import concourse.mybir as mybir
from concourse.bass_utils import run_bass_kernel_spmd

F32 = mybir.dt.float32
BF16 = mybir.dt.bfloat16
U8 = mybir.dt.uint8
AF = mybir.ActivationFunctionType
OP = mybir.AluOpType
AX = mybir.AxisListType

D = 2048
L = 2048
DIN = 4096
H = 64
DH = 64
NST = 128
G = 8
GN = G * NST
CONV = DIN + 2 * GN
NPROJ = DIN + H + 2 * GN + DIN
DFF = 8192
NK = D // 128
NCH = 16
CK = 128
MAGIC = 12582912.0
BIGM = 30000.0

ENGINES = ("sync", "tensor", "vector", "scalar", "gpsimd")

N_PHASES = 7
DEBUG = False


class Sched:
    def __init__(self):
        self.q = {e: [] for e in ENGINES}
        self.cnt = {}
        self.names = []

    def sem(self, *names):
        for n in names:
            self.cnt[n] = 0
            self.names.append(n)

    def w(self, eng, s, v=None):
        v = self.cnt[s] if v is None else v
        if v > 0:
            self.q[eng].append(("w", s, v))

    def o(self, eng, fn, inc=None, n=1):
        self.q[eng].append(("o", fn, inc, n))
        if inc is not None:
            self.cnt[inc] += n
            return self.cnt[inc]
        return None

    def barrier(self):
        snap = [(s, self.cnt[s]) for s in self.names if self.cnt[s] > 0]
        for e in ENGINES:
            for s, v in snap:
                self.q[e].append(("w", s, v))


class DGroup:

    def __init__(self, p, base, n=6):
        self.p = p
        self.names = [f"{base}{i}" for i in range(n)]
        p.sem(*self.names)
        self.i = 0
        self.waited = {s: 0 for s in self.names}

    def issue(self, eng, out_ap, in_ap, **kw):
        s = self.names[self.i % len(self.names)]
        self.i += 1
        if self.p.cnt[s] > self.waited[s]:
            self.p.w(eng, s, self.p.cnt[s])
            self.waited[s] = self.p.cnt[s]
        self.p.o(eng,
                 lambda e, o=out_ap, i=in_ap, kw=kw: e.dma_start(out=o, in_=i, **kw),
                 s, 16)
        return s, self.p.cnt[s]

    def wait_last(self, eng):
        s = self.names[(self.i - 1) % len(self.names)]
        self.p.w(eng, s, self.p.cnt[s])

    def waitall(self, eng, issuer=False):
        for s in self.names:
            self.p.w(eng, s, self.p.cnt[s])
            if issuer:
                self.waited[s] = max(self.waited[s], self.p.cnt[s])


def _sz(dt):
    return {F32: 4, BF16: 2}[dt]


def build(n_phases=N_PHASES, debug=False):
    nc = bass.Bass("TRN2", target_bir_lowering=False)

    xT = nc.dram_tensor("xT", [D, L], F32, kind="ExternalInput").ap()
    w_in = nc.dram_tensor("w_in", [D, NPROJ], F32, kind="ExternalInput").ap()
    w1t = nc.dram_tensor("w1t", [D, DFF], F32, kind="ExternalInput").ap()
    w2t = nc.dram_tensor("w2t", [DFF, D], F32, kind="ExternalInput").ap()
    wot = nc.dram_tensor("wot", [DIN, D], F32, kind="ExternalInput").ap()
    convw = nc.dram_tensor("convw", [CONV, 4], F32, kind="ExternalInput").ap()
    convb = nc.dram_tensor("convb", [CONV], F32, kind="ExternalInput").ap()
    convnw = nc.dram_tensor("convnw", [CONV], F32, kind="ExternalInput").ap()
    normw = nc.dram_tensor("normw", [D], F32, kind="ExternalInput").ap()
    outnw = nc.dram_tensor("outnw", [DIN], F32, kind="ExternalInput").ap()
    alog = nc.dram_tensor("alog", [H], F32, kind="ExternalInput").ap()
    dtb = nc.dram_tensor("dtb", [H], F32, kind="ExternalInput").ap()
    dtwt = nc.dram_tensor("dtwt", [H, H], F32, kind="ExternalInput").ap()
    outT = nc.dram_tensor("outT", [D, L], F32, kind="ExternalOutput").ap()

    r_s = nc.dram_tensor("r_s", [NK, 128, L], F32, kind="Internal").ap()
    craw_s = nc.dram_tensor("craw_s", [CONV // 128, 128, L], BF16, kind="Internal").ap()
    gate_s = nc.dram_tensor("gate_s", [DIN // 128, 128, L], BF16, kind="Internal").ap()
    a_s = nc.dram_tensor("a_s", [DFF // 128, 128, L], BF16, kind="Internal").ap()
    yg_s = nc.dram_tensor("yg_s", [NCH, DIN // 128, 128, CK], BF16, kind="Internal").ap()
    out1_s = nc.dram_tensor("out1_s", [NK, 128, L], F32, kind="Internal").ap()
    w2bf_s = nc.dram_tensor("w2bf_s", [NK, 128, 64 * 128], BF16, kind="Internal").ap()

    dbg = {}
    if debug:
        dbg["dbg_xn"] = nc.dram_tensor("dbg_xn", [D, L], BF16, kind="ExternalOutput").ap()
        dbg["dbg_r"] = nc.dram_tensor("dbg_r", [D, L], F32, kind="ExternalOutput").ap()
        dbg["dbg_craw"] = nc.dram_tensor("dbg_craw", [48, 128, L], BF16, kind="ExternalOutput").ap()
        dbg["dbg_gate"] = nc.dram_tensor("dbg_gate", [32, 128, L], BF16, kind="ExternalOutput").ap()
        dbg["dbg_adt"] = nc.dram_tensor("dbg_adt", [64, L], F32, kind="ExternalOutput").ap()
        dbg["dbg_crms"] = nc.dram_tensor("dbg_crms", [128, L], BF16, kind="ExternalOutput").ap()
        dbg["dbg_a"] = nc.dram_tensor("dbg_a", [DFF // 128, 128, L], BF16, kind="ExternalOutput").ap()
        dbg["dbg_yg"] = nc.dram_tensor("dbg_yg", [NCH, 128, DIN], BF16, kind="ExternalOutput").ap()
        dbg["dbg_y"] = nc.dram_tensor("dbg_y", [NCH, 128, DIN], BF16, kind="ExternalOutput").ap()
        dbg["dbg_acs"] = nc.dram_tensor("dbg_acs", [64, L], F32, kind="ExternalOutput").ap()
        dbg["dbg_h"] = nc.dram_tensor("dbg_h", [128, H * DH], F32, kind="ExternalOutput").ap()
        dbg["dbg_lex"] = nc.dram_tensor("dbg_lex", [8, 128, 128], BF16, kind="ExternalOutput").ap()
        dbg["dbg_mcb"] = nc.dram_tensor("dbg_mcb", [8, 128, 128], BF16, kind="ExternalOutput").ap()
        dbg["dbg_xtok"] = nc.dram_tensor("dbg_xtok", [128, DIN], BF16, kind="ExternalOutput").ap()
        dbg["dbg_cbs"] = nc.dram_tensor("dbg_cbs", [128, 128], BF16, kind="ExternalOutput").ap()

    ones_col = nc.alloc_sbuf_tensor("ones_col", [128, 1], BF16)
    ones_row = nc.alloc_sbuf_tensor("ones_row", [1, 128], F32)
    onesb = nc.alloc_sbuf_tensor("onesb", [128, 128], BF16)
    ones64 = nc.alloc_sbuf_tensor("ones64", [64, 128], BF16)
    ones64f = nc.alloc_sbuf_tensor("ones64f", [64, 128], F32)
    id64 = nc.alloc_sbuf_tensor("id64", [64, 64], BF16)
    nid64 = nc.alloc_sbuf_tensor("nid64", [64, 64], BF16)
    idb = nc.alloc_sbuf_tensor("idb", [128, 128], BF16)
    c1m = nc.alloc_sbuf_tensor("c1m", [128, 128], BF16)
    c2m = nc.alloc_sbuf_tensor("c2m", [128, 128], BF16)
    cst = nc.alloc_sbuf_tensor("cst", [128, 4], F32)
    normw_kp = nc.alloc_sbuf_tensor("normw_kp", [128, NK], F32)
    convw_kp = nc.alloc_sbuf_tensor("convw_kp", [128, 192], F32)
    convb_kp = nc.alloc_sbuf_tensor("convb_kp", [128, 48], F32)
    convnw_kp = nc.alloc_sbuf_tensor("convnw_kp", [128, 48], F32)
    outnw_kp = nc.alloc_sbuf_tensor("outnw_kp", [128, 32], F32)
    norm_slab = nc.alloc_sbuf_tensor("norm_slab", [128, D], BF16)
    outnw_slab = nc.alloc_sbuf_tensor("outnw_slab", [128, DIN], BF16)
    alog_sb = nc.alloc_sbuf_tensor("alog_sb", [64, 1], F32)
    dtb_sb = nc.alloc_sbuf_tensor("dtb_sb", [64, 1], F32)
    dtw_sb = nc.alloc_sbuf_tensor("dtw_sb", [64, 64], F32)
    negea = nc.alloc_sbuf_tensor("negea", [64, 1], F32)

    AB = 196096
    arena = nc.alloc_sbuf_tensor("arena", [128, AB], U8)

    def V(off, elems, dt, p=128):
        assert off % 64 == 0 and off + elems * _sz(dt) <= AB, (off, elems, dt)
        return arena[0:p, off:off + elems * _sz(dt)].bitcast(dt)

    SC = 256
    cact = [V(0, 48 * SC, BF16)]
    gact = [V(24576, 32 * SC, BF16)]
    xtok = V(40960, DIN, BF16)
    xdec = V(49152, DIN, BF16)
    btok = V(57344, GN, BF16)
    ytok = V(59392, DIN, BF16)
    lexp = V(67584, 512, BF16)
    mcbq = [V(68608, 512, BF16), V(69632, 512, BF16)]
    cbs = [V(70656, 128, BF16), V(70912, 128, BF16)]
    eas = V(71168, 512, BF16)
    cdl_t = V(72192, 128, BF16, p=64)
    ysc = V(72448, 512, BF16)
    yrms = V(73472, 16, F32)
    awk = V(73536, 64, F32, p=64)
    htmp = V(73792, 512, F32)
    crmsb = V(75840, L, BF16)
    hi_t = V(79936, CK, BF16, p=64)
    lo_t = V(80192, CK, BF16, p=64)
    exp_t = V(80448, CK, BF16, p=64)
    dec_t = V(80704, CK, BF16, p=64)
    acs = V(80960, L, F32, p=64)
    h32 = V(89152, H * DH, F32)
    hbf = V(105536, H * DH, BF16)
    a_dt = V(89152, L, F32, p=64)
    wk1 = V(97344, 512, F32, p=64)
    wk2 = V(99392, 512, F32, p=64)
    wk3 = V(101440, 512, F32, p=64)
    xn = [V(113728 + k * 4096, D, BF16) for k in range(NK)]
    w1b = [V(179264, D, BF16), V(183360, D, BF16)]
    qf = V(187456, 512, F32)
    qt = V(189504, 512, F32)
    astage = [V(191552, L, BF16)]
    exp_tok = V(195648, 64, BF16)
    cdx = V(195776, 64, BF16)
    s_b = V(0, L, F32)
    ms_b = V(8192, L, F32)
    xstage = [V(16384, L, F32), V(24576, L, F32)]
    rstage = [V(32768, L, F32), V(40960, L, F32)]
    xbf_t = [V(49152, L, BF16), V(53248, L, BF16)]
    sq_t = [V(57344, L, BF16), V(61440, L, BF16)]
    rowm = V(65536, L, F32, p=1)
    rowe = V(73728, L, F32, p=1)
    rows_ = V(81920, L, F32, p=1)
    rowt = V(90112, L, F32, p=1)
    rmsf_b = V(0, L, F32)
    wcast = [V(8192, D, BF16), V(12288, D, BF16), V(98304, D, BF16)]
    wf = [V(16384, D, BF16), V(20480, D, BF16), V(102400, D, BF16)]
    cslab = V(24576, 2051, BF16)
    ct0 = V(28736, L, BF16)
    ct1 = V(32832, L, BF16)
    ct2 = V(36928, L, BF16)
    csq = V(41024, L, BF16)
    crawt = V(45120, L, BF16)
    gtmp = V(28736, L, F32)
    rowcv = V(49216, L, F32, p=1)
    rowcv2 = V(57408, L, F32, p=1)
    atile = [V(k * 2048, 1024, BF16) for k in range(64)]
    ygt = [V(k * 4096, L, BF16) for k in range(32)]
    w2slab = [V(131072, 64 * 128, BF16), V(147456, 64 * 128, BF16)]
    woslab = [V(131072, 32 * 128, BF16), V(139264, 32 * 128, BF16)]
    ochunk = V(163840, 2048, F32)
    rchunk = V(172032, 2048, F32)

    p = Sched()
    p.sem("tE", "vE", "sE", "gE")
    gS = DGroup(p, "dS")
    gG = DGroup(p, "dG")
    gR = DGroup(p, "dR")
    gD = DGroup(p, "dD")

    if n_phases >= 1:
        gS.issue("sync", normw_kp[:, :], normw.rearrange("(k p) -> p k", p=128),
                 allow_slow_non_contiguous=True)
        gS.issue("sync", convw_kp[:, :].rearrange("p (k j) -> p k j", j=4),
                 convw.rearrange("(k p) j -> p k j", p=128),
                 allow_slow_non_contiguous=True)
        gS.issue("sync", convb_kp[:, :], convb.rearrange("(k p) -> p k", p=128),
                 allow_slow_non_contiguous=True)
        gS.issue("sync", convnw_kp[:, :], convnw.rearrange("(k p) -> p k", p=128),
                 allow_slow_non_contiguous=True)
        gS.issue("sync", outnw_kp[:, :], outnw.rearrange("(k p) -> p k", p=128),
                 allow_slow_non_contiguous=True)
        gS.issue("sync", alog_sb[0:64, :], alog.rearrange("(h o) -> h o", o=1))
        gS.issue("sync", dtb_sb[0:64, :], dtb.rearrange("(h o) -> h o", o=1))
        gS.issue("sync", dtw_sb[0:64, :], dtwt)

        def g_consts_a(e):
            e.memset(ones_col[:, :], 1.0)
            e.memset(onesb[:, :], 1.0)
            e.memset(ones64[0:64, :], 1.0)
            e.memset(ones64f[0:64, :], 1.0)
            e.memset(ones_row[:, :], 1.0)
            e.memset(cst[:, 0:1], 1e-5)
            e.memset(cst[:, 1:2], 1e-6)
            e.memset(cst[:, 2:3], 1.0)
            e.memset(id64[0:64, :], 1.0)
            e.memset(idb[:, :], 1.0)
            e.memset(nid64[0:64, :], -1.0)
            e.memset(c1m[:, :], 1.0)
            return e.memset(c2m[:, :], -BIGM)
        p.o("gpsimd", g_consts_a, "gE")
        p.w("gpsimd", "gE")

        def g_consts_b(e):
            e.affine_select(out=id64[0:64, :], in_=id64[0:64, :], pattern=[[-1, 64]],
                            compare_op=OP.is_equal, fill=0.0, base=0, channel_multiplier=1)
            e.affine_select(out=idb[:, :], in_=idb[:, :], pattern=[[-1, 128]],
                            compare_op=OP.is_equal, fill=0.0, base=0, channel_multiplier=1)
            e.affine_select(out=nid64[0:64, :], in_=nid64[0:64, :], pattern=[[-1, 64]],
                            compare_op=OP.is_equal, fill=0.0, base=0, channel_multiplier=1)
            e.affine_select(out=c1m[:, :], in_=c1m[:, :], pattern=[[1, 128]],
                            compare_op=OP.is_ge, fill=0.0, base=-1, channel_multiplier=-1)
            return e.affine_select(out=c2m[:, :], in_=c2m[:, :], pattern=[[-1, 128]],
                                   compare_op=OP.is_ge, fill=0.0, base=0, channel_multiplier=1)
        p.o("gpsimd", g_consts_b, "gE")

        gS.waitall("scalar")
        p.o("scalar", lambda e: e.activation(negea[0:64, :], alog_sb[0:64, :], AF.Exp),
            "sE")
        gS.waitall("vector")
        p.w("vector", "gE")
        p.w("vector", "sE")

        def v_slabs(e):
            for k in range(NK):
                e.tensor_scalar(norm_slab[:, k * 128:(k + 1) * 128], onesb[:, :],
                                normw_kp[:, k:k + 1], None, OP.mult)
            for k in range(32):
                e.tensor_scalar(outnw_slab[:, k * 128:(k + 1) * 128], onesb[:, :],
                                outnw_kp[:, k:k + 1], None, OP.mult)
            return e.tensor_scalar(negea[0:64, :], negea[0:64, :], -1.0, None, OP.mult)
        p.o("vector", v_slabs, "vE")

    pb = [nc.alloc_psum_tensor(f"pb{i}", [128, 512], F32) for i in range(8)]

    if n_phases >= 2:
        p.w("tensor", "gE")
        for k in range(NK):
            if k >= 2:
                p.w("sync", "sE", p.cnt["sE"] - 1)
            sv = gS.issue("sync", xstage[k % 2][:, :], xT[k * 128:(k + 1) * 128, :])
            p.w("scalar", *sv)
            if k >= 2:
                p.w("scalar", "tE", p.cnt["tE"] - 1)

            def s_cast(e, k=k):
                e.activation(xbf_t[k % 2][:, :], xstage[k % 2][:, :], AF.Copy)
                return e.activation(sq_t[k % 2][:, :], xstage[k % 2][:, :], AF.Square)
            p.o("scalar", s_cast, "sE")
            p.w("tensor", "sE")

            def t_stats(e, k=k):
                for tb in range(4):
                    e.matmul(pb[tb][0:1, :], ones_col[:, :],
                             xbf_t[k % 2][:, tb * 512:(tb + 1) * 512],
                             start=(k == 0), stop=(k == NK - 1))
                last = None
                for tb in range(4):
                    last = e.matmul(pb[4 + tb][0:1, :], ones_col[:, :],
                                    sq_t[k % 2][:, tb * 512:(tb + 1) * 512],
                                    start=(k == 0), stop=(k == NK - 1))
                return last
            p.o("tensor", t_stats, "tE")

        def vch(fn):
            p.w("vector", "vE")
            p.o("vector", fn, "vE")

        def sch(fn):
            p.w("scalar", "vE")
            p.o("scalar", fn, "sE")
            p.w("vector", "sE")

        p.w("vector", "tE")

        def v_fin1(e):
            last = None
            for tb in range(4):
                e.tensor_scalar(rowm[0:1, tb * 512:(tb + 1) * 512], pb[tb][0:1, :],
                                1.0 / D, None, OP.mult)
                last = e.tensor_scalar(rowe[0:1, tb * 512:(tb + 1) * 512],
                                       pb[4 + tb][0:1, :], 1.0 / D, None, OP.mult)
            return last
        p.o("vector", v_fin1, "vE")
        vch(lambda e: e.tensor_tensor(rowt[0:1, :], rowm[0:1, :], rowm[0:1, :], OP.mult))
        vch(lambda e: e.tensor_tensor(rowe[0:1, :], rowe[0:1, :], rowt[0:1, :], OP.subtract))
        sch(lambda e: e.activation(rowt[0:1, :], rowe[0:1, :], AF.Sqrt, bias=cst[0:1, 0:1]))
        vch(lambda e: e.reciprocal(rows_[0:1, :], rowt[0:1, :]))
        vch(lambda e: e.tensor_tensor(rowt[0:1, :], rowe[0:1, :], rows_[0:1, :], OP.mult))
        vch(lambda e: e.tensor_tensor(rowt[0:1, :], rowt[0:1, :], rows_[0:1, :], OP.mult))
        sch(lambda e: e.activation(rowt[0:1, :], rowt[0:1, :], AF.Sqrt, bias=cst[0:1, 1:2]))
        vch(lambda e: e.reciprocal(rowe[0:1, :], rowt[0:1, :]))
        vch(lambda e: e.tensor_tensor(rowm[0:1, :], rowm[0:1, :], rows_[0:1, :], OP.mult))

        for row, dst in ((rows_, s_b), (rowm, ms_b)):
            p.w("tensor", "vE")

            def t_bc(e, row=row):
                last = None
                for tb in range(4):
                    last = e.matmul(pb[tb][:, :], ones_row[:, :],
                                    row[0:1, tb * 512:(tb + 1) * 512],
                                    start=True, stop=True)
                return last
            p.o("tensor", t_bc, "tE")
            p.w("vector", "tE")

            def v_bc(e, dst=dst):
                last = None
                for tb in range(4):
                    last = e.tensor_copy(out=dst[:, tb * 512:(tb + 1) * 512],
                                         in_=pb[tb][:, :])
                return last
            p.o("vector", v_bc, "vE")

        xcons = []
        rsems = []
        p.w("sync", "vE")
        p.w("sync", "sE")
        for k in range(NK):
            if k >= 2:
                p.w("sync", "vE", xcons[k - 2])
            sv = gS.issue("sync", xstage[k % 2][:, :], xT[k * 128:(k + 1) * 128, :])
            p.w("vector", *sv)
            if k >= 2:
                for pr in rsems[k - 2]:
                    p.w("vector", *pr)

            def v_ln1(e, k=k):
                return e.tensor_tensor(rstage[k % 2][:, :], xstage[k % 2][:, :],
                                       s_b[:, :], OP.mult)
            p.o("vector", v_ln1, "vE")
            p.w("vector", "vE")

            def v_ln2a(e, k=k):
                return e.tensor_tensor(xn[k][:, :], rstage[k % 2][:, :], ms_b[:, :],
                                       OP.subtract)
            p.o("vector", v_ln2a, "vE")
            p.w("vector", "vE")

            def v_ln2b(e, k=k):
                return e.tensor_tensor(rstage[k % 2][:, :], rstage[k % 2][:, :],
                                       xstage[k % 2][:, :], OP.add)
            xcons.append(p.o("vector", v_ln2b, "vE"))
            p.w("vector", "vE")

            def v_ln3(e, k=k):
                return e.tensor_tensor(rstage[k % 2][:, :], rstage[k % 2][:, :],
                                       ms_b[:, :], OP.subtract)
            p.o("vector", v_ln3, "vE")
            p.w("sync", "vE")
            prs = []
            if debug:
                gD.issue("sync", dbg["dbg_xn"][k * 128:(k + 1) * 128, :], xn[k][:, :])
                prs.append(gR.issue("sync", dbg["dbg_r"][k * 128:(k + 1) * 128, :],
                                    rstage[k % 2][:, :]))
            prs.append(gR.issue("sync", r_s[k], rstage[k % 2][:, :]))
            rsems.append(prs)
        gR.waitall("sync")
        gD.waitall("sync")

    if n_phases >= 3:
        NM = 81
        p.w("vector", "sE")
        p.w("vector", "tE")
        p.w("gpsimd", "vE")
        gR.waitall("vector")
        gD.waitall("vector")
        gW = DGroup(p, "dW")

        p.w("tensor", "vE")

        def t_bcf(e):
            last = None
            for tb in range(4):
                last = e.matmul(pb[tb][:, :], ones_row[:, :],
                                rowe[0:1, tb * 512:(tb + 1) * 512],
                                start=True, stop=True)
            return last
        p.o("tensor", t_bcf, "tE")
        p.w("vector", "tE")

        def v_bcf(e):
            last = None
            for tb in range(4):
                last = e.tensor_copy(out=rmsf_b[:, tb * 512:(tb + 1) * 512],
                                     in_=pb[tb][:, :])
            return last
        p.o("vector", v_bcf, "vE")
        p.w("vector", "vE")
        p.o("vector", lambda e: e.memset(cslab[:, 0:3], 0.0), "vE")

        wsem, vfold, tmm, vevac = [], [], [], []
        statc = {}
        statt = {}
        cstore, gstore = [], []
        nslab64 = norm_slab[:, :].rearrange("p (k c) -> p k c", c=128)[:, :, 0:64]

        for m in range(NM):
            c0 = m * 128
            w = 64 if m == NM - 1 else 128
            if m >= 3:
                p.w("gpsimd", "vE", vfold[m - 3])
            src_ap = w_in[:, c0:c0 + w].rearrange("(k p) c -> p k c", p=128)
            dst_ap = wcast[m % 3][:, 0:NK * w].rearrange("p (k c) -> p k c", c=w)
            wsem.append(gG.issue("gpsimd", dst_ap, src_ap))

            p.w("vector", *wsem[m])
            if m >= 3:
                p.w("vector", "tE", tmm[m - 3])

            def v_fold(e, m=m, w=w):
                if w == 128:
                    return e.tensor_tensor(wf[m % 3][:, :], wcast[m % 3][:, :],
                                           norm_slab[:, :], OP.mult)
                return e.tensor_tensor(
                    wf[m % 3][:, 0:NK * w].rearrange("p (k c) -> p k c", c=w),
                    wcast[m % 3][:, 0:NK * w].rearrange("p (k c) -> p k c", c=w),
                    nslab64, OP.mult)
            vfold.append(p.o("vector", v_fold, "vE"))

            p.w("tensor", "vE", vfold[m])
            if m >= 1:
                p.w("tensor", "vE", vevac[m - 1])

            def t_mm(e, m=m, w=w):
                last = None
                for k in range(NK):
                    for tb in range(4):
                        last = e.matmul(pb[tb][0:w, :],
                                        wf[m % 3][:, k * w:(k + 1) * w],
                                        xn[k][:, tb * 512:(tb + 1) * 512],
                                        start=(k == 0), stop=(k == NK - 1))
                return last
            tmm.append(p.o("tensor", t_mm, "tE"))
            if 1 <= m <= 48:
                ms = m - 1
                p.w("tensor", "sE", statc[ms])

                def t_stat(e, ms=ms):
                    last = None
                    for tb in range(4):
                        last = e.matmul(pb[4 + tb][0:1, :], ones_col[:, :],
                                        csq[:, tb * 512:(tb + 1) * 512],
                                        start=(ms == 0), stop=(ms == 47))
                    return last
                statt[ms] = p.o("tensor", t_stat, "tE")

            p.w("vector", "tE", tmm[m])
            if m < 48:
                def v_ev(e, m=m):
                    last = None
                    for tb in range(4):
                        last = e.tensor_tensor(
                            cslab[:, 3 + tb * 512:3 + (tb + 1) * 512],
                            pb[tb][:, :], rmsf_b[:, tb * 512:(tb + 1) * 512], OP.mult)
                    return last
                vevac.append(p.o("vector", v_ev, "vE"))

                p.w("vector", "vE")
                p.o("vector", lambda e, m=m: e.tensor_scalar(
                    ct0[:, :], cslab[:, 0:L], convw_kp[:, 4 * m:4 * m + 1],
                    None, OP.mult), "vE")
                p.w("vector", "vE")
                if m >= 1:
                    for pr in cstore[m - 1]:
                        p.w("vector", *pr)
                p.o("vector", lambda e, m=m: e.scalar_tensor_tensor(
                    ct1[:, :], cslab[:, 1:1 + L], convw_kp[:, 4 * m + 1:4 * m + 2],
                    ct0[:, :], OP.mult, OP.add), "vE")
                p.w("vector", "vE")
                p.o("vector", lambda e, m=m: e.scalar_tensor_tensor(
                    ct2[:, :], cslab[:, 2:2 + L], convw_kp[:, 4 * m + 2:4 * m + 3],
                    ct1[:, :], OP.mult, OP.add), "vE")
                p.w("vector", "vE")
                if m >= 1:
                    p.w("vector", "sE", statc[m - 1])
                p.o("vector", lambda e, m=m: e.scalar_tensor_tensor(
                    ct0[:, :], cslab[:, 3:3 + L], convw_kp[:, 4 * m + 3:4 * m + 4],
                    ct2[:, :], OP.mult, OP.add), "vE")
                p.w("vector", "vE")
                p.o("vector", lambda e, m=m: e.tensor_scalar(
                    crawt[:, :], ct0[:, :], convb_kp[:, m:m + 1], None, OP.add), "vE")

                p.w("scalar", "vE")
                if m >= 1:
                    p.w("scalar", "tE", statt[m - 1])
                statc[m] = p.o("scalar", lambda e: e.activation(
                    csq[:, :], crawt[:, :], AF.Square), "sE")
                p.w("vector", "vE")
                p.o("vector", lambda e, m=m: e.tensor_scalar(
                    ct1[:, :], crawt[:, :], convnw_kp[:, m:m + 1], None, OP.mult), "vE")
                p.w("sync", "vE")
                prs = [gW.issue("sync", craw_s[m], ct1[:, :])]
                if debug:
                    prs.append(gD.issue("sync", dbg["dbg_craw"][m], ct1[:, :]))
                cstore.append(prs)
            elif m < 80:
                mg = m - 48
                p.w("vector", "vE", p.cnt["vE"])
                if mg >= 1:
                    p.w("vector", "sE", gsil[mg - 1])
                else:
                    for pr in cstore[-1]:
                        p.w("vector", *pr)

                def v_evg(e):
                    last = None
                    for tb in range(4):
                        last = e.tensor_tensor(
                            gtmp[:, tb * 512:(tb + 1) * 512],
                            pb[tb][:, :], rmsf_b[:, tb * 512:(tb + 1) * 512], OP.mult)
                    return last
                vevac.append(p.o("vector", v_evg, "vE"))
                p.w("scalar", "vE")
                p.w("scalar", "sE", p.cnt["sE"])
                if mg >= 1:
                    for pr in gstore[mg - 1]:
                        p.w("scalar", *pr)
                elif cstore:
                    for pr in cstore[-1]:
                        p.w("scalar", *pr)
                if mg == 0:
                    gsil = []
                gsil.append(p.o("scalar",
                                lambda e: e.activation(crawt[:, :], gtmp[:, :], AF.Silu),
                                "sE"))
                p.w("sync", "sE")
                prs = [gW.issue("sync", gate_s[mg], crawt[:, :])]
                if debug:
                    prs.append(gD.issue("sync", dbg["dbg_gate"][mg], crawt[:, :]))
                gstore.append(prs)
            else:
                def v_eva(e):
                    last = None
                    for tb in range(4):
                        last = e.tensor_tensor(
                            a_dt[0:64, tb * 512:(tb + 1) * 512],
                            pb[tb][0:64, :], rmsf_b[0:64, tb * 512:(tb + 1) * 512],
                            OP.mult)
                    return last
                vevac.append(p.o("vector", v_eva, "vE"))
                if debug:
                    p.w("sync", "vE")
                    gD.issue("sync", dbg["dbg_adt"], a_dt[0:64, :])

        p.w("vector", "tE")

        def v_cst1(e):
            last = None
            for tb in range(4):
                last = e.tensor_scalar(rowcv[0:1, tb * 512:(tb + 1) * 512],
                                       pb[4 + tb][0:1, :], 1.0 / CONV, None, OP.mult)
            return last
        p.o("vector", v_cst1, "vE")
        p.w("scalar", "vE")
        p.o("scalar", lambda e: e.activation(rowcv[0:1, :], rowcv[0:1, :], AF.Sqrt,
                                             bias=cst[0:1, 1:2]), "sE")
        p.w("vector", "sE")
        p.o("vector", lambda e: e.reciprocal(rowcv2[0:1, :], rowcv[0:1, :]), "vE")
        p.w("tensor", "vE")

        def t_bcc(e):
            last = None
            for tb in range(4):
                last = e.matmul(pb[4 + tb][:, :], ones_row[:, :],
                                rowcv2[0:1, tb * 512:(tb + 1) * 512],
                                start=True, stop=True)
            return last
        p.o("tensor", t_bcc, "tE")
        p.w("vector", "tE")

        def v_bcc(e):
            last = None
            for tb in range(4):
                last = e.tensor_copy(out=crmsb[:, tb * 512:(tb + 1) * 512],
                                     in_=pb[4 + tb][:, :])
            return last
        p.o("vector", v_bcc, "vE")
        if debug:
            p.w("sync", "vE")
            gD.issue("sync", dbg["dbg_crms"], crmsb[:, :])
        gW.waitall("sync")
        gD.waitall("sync")

    if n_phases >= 4:
        p.w("tensor", "vE")
        p.w("vector", "tE")
        p.w("vector", "sE")
        p.w("scalar", "vE")
        p.w("scalar", "tE")
        gW.waitall("vector")
        gW.waitall("scalar")
        gG.waitall("gpsimd", issuer=True)
        p.w("gpsimd", "vE")
        p.w("gpsimd", "tE")
        gR.waitall("vector")
        gR.waitall("scalar")
        gR.waitall("tensor")
        gD.waitall("vector")
        gD.waitall("scalar")
        p.w("sync", "vE")
        p.w("sync", "tE")
        p.w("sync", "sE")
        gT = DGroup(p, "dX")

        st3 = {"w1sem": [], "tmm": [], "vq": [], "astore": [], "agel": []}

        def ph3_unit(m):
            if m >= 2:
                p.w("gpsimd", "tE", st3["tmm"][m - 2])
            src_ap = w1t[:, m * 128:(m + 1) * 128].rearrange("(k p) c -> p k c", p=128)
            dst_ap = w1b[m % 2][:, :].rearrange("p (k c) -> p k c", c=128)
            st3["w1sem"].append(gG.issue("gpsimd", dst_ap, src_ap))
            p.w("tensor", *st3["w1sem"][m])
            if m >= 1:
                p.w("tensor", "vE", st3["vq"][m - 1])

            def t_mm(e, m=m):
                last = None
                for k in range(NK):
                    for tb in range(4):
                        last = e.matmul(pb[tb][:, :],
                                        w1b[m % 2][:, k * 128:(k + 1) * 128],
                                        xn[k][:, tb * 512:(tb + 1) * 512],
                                        start=(k == 0), stop=(k == NK - 1))
                return last
            st3["tmm"].append(p.o("tensor", t_mm, "tE"))
            p.w("vector", "tE", st3["tmm"][m])
            ag = None
            for tb in range(4):
                p.w("vector", "vE")
                if ag is not None:
                    p.w("vector", "sE", ag)
                elif st3["agel"]:
                    p.w("vector", "sE", st3["agel"][-1])
                p.o("vector", lambda e, tb=tb: e.tensor_scalar(
                    qf[:, :], pb[tb][:, :], 10.0, -32768.49, OP.mult, OP.max), "vE")
                p.w("vector", "vE")
                p.o("vector", lambda e: e.tensor_scalar(
                    qt[:, :], qf[:, :], 32767.49, MAGIC, OP.min, OP.add), "vE")
                p.w("vector", "vE")
                vq = p.o("vector", lambda e: e.tensor_scalar(
                    qf[:, :], qt[:, :], -MAGIC, None, OP.add), "vE")
                p.w("scalar", "vE", vq)
                if tb == 0 and m >= 1:
                    for pr in st3["astore"][m - 1]:
                        p.w("scalar", *pr)
                ag = p.o("scalar", lambda e, tb=tb: e.activation(
                    astage[0][:, tb * 512:(tb + 1) * 512], qf[:, :],
                    AF.Gelu_apprx_tanh, scale=0.1), "sE")
            st3["vq"].append(vq)
            st3["agel"].append(ag)
            p.w("sync", "sE", ag)
            st3["astore"].append([gW.issue("sync", a_s[m], astage[0][:, :])])

        if n_phases >= 5:
            st4 = {"sccons": {}, "scl": {}, "silc": {},
                   "hbfc": [0] * 8, "xdec": [0] * 8, "states": [0] * 8,
                   "yadd": [0] * 8, "ysc": [0] * 8, "htmp": [0] * 8,
                   "hadd": [0] * 8, "cbse": [0, 0], "mcbq": [0, 0],
                   "ydq": [0, 0], "lexq": 0, "ygstore": [], "prev_vyg": 0,
                   "tlast": 0, "vacs": 0, "easv": 0}
            ytok3 = ytok[:, :].rearrange("p (m t) -> p m t", t=128)

            def ph4_prelude():
                relc = {}
                for cb in range(4):
                    bank = pb[6 + (cb % 2)]
                    if cb >= 2:
                        p.w("tensor", "sE", relc[cb - 2])

                    def t_dt(e, cb=cb, bank=bank):
                        return e.matmul(bank[0:64, :], dtw_sb[0:64, :],
                                        a_dt[0:64, cb * 512:(cb + 1) * 512],
                                        start=True, stop=True)
                    tdt = p.o("tensor", t_dt, "tE")
                    p.w("scalar", "tE", tdt)
                    p.w("scalar", "vE", p.cnt["vE"])
                    p.o("scalar", lambda e, bank=bank: e.activation(
                        wk1[0:64, :], bank[0:64, :], AF.Abs, bias=dtb_sb[0:64, :]),
                        "sE")
                    p.w("scalar", "sE")
                    p.o("scalar", lambda e: e.activation(
                        wk2[0:64, :], wk1[0:64, :], AF.Exp, scale=-1.0), "sE")
                    p.w("scalar", "sE")
                    p.o("scalar", lambda e: e.activation(
                        wk3[0:64, :], wk2[0:64, :], AF.Ln, bias=cst[0:64, 2:3]), "sE")
                    p.w("scalar", "sE")
                    relc[cb] = p.o("scalar", lambda e, bank=bank: e.activation(
                        wk1[0:64, :], bank[0:64, :], AF.Relu, bias=dtb_sb[0:64, :]),
                        "sE")
                    p.w("vector", "sE", relc[cb])
                    p.o("vector", lambda e: e.tensor_tensor(
                        wk2[0:64, :], wk3[0:64, :], wk1[0:64, :], OP.add), "vE")
                    p.w("vector", "vE")
                    p.o("vector", lambda e, cb=cb: e.tensor_scalar(
                        a_dt[0:64, cb * 512:(cb + 1) * 512], wk2[0:64, :],
                        negea[0:64, 0:1], None, OP.mult), "vE")
                p.w("vector", "vE")
                for c in range(NCH):
                    p.o("vector", lambda e, c=c: e.tensor_tensor_scan(
                        acs[0:64, c * CK:(c + 1) * CK], ones64f[0:64, :],
                        a_dt[0:64, c * CK:(c + 1) * CK], 0.0, OP.mult, OP.add), "vE")
                st4["vacs"] = p.cnt["vE"]
                p.w("vector", "vE")
                p.w("vector", "sE")
                p.o("vector", lambda e: e.memset(h32[:, :], 0.0), "vE")
                p.w("vector", "vE")
                p.o("vector", lambda e: e.memset(hbf[:, :], 0.0), "vE")
                st4["hbfc"] = [p.cnt["vE"]] * 8
                if debug:
                    p.w("sync", "vE")
                    gD.issue("sync", dbg["dbg_acs"], acs[0:64, :])

            def ph4_chunk(c):
                sc, off = c // 2, (c % 2) * CK
                yield
                if c % 2 == 0:
                    if sc >= 1:
                        for s_, v_ in st4["sccons"][sc - 1]:
                            p.w("sync", s_, v_)
                    NSUB = 4
                    TW = 48 // NSUB
                    subs = []
                    for i in range(NSUB):
                        subs.append(gS.issue(
                            "sync",
                            cact[0][:, i * TW * SC:(i + 1) * TW * SC].rearrange(
                                "p (m t) -> p m t", t=SC),
                            craw_s[i * TW:(i + 1) * TW, :,
                                   sc * SC:(sc + 1) * SC].transpose([1, 0, 2])))
                    st4["scl"][sc] = [None,
                                      gS.issue("sync",
                                               gact[0][:, :].rearrange(
                                                   "p (m t) -> p m t", t=SC),
                                               gate_s[:, :, sc * SC:(sc + 1) * SC]
                                               .transpose([1, 0, 2]))]
                    sil = None
                    for i in range(NSUB):
                        p.w("vector", *subs[i])
                        p.w("vector", "vE")
                        vcv = p.o("vector", lambda e, sc=sc, i=i, TW=TW:
                                  e.tensor_tensor(
                            cact[0][:, i * TW * SC:(i + 1) * TW * SC].rearrange(
                                "p (m t) -> p m t", t=SC),
                            cact[0][:, i * TW * SC:(i + 1) * TW * SC].rearrange(
                                "p (m t) -> p m t", t=SC),
                            crmsb[:, sc * SC:(sc + 1) * SC].unsqueeze(1)
                            .broadcast_to([128, TW, SC]), OP.mult), "vE")
                        p.w("scalar", "vE", vcv)
                        sil = p.o("scalar", lambda e, i=i, TW=TW: e.activation(
                            cact[0][:, i * TW * SC:(i + 1) * TW * SC],
                            cact[0][:, i * TW * SC:(i + 1) * TW * SC], AF.Silu),
                            "sE")
                    st4["silc"][sc] = sil

                def csl(m, width=CK):
                    return cact[0][:, m * SC + off:m * SC + off + width]

                acs_c = acs[0:64, c * CK:(c + 1) * CK]
                p.w("scalar", "tE", st4["tlast"])
                p.w("scalar", "sE")
                p.o("scalar", lambda e, acs_c=acs_c: e.activation(
                    exp_t[0:64, :], acs_c, AF.Exp), "sE")
                p.o("scalar", lambda e, acs_c=acs_c, c=c: e.activation(
                    dec_t[0:64, :], acs_c, AF.Exp, scale=-1.0,
                    bias=acs[0:64, c * CK + CK - 1:c * CK + CK]), "sE")
                sa = p.o("scalar", lambda e, c=c: e.activation(
                    awk[0:64, 0:1], acs[0:64, c * CK + CK - 1:c * CK + CK], AF.Exp),
                    "sE")
                p.w("vector", "tE", st4["tlast"])
                p.w("vector", "vE")
                p.o("vector", lambda e, acs_c=acs_c: e.tensor_copy(
                    out=hi_t[0:64, :], in_=acs_c), "vE")
                p.w("vector", "vE")
                p.o("vector", lambda e, acs_c=acs_c: e.tensor_tensor(
                    lo_t[0:64, :], acs_c, hi_t[0:64, :], OP.subtract), "vE")
                p.w("vector", "sE", sa)
                vcdl = p.o("vector", lambda e: e.tensor_scalar(
                    cdl_t[0:64, :], ones64[0:64, :], awk[0:64, 0:1], None, OP.mult),
                    "vE")
                satl = p.cnt["sE"]
                vtl = p.cnt["vE"]
                p.w("tensor", "vE", vtl)

                pT = pb[6][:, :].bitcast(BF16)
                p.w("tensor", "sE", satl)
                if c >= 1:
                    p.w("tensor", "vE", st4["prev_vyg"])
                if c >= 1:
                    for pr in st4["ygstore"][c - 1]:
                        p.w("vector", *pr)
                srcs = ([csl(m) for m in range(32)] + [csl(32 + g) for g in range(8)]
                        + [exp_t[0:64, :], cdl_t[0:64, :]])
                dsts = ([xtok[:, m * 128:(m + 1) * 128] for m in range(32)] +
                        [btok[:, g * 128:(g + 1) * 128] for g in range(8)] +
                        [exp_tok[:, :], cdx[:, :]])
                tevb = {0: p.cnt["sE"], 1: p.cnt["sE"]}
                for bi, b0 in enumerate(range(0, 42, 8)):
                    nb = min(8, 42 - b0)
                    pTb = pb[6][:, :].bitcast(BF16) if bi % 2 == 0 else \
                        pb[4][:, :].bitcast(BF16)
                    p.w("tensor", "sE", tevb[bi % 2])

                    def t_trb(e, b0=b0, nb=nb, srcs=srcs, pT=pTb):
                        last = None
                        for j in range(nb):
                            s_ = srcs[b0 + j]
                            if s_.shape[0] == 64:
                                last = e.transpose(pT[:, j * 128:j * 128 + 64],
                                                   s_, id64[0:64, :])
                            else:
                                last = e.transpose(pT[:, j * 128:(j + 1) * 128],
                                                   s_, idb[:, :])
                        return last
                    ttb = p.o("tensor", t_trb, "tE")
                    p.w("scalar", "tE", ttb)
                    if b0 == 0:
                        if c >= 1:
                            p.w("scalar", "tE", st4["states"][7])
                            p.w("scalar", "vE", st4["prev_vyg"])

                    def s_trev(e, b0=b0, nb=nb, dsts=dsts, pT=pTb):
                        last = None
                        for j in range(nb):
                            d_ = dsts[b0 + j]
                            w_ = d_.shape[-1]
                            last = e.activation(d_, pT[:, j * 128:j * 128 + w_],
                                                AF.Copy)
                        return last
                    tevb[bi % 2] = p.o("scalar", s_trev, "sE")
                tevc = max(tevb.values())
                p.w("tensor", "sE", tevc)
                p.w("vector", "sE", tevc)
                yield

                for g in range(8):
                    gsl = slice(g * 512, (g + 1) * 512)
                    selg = id64[0:64, 8 * g:8 * g + 8].unsqueeze(2).broadcast_to(
                        [64, 8, 64])
                    bAP, cAP = csl(32 + g), csl(40 + g)
                    if g >= 1:
                        p.w("tensor", "vE", st4["hadd"][g - 1])

                    def t_dec(e, selg=selg):
                        return e.matmul(pb[6][:, :], dec_t[0:64, :], selg,
                                        start=True, stop=True)
                    tdec = p.o("tensor", t_dec, "tE")
                    if g >= 1:
                        p.w("tensor", "vE", st4["ysc"][g - 1])

                    def t_cb(e, bAP=bAP, cAP=cAP):
                        return e.matmul(pb[7][:, 0:128], bAP, cAP,
                                        start=True, stop=True)
                    tcb = p.o("tensor", t_cb, "tE")
                    p.w("vector", "tE", tdec)
                    p.w("vector", "vE")
                    st4["xdec"][g] = p.o("vector", lambda e, gsl=gsl: e.tensor_tensor(
                        xdec[:, gsl], xtok[:, gsl], pb[6][:, :], OP.mult), "vE")
                    for q in range(2):
                        p.w("tensor", "sE", st4["lexq"])
                        hi = hi_t[0:64, :]
                        lo = lo_t[0:64, :]

                        def t_lm(e, g=g, q=q, hi=hi, lo=lo):
                            for j in range(4):
                                h = 8 * g + 4 * q + j
                                selh = id64[0:64, h:h + 1].broadcast_to([64, 128])
                                nselh = nid64[0:64, h:h + 1].broadcast_to([64, 128])
                                sl = pb[4][:, j * 128:(j + 1) * 128]
                                e.matmul(sl, selh, hi, start=(j == 0), stop=False)
                                e.matmul(sl, selh, lo, start=False, stop=False)
                                e.matmul(sl, hi, nselh, start=False, stop=False)
                                e.matmul(sl, lo, nselh, start=False, stop=False)
                            c2r = c2m[:, :].unsqueeze(1).broadcast_to([128, 4, 128])
                            return e.matmul(
                                pb[4][:, 0:512].rearrange("p (j t) -> p j t", t=128),
                                c1m[:, :], c2r, start=False, stop=True)
                        tlq = p.o("tensor", t_lm, "tE")
                        p.w("scalar", "tE", tlq)
                        p.w("scalar", "vE", st4["mcbq"][q])
                        st4["lexq"] = p.o("scalar", lambda e: e.activation(
                            lexp[:, :], pb[4][:, 0:512], AF.Exp), "sE")
                        p.w("vector", "sE", st4["lexq"])
                        p.w("vector", "tE", max(st4["ydq"][q], tcb))
                        cbr = pb[7][:, 0:128].unsqueeze(1).broadcast_to([128, 4, 128])
                        st4["mcbq"][q] = p.o("vector",
                                             lambda e, q=q, cbr=cbr: e.tensor_tensor(
                            mcbq[q][:, :].rearrange("p (j t) -> p j t", t=128),
                            lexp[:, :].rearrange("p (j t) -> p j t", t=128),
                            cbr, OP.mult), "vE")
                        p.w("tensor", "vE", st4["mcbq"][q])
                        if q == 0 and g >= 1:
                            p.w("tensor", "vE", st4["yadd"][g - 1])

                        def t_yd(e, g=g, q=q):
                            last = None
                            for j in range(4):
                                h8 = 4 * q + j
                                last = e.matmul(
                                    pb[5][:, h8 * 64:(h8 + 1) * 64],
                                    mcbq[q][:, j * 128:(j + 1) * 128],
                                    xtok[:, g * 512 + h8 * 64:g * 512 + (h8 + 1) * 64],
                                    start=True, stop=True)
                            return last
                        st4["ydq"][q] = p.o("tensor", t_yd, "tE")
                    p.w("tensor", "vE", st4["xdec"][g])

                    def t_st(e, g=g, gsl=gsl):
                        return e.matmul(pb[6][:, :], btok[:, g * 128:(g + 1) * 128],
                                        xdec[:, gsl], start=True, stop=True)
                    st4["states"][g] = p.o("tensor", t_st, "tE")
                    p.w("tensor", "vE", max(st4["hbfc"][g], st4["mcbq"][1]))

                    def t_yc(e, cAP=cAP, gsl=gsl):
                        return e.matmul(pb[7][:, :], cAP, hbf[:, gsl],
                                        start=True, stop=True)
                    tyc = p.o("tensor", t_yc, "tE")
                    p.w("vector", "vE")
                    cdr = cdx[:, 8 * g:8 * g + 8].unsqueeze(2).broadcast_to(
                        [128, 8, 64])
                    st4["htmp"][g] = p.o("vector", lambda e, gsl=gsl, cdr=cdr:
                                         e.tensor_tensor(
                        htmp[:, :].rearrange("p (j d) -> p j d", d=64),
                        h32[:, gsl].rearrange("p (j d) -> p j d", d=64),
                        cdr, OP.mult), "vE")
                    p.w("vector", "tE", tyc)
                    p.w("vector", "vE")
                    exr = exp_tok[:, 8 * g:8 * g + 8].unsqueeze(2).broadcast_to(
                        [128, 8, 64])
                    st4["ysc"][g] = p.o("vector", lambda e, exr=exr: e.tensor_tensor(
                        ysc[:, :].rearrange("p (j d) -> p j d", d=64),
                        pb[7][:, :].rearrange("p (j d) -> p j d", d=64),
                        exr, OP.mult), "vE")
                    p.w("vector", "tE", st4["ydq"][1])
                    p.w("vector", "vE")
                    st4["yadd"][g] = p.o("vector", lambda e, gsl=gsl: e.tensor_tensor(
                        ytok[:, gsl], pb[5][:, :], ysc[:, :], OP.add), "vE")
                    p.w("vector", "tE", st4["states"][g])
                    p.w("vector", "vE")
                    st4["hadd"][g] = p.o("vector", lambda e, gsl=gsl: e.tensor_tensor(
                        h32[:, gsl], htmp[:, :], pb[6][:, :], OP.add), "vE")
                    p.w("vector", "tE", tyc)
                    p.w("vector", "vE")
                    st4["hbfc"][g] = p.o("vector", lambda e, gsl=gsl: e.tensor_copy(
                        out=hbf[:, gsl], in_=h32[:, gsl]), "vE")
                    yield

                ydump = []
                if debug:
                    p.w("sync", "vE", st4["yadd"][7])
                    ydump.append(gD.issue("sync", dbg["dbg_y"][c], ytok[:, :]))
                p.w("scalar", "vE", st4["yadd"][7])
                p.w("scalar", "tE", st4["states"][7])
                ssq = p.o("scalar", lambda e: e.activation(xdec[:, :], ytok[:, :],
                                                           AF.Square), "sE")
                p.w("vector", "sE", ssq)
                p.o("vector", lambda e: e.reduce_sum(yrms[:, 0:1], xdec[:, :],
                                                     axis=AX.X), "vE")
                p.w("vector", "vE")
                p.o("vector", lambda e: e.tensor_scalar(
                    yrms[:, 1:2], yrms[:, 0:1], 1.0 / DIN, 1e-6, OP.mult, OP.add),
                    "vE")
                p.w("scalar", "vE", p.cnt["vE"])
                p.o("scalar", lambda e: e.activation(yrms[:, 2:3], yrms[:, 1:2],
                                                     AF.Ln), "sE")
                p.w("scalar", "sE")
                p.o("scalar", lambda e: e.activation(yrms[:, 3:4], yrms[:, 2:3],
                                                     AF.Exp, scale=-0.5), "sE")
                p.w("scalar", "sE")
                syn = p.o("scalar", lambda e: e.activation(
                    xdec[:, :], ytok[:, :], AF.Copy, scale=yrms[:, 3:4]), "sE")
                tevb = {0: syn, 1: syn}
                for bi, b0 in enumerate(range(0, 32, 8)):
                    pTb = pb[6][:, :].bitcast(BF16) if bi % 2 == 0 else \
                        pb[4][:, :].bitcast(BF16)
                    p.w("tensor", "sE", tevb[bi % 2])
                    if b0 == 0:
                        p.w("tensor", "vE", st4["hadd"][7])

                    def t_trb(e, b0=b0, pT=pTb):
                        last = None
                        for j in range(8):
                            last = e.transpose(pT[:, j * 128:(j + 1) * 128],
                                               xdec[:, (b0 + j) * 128:(b0 + j + 1) * 128],
                                               idb[:, :])
                        return last
                    ttb = p.o("tensor", t_trb, "tE")
                    p.w("scalar", "tE", ttb)
                    if b0 == 0:
                        p.w("scalar", "vE", st4["hadd"][7])

                    def s_trev(e, b0=b0, pT=pTb):
                        last = None
                        for j in range(8):
                            last = e.activation(xtok[:, (b0 + j) * 128:(b0 + j + 1) * 128],
                                                pT[:, j * 128:(j + 1) * 128], AF.Copy)
                        return last
                    tevb[bi % 2] = p.o("scalar", s_trev, "sE")
                tevc = max(tevb.values())
                p.w("vector", "sE", tevc)
                p.w("vector", *st4["scl"][sc][1])
                for pr in ydump:
                    p.w("vector", *pr)
                p.w("vector", "sE", syn)
                p.w("vector", "vE")
                vyg = p.o("vector", lambda e, off=off: e.tensor_tensor(
                    ytok3, xtok[:, :].rearrange("p (m t) -> p m t", t=128),
                    gact[0][:, :].rearrange("p (m t) -> p m t", t=SC)[:, :,
                                                                     off:off + 128],
                    OP.mult), "vE")
                st4["prev_vyg"] = vyg
                p.w("sync", "vE", vyg)
                prs = [gW.issue("sync", yg_s[c].transpose([1, 0, 2]),
                                ytok[:, :].rearrange("p (m t) -> p m t", t=128))]
                if debug:
                    prs.append(gD.issue("sync", dbg["dbg_yg"][c], ytok[:, :]))
                st4["ygstore"].append(prs)
                st4["tlast"] = p.cnt["tE"]
                if c % 2 == 1:
                    st4["sccons"][sc] = [("tE", p.cnt["tE"]), ("vE", p.cnt["vE"]),
                                         ("sE", p.cnt["sE"])]

            ph4_prelude()
            gens = [ph4_chunk(c) for c in range(NCH)]
            todo = [s for g_ in gens for s in [g_]]

        NU = DFF // 128
        gi = 0
        for m in range(NU):
            ph3_unit(m)
            if n_phases >= 6 and m % 4 == 2 and m // 4 < NK:
                mm2 = m // 4
                gG.issue("gpsimd",
                         w2bf_s[mm2].rearrange("p (k c) -> p k c", c=128),
                         w2t[:, mm2 * 128:(mm2 + 1) * 128].rearrange(
                             "(k p) c -> p k c", p=128))
            if n_phases >= 5:
                for _ in range(3):
                    if gi < len(todo):
                        try:
                            next(todo[gi])
                        except StopIteration:
                            gi += 1
        if n_phases >= 5:
            while gi < len(todo):
                try:
                    next(todo[gi])
                except StopIteration:
                    gi += 1
        gW.waitall("sync")
        gD.waitall("sync")

    if n_phases >= 6:
        p.barrier()
        ablock = arena[:, 0:131072].bitcast(BF16).rearrange("p (k t) -> p k t", t=1024)
        o1sem = []
        for tb2 in range(2):
            if tb2 == 1:
                p.w("sync", "vE", p.cnt["vE"])
                p.w("sync", "tE", p.cnt["tE"])
            asem = gS.issue("sync", ablock,
                            a_s[:, :, tb2 * 1024:(tb2 + 1) * 1024].transpose([1, 0, 2]))
            w2sem, t5, v5 = [], [], []
            for m in range(NK):
                if m >= 2:
                    p.w("sync", "tE", t5[m - 2])
                elif tb2 == 1:
                    p.w("sync", "tE")
                w2sem.append(gS.issue("sync", w2slab[m % 2][:, :], w2bf_s[m]))
                if m >= 1:
                    p.w("sync", "vE", v5[m - 1])
                rse = gS.issue("sync", rchunk[:, 0:1024],
                               r_s[m][:, tb2 * 1024:(tb2 + 1) * 1024])
                p.w("tensor", *w2sem[m])
                p.w("tensor", *asem)
                if m >= 1:
                    p.w("tensor", "vE", v5[m - 1])

                def t_mm(e, m=m):
                    last = None
                    for k in range(64):
                        for tb in range(2):
                            last = e.matmul(
                                pb[2 * (m % 2) + tb][:, :],
                                w2slab[m % 2][:, k * 128:(k + 1) * 128],
                                atile[k][:, tb * 512:(tb + 1) * 512],
                                start=(k == 0), stop=(k == 63))
                    return last
                t5.append(p.o("tensor", t_mm, "tE"))
                p.w("vector", "tE", t5[m])
                p.w("vector", *rse)
                if o1sem:
                    p.w("vector", *o1sem[-1])
                p.w("vector", "vE")

                def v_ev(e, m=m):
                    last = None
                    for tb in range(2):
                        last = e.tensor_tensor(
                            ochunk[:, tb * 512:(tb + 1) * 512],
                            pb[2 * (m % 2) + tb][:, :],
                            rchunk[:, tb * 512:(tb + 1) * 512], OP.add)
                    return last
                v5.append(p.o("vector", v_ev, "vE"))
                p.w("sync", "vE", v5[m])
                o1sem.append(gW.issue("sync",
                                      out1_s[m][:, tb2 * 1024:(tb2 + 1) * 1024],
                                      ochunk[:, 0:1024]))
        gW.waitall("sync")

    if n_phases >= 7:
        p.barrier()
        for kf in range(32):
            gS.issue("sync",
                     ygt[kf][:, :].rearrange("p (c t) -> p c t", t=128),
                     yg_s[:, kf].transpose([1, 0, 2]))
        wosem, t6, v6, f6, ostores = [], [], [], [], []
        for m in range(NK):
            if m >= 2:
                p.w("gpsimd", "vE", f6[m - 2])
                p.w("gpsimd", "tE", t6[m - 2])
            src_ap = wot[:, m * 128:(m + 1) * 128].rearrange("(k p) c -> p k c", p=128)
            dst_ap = woslab[m % 2][:, :].rearrange("p (k c) -> p k c", c=128)
            wosem.append(gG.issue("gpsimd", dst_ap, src_ap))
            p.w("vector", *wosem[m])
            if m >= 2:
                p.w("vector", "tE", t6[m - 2])

            def v_fold(e, m=m):
                return e.tensor_tensor(woslab[m % 2][:, :], woslab[m % 2][:, :],
                                       outnw_slab[:, :], OP.mult)
            f6.append(p.o("vector", v_fold, "vE"))
            p.w("tensor", "vE", f6[m])
            if m == 0:
                gS.waitall("tensor")
            if m >= 1:
                p.w("tensor", "vE", v6[m - 1])

            def t_mm(e, m=m):
                last = None
                for k in range(32):
                    for tb in range(4):
                        last = e.matmul(pb[tb][:, :],
                                        woslab[m % 2][:, k * 128:(k + 1) * 128],
                                        ygt[k][:, tb * 512:(tb + 1) * 512],
                                        start=(k == 0), stop=(k == 31))
                return last
            t6.append(p.o("tensor", t_mm, "tE"))
            if m >= 1:
                p.w("sync", "vE", v6[m - 1])
            ose = gS.issue("sync", rchunk[:, 0:1024], out1_s[m][:, 0:1024])
            ose2 = gS.issue("sync", rchunk[:, 1024:2048], out1_s[m][:, 1024:2048])
            p.w("vector", "tE", t6[m])
            p.w("vector", *ose)
            p.w("vector", *ose2)
            if ostores:
                p.w("vector", *ostores[-1])
            p.w("vector", "vE")

            def v_fin(e, m=m):
                last = None
                for tb in range(4):
                    last = e.tensor_tensor(ochunk[:, tb * 512:(tb + 1) * 512],
                                           pb[tb][:, :],
                                           rchunk[:, tb * 512:(tb + 1) * 512], OP.add)
                return last
            v6.append(p.o("vector", v_fin, "vE"))
            p.w("sync", "vE", v6[m])
            ostores.append(gW.issue("sync", outT[m * 128:(m + 1) * 128, :],
                                    ochunk[:, :]))
        gW.waitall("sync")
        gW.waitall("vector")

    with ExitStack() as st:
        sems = {n: st.enter_context(nc.semaphore(n)) for n in p.names}
        block = st.enter_context(nc.Block())
        for ename in ENGINES:
            def mk(en):
                def go(eng):
                    for it in p.q[en]:
                        if it[0] == "w":
                            eng.wait_ge(sems[it[1]], it[2])
                        else:
                            inst = it[1](eng)
                            if it[2] is not None:
                                assert inst is not None, f"op returned None on {en}"
                                inst.then_inc(sems[it[2]], it[3])
                return go
            getattr(block, ename)(mk(ename))
    return nc


def _prep_inputs(L0_attn_x, L0_attn_norm_w, L0_attn_in_proj_w, L0_attn_conv_w,
                 L0_attn_conv_b, L0_attn_conv_norm_w, L0_attn_A_log, L0_attn_dt_w,
                 L0_attn_dt_b, L0_attn_out_norm_w, L0_attn_out_proj_w,
                 L0_ffn_W1, L0_ffn_W2):
    f = np.float32
    x = np.asarray(L0_attn_x, f)
    w_in = np.asarray(L0_attn_in_proj_w, f)
    perm = np.concatenate([
        np.arange(0, DIN),
        np.arange(DIN + H, DIN + H + GN),
        np.arange(DIN + H + GN, DIN + H + 2 * GN),
        np.arange(DIN + H + 2 * GN, NPROJ),
        np.arange(DIN, DIN + H),
    ])
    shared = {
        "w_in": np.ascontiguousarray(w_in[perm].T),
        "w1t": np.ascontiguousarray(np.asarray(L0_ffn_W1, f).T),
        "w2t": np.ascontiguousarray(np.asarray(L0_ffn_W2, f).T),
        "wot": np.ascontiguousarray(np.asarray(L0_attn_out_proj_w, f).T),
        "convw": np.ascontiguousarray(np.asarray(L0_attn_conv_w, f).reshape(CONV, 4)),
        "convb": np.ascontiguousarray(np.asarray(L0_attn_conv_b, f)),
        "convnw": np.ascontiguousarray(np.asarray(L0_attn_conv_norm_w, f)),
        "normw": np.ascontiguousarray(np.asarray(L0_attn_norm_w, f)),
        "outnw": np.ascontiguousarray(np.asarray(L0_attn_out_norm_w, f)),
        "alog": np.ascontiguousarray(np.asarray(L0_attn_A_log, f)),
        "dtb": np.ascontiguousarray(np.asarray(L0_attn_dt_b, f)),
        "dtwt": np.ascontiguousarray(np.asarray(L0_attn_dt_w, f).T),
    }
    in_maps = []
    for c in range(8):
        m = dict(shared)
        m["xT"] = np.ascontiguousarray(x[c].T)
        in_maps.append(m)
    return in_maps


def kernel(**inputs):
    in_maps = _prep_inputs(**inputs)
    nc = build(N_PHASES, DEBUG)
    res = run_bass_kernel_spmd(nc, in_maps, core_ids=list(range(8)))
    out = np.stack([np.asarray(r["outT"]).T for r in res.results], axis=0)
    return np.ascontiguousarray(out.astype(np.float32))


# version 11
# speedup vs baseline: 1.2329x; 1.2174x over previous
import numpy as np
import ml_dtypes
from contextlib import ExitStack
import concourse.bass as bass
import concourse.mybir as mybir
from concourse.bass_utils import run_bass_kernel_spmd

F32 = mybir.dt.float32
BF16 = mybir.dt.bfloat16
U8 = mybir.dt.uint8
AF = mybir.ActivationFunctionType
OP = mybir.AluOpType
AX = mybir.AxisListType

D = 2048
L = 2048
DIN = 4096
H = 64
DH = 64
NST = 128
G = 8
GN = G * NST
CONV = DIN + 2 * GN
NPROJ = DIN + H + 2 * GN + DIN
DFF = 8192
NK = D // 128
NCH = 16
CK = 128
MAGIC = 12582912.0
BIGM = 30000.0

ENGINES = ("sync", "tensor", "vector", "scalar", "gpsimd")

N_PHASES = 7
DEBUG = False


class Sched:
    def __init__(self):
        self.q = {e: [] for e in ENGINES}
        self.cnt = {}
        self.names = []

    def sem(self, *names):
        for n in names:
            self.cnt[n] = 0
            self.names.append(n)

    def w(self, eng, s, v=None):
        v = self.cnt[s] if v is None else v
        if v > 0:
            self.q[eng].append(("w", s, v))

    def o(self, eng, fn, inc=None, n=1):
        self.q[eng].append(("o", fn, inc, n))
        if inc is not None:
            self.cnt[inc] += n
            return self.cnt[inc]
        return None

    def barrier(self):
        snap = [(s, self.cnt[s]) for s in self.names if self.cnt[s] > 0]
        for e in ENGINES:
            for s, v in snap:
                self.q[e].append(("w", s, v))


class DGroup:

    def __init__(self, p, base, n=6):
        self.p = p
        self.names = [f"{base}{i}" for i in range(n)]
        p.sem(*self.names)
        self.i = 0
        self.waited = {s: 0 for s in self.names}

    def issue(self, eng, out_ap, in_ap, **kw):
        s = self.names[self.i % len(self.names)]
        self.i += 1
        if self.p.cnt[s] > self.waited[s]:
            self.p.w(eng, s, self.p.cnt[s])
            self.waited[s] = self.p.cnt[s]
        self.p.o(eng,
                 lambda e, o=out_ap, i=in_ap, kw=kw: e.dma_start(out=o, in_=i, **kw),
                 s, 16)
        return s, self.p.cnt[s]

    def wait_last(self, eng):
        s = self.names[(self.i - 1) % len(self.names)]
        self.p.w(eng, s, self.p.cnt[s])

    def waitall(self, eng, issuer=False):
        for s in self.names:
            self.p.w(eng, s, self.p.cnt[s])
            if issuer:
                self.waited[s] = max(self.waited[s], self.p.cnt[s])


def _sz(dt):
    return {F32: 4, BF16: 2}[dt]


def build(n_phases=N_PHASES, debug=False):
    nc = bass.Bass("TRN2", target_bir_lowering=False)

    xT = nc.dram_tensor("xT", [D, L], F32, kind="ExternalInput").ap()
    w_in = nc.dram_tensor("w_in", [D, NPROJ], F32, kind="ExternalInput").ap()
    w1t = nc.dram_tensor("w1t", [D, DFF], F32, kind="ExternalInput").ap()
    w2t = nc.dram_tensor("w2t", [DFF, D], F32, kind="ExternalInput").ap()
    wot = nc.dram_tensor("wot", [DIN, D], F32, kind="ExternalInput").ap()
    convw = nc.dram_tensor("convw", [CONV, 4], F32, kind="ExternalInput").ap()
    convb = nc.dram_tensor("convb", [CONV], F32, kind="ExternalInput").ap()
    convnw = nc.dram_tensor("convnw", [CONV], F32, kind="ExternalInput").ap()
    normw = nc.dram_tensor("normw", [D], F32, kind="ExternalInput").ap()
    outnw = nc.dram_tensor("outnw", [DIN], F32, kind="ExternalInput").ap()
    alog = nc.dram_tensor("alog", [H], F32, kind="ExternalInput").ap()
    dtb = nc.dram_tensor("dtb", [H], F32, kind="ExternalInput").ap()
    dtwt = nc.dram_tensor("dtwt", [H, H], F32, kind="ExternalInput").ap()
    outT = nc.dram_tensor("outT", [D, L], F32, kind="ExternalOutput").ap()

    r_s = nc.dram_tensor("r_s", [NK, 128, L], F32, kind="Internal").ap()
    craw_s = nc.dram_tensor("craw_s", [CONV // 128, 128, L], BF16, kind="Internal").ap()
    gate_s = nc.dram_tensor("gate_s", [DIN // 128, 128, L], BF16, kind="Internal").ap()
    a_s = nc.dram_tensor("a_s", [DFF // 128, 128, L], BF16, kind="Internal").ap()
    yg_s = nc.dram_tensor("yg_s", [NCH, DIN // 128, 128, CK], BF16, kind="Internal").ap()
    out1_s = nc.dram_tensor("out1_s", [NK, 128, L], F32, kind="Internal").ap()

    dbg = {}
    if debug:
        dbg["dbg_xn"] = nc.dram_tensor("dbg_xn", [D, L], BF16, kind="ExternalOutput").ap()
        dbg["dbg_r"] = nc.dram_tensor("dbg_r", [D, L], F32, kind="ExternalOutput").ap()
        dbg["dbg_craw"] = nc.dram_tensor("dbg_craw", [48, 128, L], BF16, kind="ExternalOutput").ap()
        dbg["dbg_gate"] = nc.dram_tensor("dbg_gate", [32, 128, L], BF16, kind="ExternalOutput").ap()
        dbg["dbg_adt"] = nc.dram_tensor("dbg_adt", [64, L], F32, kind="ExternalOutput").ap()
        dbg["dbg_crms"] = nc.dram_tensor("dbg_crms", [128, L], BF16, kind="ExternalOutput").ap()
        dbg["dbg_a"] = nc.dram_tensor("dbg_a", [DFF // 128, 128, L], BF16, kind="ExternalOutput").ap()
        dbg["dbg_yg"] = nc.dram_tensor("dbg_yg", [NCH, 128, DIN], BF16, kind="ExternalOutput").ap()
        dbg["dbg_y"] = nc.dram_tensor("dbg_y", [NCH, 128, DIN], BF16, kind="ExternalOutput").ap()
        dbg["dbg_acs"] = nc.dram_tensor("dbg_acs", [64, L], F32, kind="ExternalOutput").ap()
        dbg["dbg_h"] = nc.dram_tensor("dbg_h", [128, H * DH], F32, kind="ExternalOutput").ap()
        dbg["dbg_lex"] = nc.dram_tensor("dbg_lex", [8, 128, 128], BF16, kind="ExternalOutput").ap()
        dbg["dbg_mcb"] = nc.dram_tensor("dbg_mcb", [8, 128, 128], BF16, kind="ExternalOutput").ap()
        dbg["dbg_xtok"] = nc.dram_tensor("dbg_xtok", [128, DIN], BF16, kind="ExternalOutput").ap()
        dbg["dbg_cbs"] = nc.dram_tensor("dbg_cbs", [128, 128], BF16, kind="ExternalOutput").ap()

    ones_col = nc.alloc_sbuf_tensor("ones_col", [128, 1], BF16)
    ones_row = nc.alloc_sbuf_tensor("ones_row", [1, 128], F32)
    onesb = nc.alloc_sbuf_tensor("onesb", [128, 128], BF16)
    ones64 = nc.alloc_sbuf_tensor("ones64", [64, 128], BF16)
    ones64f = nc.alloc_sbuf_tensor("ones64f", [64, 128], F32)
    id64 = nc.alloc_sbuf_tensor("id64", [64, 64], BF16)
    nid64 = nc.alloc_sbuf_tensor("nid64", [64, 64], BF16)
    idb = nc.alloc_sbuf_tensor("idb", [128, 128], BF16)
    c1m = nc.alloc_sbuf_tensor("c1m", [128, 128], BF16)
    c2m = nc.alloc_sbuf_tensor("c2m", [128, 128], BF16)
    cst = nc.alloc_sbuf_tensor("cst", [128, 4], F32)
    normw_kp = nc.alloc_sbuf_tensor("normw_kp", [128, NK], F32)
    convw_kp = nc.alloc_sbuf_tensor("convw_kp", [128, 192], F32)
    convb_kp = nc.alloc_sbuf_tensor("convb_kp", [128, 48], F32)
    convnw_kp = nc.alloc_sbuf_tensor("convnw_kp", [128, 48], F32)
    outnw_kp = nc.alloc_sbuf_tensor("outnw_kp", [128, 32], F32)
    norm_slab = nc.alloc_sbuf_tensor("norm_slab", [128, D], BF16)
    outnw_slab = nc.alloc_sbuf_tensor("outnw_slab", [128, DIN], BF16)
    alog_sb = nc.alloc_sbuf_tensor("alog_sb", [64, 1], F32)
    dtb_sb = nc.alloc_sbuf_tensor("dtb_sb", [64, 1], F32)
    dtw_sb = nc.alloc_sbuf_tensor("dtw_sb", [64, 64], F32)
    negea = nc.alloc_sbuf_tensor("negea", [64, 1], F32)

    AB = 196096
    arena = nc.alloc_sbuf_tensor("arena", [128, AB], U8)

    def V(off, elems, dt, p=128):
        assert off % 64 == 0 and off + elems * _sz(dt) <= AB, (off, elems, dt)
        return arena[0:p, off:off + elems * _sz(dt)].bitcast(dt)

    SC = 256
    cact = [V(0, 48 * SC, BF16)]
    gact = [V(24576, 32 * SC, BF16)]
    xtok = V(40960, DIN, BF16)
    xdec = V(49152, DIN, BF16)
    btok = V(57344, GN, BF16)
    ytok = V(59392, DIN, BF16)
    lexp = V(67584, 512, BF16)
    mcbq = [V(68608, 512, BF16), V(69632, 512, BF16)]
    cbs = [V(70656, 128, BF16), V(70912, 128, BF16)]
    eas = V(71168, 512, BF16)
    cdl_t = V(72192, 128, BF16, p=64)
    ysc = V(72448, 512, BF16)
    yrms = V(73472, 16, F32)
    awk = V(73536, 64, F32, p=64)
    htmp = V(73792, 512, F32)
    crmsb = V(75840, L, BF16)
    hi_t = V(79936, CK, BF16, p=64)
    lo_t = V(80192, CK, BF16, p=64)
    exp_t = V(80448, CK, BF16, p=64)
    dec_t = V(80704, CK, BF16, p=64)
    acs = V(80960, L, F32, p=64)
    h32 = V(89152, H * DH, F32)
    hbf = V(105536, H * DH, BF16)
    a_dt = V(89152, L, F32, p=64)
    wk1 = V(97344, 512, F32, p=64)
    wk2 = V(99392, 512, F32, p=64)
    wk3 = V(101440, 512, F32, p=64)
    xn = [V(113728 + k * 4096, D, BF16) for k in range(NK)]
    w1b = [V(179264, D, BF16), V(183360, D, BF16)]
    qf = V(187456, 512, F32)
    qt = V(189504, 512, F32)
    astage = [V(191552, L, BF16)]
    exp_tok = V(195648, 64, BF16)
    cdx = V(195776, 64, BF16)
    dec_tok = V(195904, 64, BF16)
    s_b = V(0, L, F32)
    ms_b = V(8192, L, F32)
    xstage = [V(16384, L, F32), V(24576, L, F32)]
    rstage = [V(32768, L, F32), V(40960, L, F32)]
    xbf_t = [V(49152, L, BF16), V(53248, L, BF16)]
    sq_t = [V(57344, L, BF16), V(61440, L, BF16)]
    rowm = V(65536, L, F32, p=1)
    rowe = V(73728, L, F32, p=1)
    rows_ = V(81920, L, F32, p=1)
    rowt = V(90112, L, F32, p=1)
    rmsf_b = V(0, L, F32)
    wcast = [V(8192, D, BF16), V(12288, D, BF16)]
    wf = [V(16384, D, BF16), V(20480, D, BF16)]
    cslab = V(24576, 2051, BF16)
    ct0 = V(28736, L, BF16)
    ct1 = V(32832, L, BF16)
    ct2 = V(36928, L, BF16)
    csq = V(41024, L, BF16)
    crawt = V(45120, L, BF16)
    gtmp = V(28736, L, F32)
    rowcv = V(49216, L, F32, p=1)
    rowcv2 = V(57408, L, F32, p=1)
    atile = [V(k * 2048, 1024, BF16) for k in range(64)]
    ygt = [V(k * 4096, L, BF16) for k in range(32)]
    w2slab = [V(131072, 64 * 128, BF16), V(147456, 64 * 128, BF16)]
    woslab = [V(131072, 32 * 128, BF16), V(139264, 32 * 128, BF16)]
    ochunk = V(163840, 2048, F32)
    rchunk = V(172032, 2048, F32)

    p = Sched()
    p.sem("tE", "vE", "sE", "gE")
    gS = DGroup(p, "dS")
    gG = DGroup(p, "dG")
    gR = DGroup(p, "dR")
    gD = DGroup(p, "dD")

    if n_phases >= 1:
        gS.issue("sync", normw_kp[:, :], normw.rearrange("(k p) -> p k", p=128),
                 allow_slow_non_contiguous=True)
        gS.issue("sync", convw_kp[:, :].rearrange("p (k j) -> p k j", j=4),
                 convw.rearrange("(k p) j -> p k j", p=128),
                 allow_slow_non_contiguous=True)
        gS.issue("sync", convb_kp[:, :], convb.rearrange("(k p) -> p k", p=128),
                 allow_slow_non_contiguous=True)
        gS.issue("sync", convnw_kp[:, :], convnw.rearrange("(k p) -> p k", p=128),
                 allow_slow_non_contiguous=True)
        gS.issue("sync", outnw_kp[:, :], outnw.rearrange("(k p) -> p k", p=128),
                 allow_slow_non_contiguous=True)
        gS.issue("sync", alog_sb[0:64, :], alog.rearrange("(h o) -> h o", o=1))
        gS.issue("sync", dtb_sb[0:64, :], dtb.rearrange("(h o) -> h o", o=1))
        gS.issue("sync", dtw_sb[0:64, :], dtwt)

        def g_consts_a(e):
            e.memset(ones_col[:, :], 1.0)
            e.memset(onesb[:, :], 1.0)
            e.memset(ones64[0:64, :], 1.0)
            e.memset(ones64f[0:64, :], 1.0)
            e.memset(ones_row[:, :], 1.0)
            e.memset(cst[:, 0:1], 1e-5)
            e.memset(cst[:, 1:2], 1e-6)
            e.memset(cst[:, 2:3], 1.0)
            e.memset(id64[0:64, :], 1.0)
            e.memset(idb[:, :], 1.0)
            e.memset(nid64[0:64, :], -1.0)
            e.memset(c1m[:, :], 1.0)
            return e.memset(c2m[:, :], -BIGM)
        p.o("gpsimd", g_consts_a, "gE")
        p.w("gpsimd", "gE")

        def g_consts_b(e):
            e.affine_select(out=id64[0:64, :], in_=id64[0:64, :], pattern=[[-1, 64]],
                            compare_op=OP.is_equal, fill=0.0, base=0, channel_multiplier=1)
            e.affine_select(out=idb[:, :], in_=idb[:, :], pattern=[[-1, 128]],
                            compare_op=OP.is_equal, fill=0.0, base=0, channel_multiplier=1)
            e.affine_select(out=nid64[0:64, :], in_=nid64[0:64, :], pattern=[[-1, 64]],
                            compare_op=OP.is_equal, fill=0.0, base=0, channel_multiplier=1)
            e.affine_select(out=c1m[:, :], in_=c1m[:, :], pattern=[[1, 128]],
                            compare_op=OP.is_ge, fill=0.0, base=-1, channel_multiplier=-1)
            return e.affine_select(out=c2m[:, :], in_=c2m[:, :], pattern=[[-1, 128]],
                                   compare_op=OP.is_ge, fill=0.0, base=0, channel_multiplier=1)
        p.o("gpsimd", g_consts_b, "gE")

        gS.waitall("scalar")
        p.o("scalar", lambda e: e.activation(negea[0:64, :], alog_sb[0:64, :], AF.Exp),
            "sE")
        gS.waitall("vector")
        p.w("vector", "gE")
        p.w("vector", "sE")

        def v_slabs(e):
            for k in range(NK):
                e.tensor_scalar(norm_slab[:, k * 128:(k + 1) * 128], onesb[:, :],
                                normw_kp[:, k:k + 1], None, OP.mult)
            for k in range(32):
                e.tensor_scalar(outnw_slab[:, k * 128:(k + 1) * 128], onesb[:, :],
                                outnw_kp[:, k:k + 1], None, OP.mult)
            return e.tensor_scalar(negea[0:64, :], negea[0:64, :], -1.0, None, OP.mult)
        p.o("vector", v_slabs, "vE")

    pb = [nc.alloc_psum_tensor(f"pb{i}", [128, 512], F32) for i in range(8)]

    if n_phases >= 2:
        p.w("tensor", "gE")
        for k in range(NK):
            if k >= 2:
                p.w("sync", "sE", p.cnt["sE"] - 1)
            sv = gS.issue("sync", xstage[k % 2][:, :], xT[k * 128:(k + 1) * 128, :])
            p.w("scalar", *sv)
            if k >= 2:
                p.w("scalar", "tE", p.cnt["tE"] - 1)

            def s_cast(e, k=k):
                e.activation(xbf_t[k % 2][:, :], xstage[k % 2][:, :], AF.Copy)
                return e.activation(sq_t[k % 2][:, :], xstage[k % 2][:, :], AF.Square)
            p.o("scalar", s_cast, "sE")
            p.w("tensor", "sE")

            def t_stats(e, k=k):
                for tb in range(4):
                    e.matmul(pb[tb][0:1, :], ones_col[:, :],
                             xbf_t[k % 2][:, tb * 512:(tb + 1) * 512],
                             start=(k == 0), stop=(k == NK - 1))
                last = None
                for tb in range(4):
                    last = e.matmul(pb[4 + tb][0:1, :], ones_col[:, :],
                                    sq_t[k % 2][:, tb * 512:(tb + 1) * 512],
                                    start=(k == 0), stop=(k == NK - 1))
                return last
            p.o("tensor", t_stats, "tE")

        def vch(fn):
            p.w("vector", "vE")
            p.o("vector", fn, "vE")

        def sch(fn):
            p.w("scalar", "vE")
            p.o("scalar", fn, "sE")
            p.w("vector", "sE")

        p.w("vector", "tE")

        def v_fin1(e):
            last = None
            for tb in range(4):
                e.tensor_scalar(rowm[0:1, tb * 512:(tb + 1) * 512], pb[tb][0:1, :],
                                1.0 / D, None, OP.mult)
                last = e.tensor_scalar(rowe[0:1, tb * 512:(tb + 1) * 512],
                                       pb[4 + tb][0:1, :], 1.0 / D, None, OP.mult)
            return last
        p.o("vector", v_fin1, "vE")
        vch(lambda e: e.tensor_tensor(rowt[0:1, :], rowm[0:1, :], rowm[0:1, :], OP.mult))
        vch(lambda e: e.tensor_tensor(rowe[0:1, :], rowe[0:1, :], rowt[0:1, :], OP.subtract))
        sch(lambda e: e.activation(rowt[0:1, :], rowe[0:1, :], AF.Sqrt, bias=cst[0:1, 0:1]))
        vch(lambda e: e.reciprocal(rows_[0:1, :], rowt[0:1, :]))
        vch(lambda e: e.tensor_tensor(rowt[0:1, :], rowe[0:1, :], rows_[0:1, :], OP.mult))
        vch(lambda e: e.tensor_tensor(rowt[0:1, :], rowt[0:1, :], rows_[0:1, :], OP.mult))
        sch(lambda e: e.activation(rowt[0:1, :], rowt[0:1, :], AF.Sqrt, bias=cst[0:1, 1:2]))
        vch(lambda e: e.reciprocal(rowe[0:1, :], rowt[0:1, :]))
        vch(lambda e: e.tensor_tensor(rowm[0:1, :], rowm[0:1, :], rows_[0:1, :], OP.mult))

        for row, dst in ((rows_, s_b), (rowm, ms_b)):
            p.w("tensor", "vE")

            def t_bc(e, row=row):
                last = None
                for tb in range(4):
                    last = e.matmul(pb[tb][:, :], ones_row[:, :],
                                    row[0:1, tb * 512:(tb + 1) * 512],
                                    start=True, stop=True)
                return last
            p.o("tensor", t_bc, "tE")
            p.w("vector", "tE")

            def v_bc(e, dst=dst):
                last = None
                for tb in range(4):
                    last = e.tensor_copy(out=dst[:, tb * 512:(tb + 1) * 512],
                                         in_=pb[tb][:, :])
                return last
            p.o("vector", v_bc, "vE")

        xcons = []
        rsems = []
        p.w("sync", "vE")
        p.w("sync", "sE")
        for k in range(NK):
            if k >= 2:
                p.w("sync", "vE", xcons[k - 2])
            sv = gS.issue("sync", xstage[k % 2][:, :], xT[k * 128:(k + 1) * 128, :])
            p.w("vector", *sv)
            if k >= 2:
                for pr in rsems[k - 2]:
                    p.w("vector", *pr)

            def v_ln1(e, k=k):
                return e.tensor_tensor(rstage[k % 2][:, :], xstage[k % 2][:, :],
                                       s_b[:, :], OP.mult)
            p.o("vector", v_ln1, "vE")
            p.w("vector", "vE")

            def v_ln2a(e, k=k):
                return e.tensor_tensor(xn[k][:, :], rstage[k % 2][:, :], ms_b[:, :],
                                       OP.subtract)
            p.o("vector", v_ln2a, "vE")
            p.w("vector", "vE")

            def v_ln2b(e, k=k):
                return e.tensor_tensor(rstage[k % 2][:, :], rstage[k % 2][:, :],
                                       xstage[k % 2][:, :], OP.add)
            xcons.append(p.o("vector", v_ln2b, "vE"))
            p.w("vector", "vE")

            def v_ln3(e, k=k):
                return e.tensor_tensor(rstage[k % 2][:, :], rstage[k % 2][:, :],
                                       ms_b[:, :], OP.subtract)
            p.o("vector", v_ln3, "vE")
            p.w("sync", "vE")
            prs = []
            if debug:
                gD.issue("sync", dbg["dbg_xn"][k * 128:(k + 1) * 128, :], xn[k][:, :])
                prs.append(gR.issue("sync", dbg["dbg_r"][k * 128:(k + 1) * 128, :],
                                    rstage[k % 2][:, :]))
            prs.append(gR.issue("sync", r_s[k], rstage[k % 2][:, :]))
            rsems.append(prs)
        gR.waitall("sync")
        gD.waitall("sync")

    if n_phases >= 3:
        NM = 81
        p.w("vector", "sE")
        p.w("vector", "tE")
        p.w("gpsimd", "vE")
        gR.waitall("vector")
        gD.waitall("vector")
        gW = DGroup(p, "dW")

        p.w("tensor", "vE")

        def t_bcf(e):
            last = None
            for tb in range(4):
                last = e.matmul(pb[tb][:, :], ones_row[:, :],
                                rowe[0:1, tb * 512:(tb + 1) * 512],
                                start=True, stop=True)
            return last
        p.o("tensor", t_bcf, "tE")
        p.w("vector", "tE")

        def v_bcf(e):
            last = None
            for tb in range(4):
                last = e.tensor_copy(out=rmsf_b[:, tb * 512:(tb + 1) * 512],
                                     in_=pb[tb][:, :])
            return last
        p.o("vector", v_bcf, "vE")
        p.w("vector", "vE")
        p.o("vector", lambda e: e.memset(cslab[:, 0:3], 0.0), "vE")

        wsem, vfold, tmm, vevac = [], [], [], []
        statc = {}
        statt = {}
        cstore, gstore = [], []
        nslab64 = norm_slab[:, :].rearrange("p (k c) -> p k c", c=128)[:, :, 0:64]

        def emit_cast_fold(mm):
            c0 = mm * 128
            w = 64 if mm == NM - 1 else 128
            if mm >= 2:
                p.w("gpsimd", "vE", vfold[mm - 2])
            src_ap = w_in[:, c0:c0 + w].rearrange("(k p) c -> p k c", p=128)
            dst_ap = wcast[mm % 2][:, 0:NK * w].rearrange("p (k c) -> p k c", c=w)
            wsem.append(gG.issue("gpsimd", dst_ap, src_ap))

            p.w("vector", *wsem[mm])
            if mm >= 2:
                p.w("vector", "tE", tmm[mm - 2])

            def v_fold(e, mm=mm, w=w):
                if w == 128:
                    return e.tensor_tensor(wf[mm % 2][:, :], wcast[mm % 2][:, :],
                                           norm_slab[:, :], OP.mult)
                return e.tensor_tensor(
                    wf[mm % 2][:, 0:NK * w].rearrange("p (k c) -> p k c", c=w),
                    wcast[mm % 2][:, 0:NK * w].rearrange("p (k c) -> p k c", c=w),
                    nslab64, OP.mult)
            vfold.append(p.o("vector", v_fold, "vE"))

        emit_cast_fold(0)
        for m in range(NM):
            w = 64 if m == NM - 1 else 128
            if m + 1 < NM:
                emit_cast_fold(m + 1)
            p.w("tensor", "vE", vfold[m])
            if m >= 1:
                p.w("tensor", "vE", vevac[m - 1])

            def t_mm(e, m=m, w=w):
                last = None
                for k in range(NK):
                    for tb in range(4):
                        last = e.matmul(pb[tb][0:w, :],
                                        wf[m % 2][:, k * w:(k + 1) * w],
                                        xn[k][:, tb * 512:(tb + 1) * 512],
                                        start=(k == 0), stop=(k == NK - 1))
                return last
            tmm.append(p.o("tensor", t_mm, "tE"))
            if 1 <= m <= 48:
                ms = m - 1
                p.w("tensor", "sE", statc[ms])

                def t_stat(e, ms=ms):
                    last = None
                    for tb in range(4):
                        last = e.matmul(pb[4 + tb][0:1, :], ones_col[:, :],
                                        csq[:, tb * 512:(tb + 1) * 512],
                                        start=(ms == 0), stop=(ms == 47))
                    return last
                statt[ms] = p.o("tensor", t_stat, "tE")

            p.w("vector", "tE", tmm[m])
            if m < 48:
                def v_ev(e, m=m):
                    last = None
                    for tb in range(4):
                        last = e.tensor_tensor(
                            cslab[:, 3 + tb * 512:3 + (tb + 1) * 512],
                            pb[tb][:, :], rmsf_b[:, tb * 512:(tb + 1) * 512], OP.mult)
                    return last
                vevac.append(p.o("vector", v_ev, "vE"))

                p.w("vector", "vE")
                p.o("vector", lambda e, m=m: e.tensor_scalar(
                    ct0[:, :], cslab[:, 0:L], convw_kp[:, 4 * m:4 * m + 1],
                    None, OP.mult), "vE")
                p.w("vector", "vE")
                if m >= 1:
                    for pr in cstore[m - 1]:
                        p.w("vector", *pr)
                p.o("vector", lambda e, m=m: e.scalar_tensor_tensor(
                    ct1[:, :], cslab[:, 1:1 + L], convw_kp[:, 4 * m + 1:4 * m + 2],
                    ct0[:, :], OP.mult, OP.add), "vE")
                p.w("vector", "vE")
                p.o("vector", lambda e, m=m: e.scalar_tensor_tensor(
                    ct2[:, :], cslab[:, 2:2 + L], convw_kp[:, 4 * m + 2:4 * m + 3],
                    ct1[:, :], OP.mult, OP.add), "vE")
                p.w("vector", "vE")
                if m >= 1:
                    p.w("vector", "sE", statc[m - 1])
                p.o("vector", lambda e, m=m: e.scalar_tensor_tensor(
                    ct0[:, :], cslab[:, 3:3 + L], convw_kp[:, 4 * m + 3:4 * m + 4],
                    ct2[:, :], OP.mult, OP.add), "vE")
                p.w("vector", "vE")
                p.o("vector", lambda e, m=m: e.tensor_scalar(
                    crawt[:, :], ct0[:, :], convb_kp[:, m:m + 1], None, OP.add), "vE")

                p.w("scalar", "vE")
                if m >= 1:
                    p.w("scalar", "tE", statt[m - 1])
                statc[m] = p.o("scalar", lambda e: e.activation(
                    csq[:, :], crawt[:, :], AF.Square), "sE")
                p.w("vector", "vE")
                p.o("vector", lambda e, m=m: e.tensor_scalar(
                    ct1[:, :], crawt[:, :], convnw_kp[:, m:m + 1], None, OP.mult), "vE")
                p.w("sync", "vE")
                prs = [gW.issue("sync", craw_s[m], ct1[:, :])]
                if debug:
                    prs.append(gD.issue("sync", dbg["dbg_craw"][m], ct1[:, :]))
                cstore.append(prs)
            elif m < 80:
                mg = m - 48
                p.w("vector", "vE", p.cnt["vE"])
                if mg >= 1:
                    p.w("vector", "sE", gsil[mg - 1])
                else:
                    for pr in cstore[-1]:
                        p.w("vector", *pr)

                def v_evg(e):
                    last = None
                    for tb in range(4):
                        last = e.tensor_tensor(
                            gtmp[:, tb * 512:(tb + 1) * 512],
                            pb[tb][:, :], rmsf_b[:, tb * 512:(tb + 1) * 512], OP.mult)
                    return last
                vevac.append(p.o("vector", v_evg, "vE"))
                p.w("scalar", "vE")
                p.w("scalar", "sE", p.cnt["sE"])
                if mg >= 1:
                    for pr in gstore[mg - 1]:
                        p.w("scalar", *pr)
                elif cstore:
                    for pr in cstore[-1]:
                        p.w("scalar", *pr)
                if mg == 0:
                    gsil = []
                gsil.append(p.o("scalar",
                                lambda e: e.activation(crawt[:, :], gtmp[:, :], AF.Silu),
                                "sE"))
                p.w("sync", "sE")
                prs = [gW.issue("sync", gate_s[mg], crawt[:, :])]
                if debug:
                    prs.append(gD.issue("sync", dbg["dbg_gate"][mg], crawt[:, :]))
                gstore.append(prs)
            else:
                def v_eva(e):
                    last = None
                    for tb in range(4):
                        last = e.tensor_tensor(
                            a_dt[0:64, tb * 512:(tb + 1) * 512],
                            pb[tb][0:64, :], rmsf_b[0:64, tb * 512:(tb + 1) * 512],
                            OP.mult)
                    return last
                vevac.append(p.o("vector", v_eva, "vE"))
                if debug:
                    p.w("sync", "vE")
                    gD.issue("sync", dbg["dbg_adt"], a_dt[0:64, :])

        p.w("vector", "tE")

        def v_cst1(e):
            last = None
            for tb in range(4):
                last = e.tensor_scalar(rowcv[0:1, tb * 512:(tb + 1) * 512],
                                       pb[4 + tb][0:1, :], 1.0 / CONV, None, OP.mult)
            return last
        p.o("vector", v_cst1, "vE")
        p.w("scalar", "vE")
        p.o("scalar", lambda e: e.activation(rowcv[0:1, :], rowcv[0:1, :], AF.Sqrt,
                                             bias=cst[0:1, 1:2]), "sE")
        p.w("vector", "sE")
        p.o("vector", lambda e: e.reciprocal(rowcv2[0:1, :], rowcv[0:1, :]), "vE")
        p.w("tensor", "vE")

        def t_bcc(e):
            last = None
            for tb in range(4):
                last = e.matmul(pb[4 + tb][:, :], ones_row[:, :],
                                rowcv2[0:1, tb * 512:(tb + 1) * 512],
                                start=True, stop=True)
            return last
        p.o("tensor", t_bcc, "tE")
        p.w("vector", "tE")

        def v_bcc(e):
            last = None
            for tb in range(4):
                last = e.tensor_copy(out=crmsb[:, tb * 512:(tb + 1) * 512],
                                     in_=pb[4 + tb][:, :])
            return last
        p.o("vector", v_bcc, "vE")
        if debug:
            p.w("sync", "vE")
            gD.issue("sync", dbg["dbg_crms"], crmsb[:, :])
        gW.waitall("sync")
        gD.waitall("sync")

    if n_phases >= 4:
        p.w("tensor", "vE")
        p.w("vector", "tE")
        p.w("vector", "sE")
        p.w("scalar", "vE")
        p.w("scalar", "tE")
        gW.waitall("vector")
        gW.waitall("scalar")
        gG.waitall("gpsimd", issuer=True)
        p.w("gpsimd", "vE")
        p.w("gpsimd", "tE")
        gR.waitall("vector")
        gR.waitall("scalar")
        gR.waitall("tensor")
        gD.waitall("vector")
        gD.waitall("scalar")
        p.w("sync", "vE")
        p.w("sync", "tE")
        p.w("sync", "sE")
        gT = DGroup(p, "dX")

        st3 = {"w1sem": [], "tmm": [], "vq": [], "astore": [], "agel": []}

        def ph3_unit(m):
            if m >= 2:
                p.w("gpsimd", "tE", st3["tmm"][m - 2])
            src_ap = w1t[:, m * 128:(m + 1) * 128].rearrange("(k p) c -> p k c", p=128)
            dst_ap = w1b[m % 2][:, :].rearrange("p (k c) -> p k c", c=128)
            st3["w1sem"].append(gG.issue("gpsimd", dst_ap, src_ap))
            p.w("tensor", *st3["w1sem"][m])
            if m >= 1:
                p.w("tensor", "vE", st3["vq"][m - 1])

            def t_mm(e, m=m):
                last = None
                for k in range(NK):
                    for tb in range(4):
                        last = e.matmul(pb[tb][:, :],
                                        w1b[m % 2][:, k * 128:(k + 1) * 128],
                                        xn[k][:, tb * 512:(tb + 1) * 512],
                                        start=(k == 0), stop=(k == NK - 1))
                return last
            st3["tmm"].append(p.o("tensor", t_mm, "tE"))
            p.w("vector", "tE", st3["tmm"][m])
            ag = None
            for tb in range(4):
                p.w("vector", "vE")
                if ag is not None:
                    p.w("vector", "sE", ag)
                elif st3["agel"]:
                    p.w("vector", "sE", st3["agel"][-1])
                p.o("vector", lambda e, tb=tb: e.tensor_scalar(
                    qf[:, :], pb[tb][:, :], 10.0, -32768.49, OP.mult, OP.max), "vE")
                p.w("vector", "vE")
                p.o("vector", lambda e: e.tensor_scalar(
                    qt[:, :], qf[:, :], 32767.49, MAGIC, OP.min, OP.add), "vE")
                p.w("vector", "vE")
                vq = p.o("vector", lambda e: e.tensor_scalar(
                    qf[:, :], qt[:, :], -MAGIC, None, OP.add), "vE")
                p.w("scalar", "vE", vq)
                if tb == 0 and m >= 1:
                    for pr in st3["astore"][m - 1]:
                        p.w("scalar", *pr)
                ag = p.o("scalar", lambda e, tb=tb: e.activation(
                    astage[0][:, tb * 512:(tb + 1) * 512], qf[:, :],
                    AF.Gelu_apprx_tanh, scale=0.1), "sE")
            st3["vq"].append(vq)
            st3["agel"].append(ag)
            p.w("sync", "sE", ag)
            st3["astore"].append([gW.issue("sync", a_s[m], astage[0][:, :])])

        if n_phases >= 5:
            st4 = {"sccons": {}, "scl": {}, "silc": {},
                   "hbfc": [0] * 8, "xdec": [0] * 8, "states": [0] * 8,
                   "yadd": [0] * 8, "ysc": [0] * 8, "htmp": [0] * 8,
                   "hadd": [0] * 8, "cbse": [0, 0], "mcbq": [0, 0],
                   "ydq": [0, 0], "lexq0": 0, "lexq1": 0, "ygstore": [],
                   "prev_vyg": 0, "tlast": 0, "vacs": 0, "easv": 0}
            ytok3 = ytok[:, :].rearrange("p (m t) -> p m t", t=128)

            def ph4_prelude():
                relc = {}
                for cb in range(4):
                    bank = pb[6 + (cb % 2)]
                    if cb >= 2:
                        p.w("tensor", "sE", relc[cb - 2])

                    def t_dt(e, cb=cb, bank=bank):
                        return e.matmul(bank[0:64, :], dtw_sb[0:64, :],
                                        a_dt[0:64, cb * 512:(cb + 1) * 512],
                                        start=True, stop=True)
                    tdt = p.o("tensor", t_dt, "tE")
                    p.w("scalar", "tE", tdt)
                    p.w("scalar", "vE", p.cnt["vE"])
                    p.o("scalar", lambda e, bank=bank: e.activation(
                        wk1[0:64, :], bank[0:64, :], AF.Abs, bias=dtb_sb[0:64, :]),
                        "sE")
                    p.w("scalar", "sE")
                    p.o("scalar", lambda e: e.activation(
                        wk2[0:64, :], wk1[0:64, :], AF.Exp, scale=-1.0), "sE")
                    p.w("scalar", "sE")
                    p.o("scalar", lambda e: e.activation(
                        wk3[0:64, :], wk2[0:64, :], AF.Ln, bias=cst[0:64, 2:3]), "sE")
                    p.w("scalar", "sE")
                    relc[cb] = p.o("scalar", lambda e, bank=bank: e.activation(
                        wk1[0:64, :], bank[0:64, :], AF.Relu, bias=dtb_sb[0:64, :]),
                        "sE")
                    p.w("vector", "sE", relc[cb])
                    p.o("vector", lambda e: e.tensor_tensor(
                        wk2[0:64, :], wk3[0:64, :], wk1[0:64, :], OP.add), "vE")
                    p.w("vector", "vE")
                    p.o("vector", lambda e, cb=cb: e.tensor_scalar(
                        a_dt[0:64, cb * 512:(cb + 1) * 512], wk2[0:64, :],
                        negea[0:64, 0:1], None, OP.mult), "vE")
                p.w("vector", "vE")
                for c in range(NCH):
                    p.o("vector", lambda e, c=c: e.tensor_tensor_scan(
                        acs[0:64, c * CK:(c + 1) * CK], ones64f[0:64, :],
                        a_dt[0:64, c * CK:(c + 1) * CK], 0.0, OP.mult, OP.add), "vE")
                st4["vacs"] = p.cnt["vE"]
                p.w("vector", "vE")
                p.w("vector", "sE")
                p.o("vector", lambda e: e.memset(h32[:, :], 0.0), "vE")
                p.w("vector", "vE")
                p.o("vector", lambda e: e.memset(hbf[:, :], 0.0), "vE")
                st4["hbfc"] = [p.cnt["vE"]] * 8
                if debug:
                    p.w("sync", "vE")
                    gD.issue("sync", dbg["dbg_acs"], acs[0:64, :])

            def ph4_chunk(c):
                sc, off = c // 2, (c % 2) * CK
                yield
                if c % 2 == 0:
                    if sc >= 1:
                        for s_, v_ in st4["sccons"][sc - 1]:
                            p.w("sync", s_, v_)
                    NSUB = 4
                    TW = 48 // NSUB
                    subs = []
                    for i in range(NSUB):
                        subs.append(gS.issue(
                            "sync",
                            cact[0][:, i * TW * SC:(i + 1) * TW * SC].rearrange(
                                "p (m t) -> p m t", t=SC),
                            craw_s[i * TW:(i + 1) * TW, :,
                                   sc * SC:(sc + 1) * SC].transpose([1, 0, 2])))
                    st4["scl"][sc] = [None,
                                      gS.issue("sync",
                                               gact[0][:, :].rearrange(
                                                   "p (m t) -> p m t", t=SC),
                                               gate_s[:, :, sc * SC:(sc + 1) * SC]
                                               .transpose([1, 0, 2]))]
                    sil = None
                    for i in range(NSUB):
                        p.w("vector", *subs[i])
                        p.w("vector", "vE")
                        vcv = p.o("vector", lambda e, sc=sc, i=i, TW=TW:
                                  e.tensor_tensor(
                            cact[0][:, i * TW * SC:(i + 1) * TW * SC].rearrange(
                                "p (m t) -> p m t", t=SC),
                            cact[0][:, i * TW * SC:(i + 1) * TW * SC].rearrange(
                                "p (m t) -> p m t", t=SC),
                            crmsb[:, sc * SC:(sc + 1) * SC].unsqueeze(1)
                            .broadcast_to([128, TW, SC]), OP.mult), "vE")
                        p.w("scalar", "vE", vcv)
                        sil = p.o("scalar", lambda e, i=i, TW=TW: e.activation(
                            cact[0][:, i * TW * SC:(i + 1) * TW * SC],
                            cact[0][:, i * TW * SC:(i + 1) * TW * SC], AF.Silu),
                            "sE")
                    st4["silc"][sc] = sil

                def csl(m, width=CK):
                    return cact[0][:, m * SC + off:m * SC + off + width]

                acs_c = acs[0:64, c * CK:(c + 1) * CK]
                p.w("scalar", "tE", st4["tlast"])
                p.w("scalar", "sE")
                p.o("scalar", lambda e, acs_c=acs_c: e.activation(
                    exp_t[0:64, :], acs_c, AF.Exp), "sE")
                p.o("scalar", lambda e, acs_c=acs_c, c=c: e.activation(
                    dec_t[0:64, :], acs_c, AF.Exp, scale=-1.0,
                    bias=acs[0:64, c * CK + CK - 1:c * CK + CK]), "sE")
                sa = p.o("scalar", lambda e, c=c: e.activation(
                    awk[0:64, 0:1], acs[0:64, c * CK + CK - 1:c * CK + CK], AF.Exp),
                    "sE")
                p.w("vector", "tE", st4["tlast"])
                p.w("vector", "vE")
                p.o("vector", lambda e, acs_c=acs_c: e.tensor_copy(
                    out=hi_t[0:64, :], in_=acs_c), "vE")
                p.w("vector", "vE")
                p.o("vector", lambda e, acs_c=acs_c: e.tensor_tensor(
                    lo_t[0:64, :], acs_c, hi_t[0:64, :], OP.subtract), "vE")
                p.w("vector", "sE", sa)
                vcdl = p.o("vector", lambda e: e.tensor_scalar(
                    cdl_t[0:64, :], ones64[0:64, :], awk[0:64, 0:1], None, OP.mult),
                    "vE")
                satl = p.cnt["sE"]
                vtl = p.cnt["vE"]
                p.w("tensor", "vE", vtl)

                p.w("tensor", "sE", satl)
                if c >= 1:
                    p.w("tensor", "vE", st4["prev_vyg"])
                    p.w("tensor", "sE", st4["lexq0"])
                    for pr in st4["ygstore"][c - 1]:
                        p.w("vector", *pr)
                srcs = ([csl(m) for m in range(32)] + [csl(32 + g) for g in range(8)]
                        + [exp_t[0:64, :], cdl_t[0:64, :], dec_t[0:64, :]])
                dsts = ([xtok[:, m * 128:(m + 1) * 128] for m in range(32)] +
                        [btok[:, g * 128:(g + 1) * 128] for g in range(8)] +
                        [exp_tok[:, :], cdx[:, :], dec_tok[:, :]])
                tevb = {}
                for bi, b0 in enumerate(range(0, 43, 8)):
                    nb = min(8, 43 - b0)
                    bank = 4 + (bi % 4)
                    pTb = pb[bank][:, :].bitcast(BF16)
                    if bi >= 4:
                        p.w("tensor", *tevb[bi - 4])

                    def t_trb(e, b0=b0, nb=nb, srcs=srcs, pT=pTb):
                        last = None
                        for j in range(nb):
                            s_ = srcs[b0 + j]
                            if s_.shape[0] == 64:
                                last = e.transpose(pT[:, j * 128:j * 128 + 64],
                                                   s_, id64[0:64, :])
                            else:
                                last = e.transpose(pT[:, j * 128:(j + 1) * 128],
                                                   s_, idb[:, :])
                        return last
                    ttb = p.o("tensor", t_trb, "tE")
                    ee = "scalar" if bi % 2 == 0 else "vector"
                    p.w(ee, "tE", ttb)
                    if bi < 2 and c >= 1:
                        p.w(ee, "vE", st4["prev_vyg"])

                    def x_trev(e, b0=b0, nb=nb, dsts=dsts, pT=pTb, ee=ee):
                        last = None
                        for j in range(nb):
                            d_ = dsts[b0 + j]
                            w_ = d_.shape[-1]
                            if ee == "scalar":
                                last = e.activation(d_, pT[:, j * 128:j * 128 + w_],
                                                    AF.Copy)
                            else:
                                last = e.tensor_copy(
                                    out=d_, in_=pT[:, j * 128:j * 128 + w_])
                        return last
                    tevb[bi] = ("sE" if ee == "scalar" else "vE",
                                p.o(ee, x_trev, "sE" if ee == "scalar" else "vE"))
                sevf = max(v for k, v in tevb.values() if k == "sE")
                vevf = max(v for k, v in tevb.values() if k == "vE")
                p.w("tensor", "sE", sevf)
                p.w("tensor", "vE", vevf)
                p.w("vector", "sE", sevf)
                p.w("vector", "vE", vevf)
                p.w("scalar", "vE", vevf)
                yield

                for g in range(8):
                    gsl = slice(g * 512, (g + 1) * 512)
                    bAP, cAP = csl(32 + g), csl(40 + g)
                    if g >= 1:
                        p.w("tensor", "vE", st4["hadd"][g - 1])

                    def t_cb(e, bAP=bAP, cAP=cAP):
                        return e.matmul(pb[6][:, 0:128], bAP, cAP,
                                        start=True, stop=True)
                    tcb = p.o("tensor", t_cb, "tE")
                    p.w("vector", "vE")
                    decr = dec_tok[:, 8 * g:8 * g + 8].unsqueeze(2).broadcast_to(
                        [128, 8, 64])
                    st4["xdec"][g] = p.o("vector", lambda e, gsl=gsl, decr=decr:
                                         e.tensor_tensor(
                        xdec[:, gsl].rearrange("p (j d) -> p j d", d=64),
                        xtok[:, gsl].rearrange("p (j d) -> p j d", d=64),
                        decr, OP.mult), "vE")
                    hi = hi_t[0:64, :]
                    lo = lo_t[0:64, :]

                    def t_lm(e, g=g, q=0, hi=hi, lo=lo, bank=4):
                        for j in range(4):
                            h = 8 * g + 4 * q + j
                            selh = id64[0:64, h:h + 1].broadcast_to([64, 128])
                            nselh = nid64[0:64, h:h + 1].broadcast_to([64, 128])
                            sl = pb[bank][:, j * 128:(j + 1) * 128]
                            e.matmul(sl, selh, hi, start=(j == 0), stop=False)
                            e.matmul(sl, selh, lo, start=False, stop=False)
                            e.matmul(sl, hi, nselh, start=False, stop=False)
                            e.matmul(sl, lo, nselh, start=False, stop=False)
                        c2r = c2m[:, :].unsqueeze(1).broadcast_to([128, 4, 128])
                        return e.matmul(
                            pb[bank][:, 0:512].rearrange("p (j t) -> p j t", t=128),
                            c1m[:, :], c2r, start=False, stop=True)

                    if g >= 1:
                        p.w("tensor", "sE", st4["lexq0"])
                    tl0 = p.o("tensor", lambda e, f=t_lm: f(e, q=0, bank=4), "tE")
                    if g >= 1:
                        p.w("tensor", "vE", st4["ysc"][g - 1])
                    tl1 = p.o("tensor", lambda e, f=t_lm: f(e, q=1, bank=7), "tE")

                    p.w("scalar", "tE", tl0)
                    p.w("scalar", "vE", st4["mcbq"][0])
                    st4["lexq0"] = p.o("scalar", lambda e: e.activation(
                        lexp[:, :], pb[4][:, 0:512], AF.Exp), "sE")
                    p.w("vector", "sE", st4["lexq0"])
                    p.w("vector", "tE", max(st4["ydq"][0], tcb))
                    cbr = pb[6][:, 0:128].unsqueeze(1).broadcast_to([128, 4, 128])
                    st4["mcbq"][0] = p.o("vector",
                                         lambda e, cbr=cbr: e.tensor_tensor(
                        mcbq[0][:, :].rearrange("p (j t) -> p j t", t=128),
                        lexp[:, :].rearrange("p (j t) -> p j t", t=128),
                        cbr, OP.mult), "vE")
                    p.w("scalar", "tE", tl1)
                    p.w("scalar", "vE", max(st4["mcbq"][0], st4["mcbq"][1]))
                    st4["lexq1"] = p.o("scalar", lambda e: e.activation(
                        lexp[:, :], pb[7][:, 0:512], AF.Exp), "sE")

                    p.w("tensor", "vE", st4["mcbq"][0])
                    if g >= 1:
                        p.w("tensor", "vE", st4["yadd"][g - 1])

                    def t_yd(e, g=g, q=0):
                        last = None
                        for j in range(4):
                            h8 = 4 * q + j
                            last = e.matmul(
                                pb[5][:, h8 * 64:(h8 + 1) * 64],
                                mcbq[q][:, j * 128:(j + 1) * 128],
                                xtok[:, g * 512 + h8 * 64:g * 512 + (h8 + 1) * 64],
                                start=True, stop=True)
                        return last
                    st4["ydq"][0] = p.o("tensor", lambda e, f=t_yd: f(e, q=0), "tE")

                    p.w("vector", "sE", st4["lexq1"])
                    p.w("vector", "tE", st4["ydq"][1])
                    st4["mcbq"][1] = p.o("vector",
                                         lambda e, cbr=cbr: e.tensor_tensor(
                        mcbq[1][:, :].rearrange("p (j t) -> p j t", t=128),
                        lexp[:, :].rearrange("p (j t) -> p j t", t=128),
                        cbr, OP.mult), "vE")
                    p.w("tensor", "vE", st4["mcbq"][1])
                    st4["ydq"][1] = p.o("tensor", lambda e, f=t_yd: f(e, q=1), "tE")
                    p.w("tensor", "vE", st4["xdec"][g])

                    def t_st(e, g=g, gsl=gsl):
                        return e.matmul(pb[6][:, :], btok[:, g * 128:(g + 1) * 128],
                                        xdec[:, gsl], start=True, stop=True)
                    st4["states"][g] = p.o("tensor", t_st, "tE")
                    p.w("tensor", "sE", st4["lexq1"])
                    p.w("tensor", "vE", st4["hbfc"][g])

                    def t_yc(e, cAP=cAP, gsl=gsl):
                        return e.matmul(pb[7][:, :], cAP, hbf[:, gsl],
                                        start=True, stop=True)
                    tyc = p.o("tensor", t_yc, "tE")
                    p.w("vector", "vE")
                    cdr = cdx[:, 8 * g:8 * g + 8].unsqueeze(2).broadcast_to(
                        [128, 8, 64])
                    st4["htmp"][g] = p.o("vector", lambda e, gsl=gsl, cdr=cdr:
                                         e.tensor_tensor(
                        htmp[:, :].rearrange("p (j d) -> p j d", d=64),
                        h32[:, gsl].rearrange("p (j d) -> p j d", d=64),
                        cdr, OP.mult), "vE")
                    p.w("vector", "tE", tyc)
                    p.w("vector", "vE")
                    exr = exp_tok[:, 8 * g:8 * g + 8].unsqueeze(2).broadcast_to(
                        [128, 8, 64])
                    st4["ysc"][g] = p.o("vector", lambda e, exr=exr: e.tensor_tensor(
                        ysc[:, :].rearrange("p (j d) -> p j d", d=64),
                        pb[7][:, :].rearrange("p (j d) -> p j d", d=64),
                        exr, OP.mult), "vE")
                    p.w("vector", "tE", st4["ydq"][1])
                    p.w("vector", "vE")
                    st4["yadd"][g] = p.o("vector", lambda e, gsl=gsl: e.tensor_tensor(
                        ytok[:, gsl], pb[5][:, :], ysc[:, :], OP.add), "vE")
                    p.w("vector", "tE", st4["states"][g])
                    p.w("vector", "vE")
                    st4["hadd"][g] = p.o("vector", lambda e, gsl=gsl: e.tensor_tensor(
                        h32[:, gsl], htmp[:, :], pb[6][:, :], OP.add), "vE")
                    p.w("vector", "tE", tyc)
                    p.w("vector", "vE")
                    st4["hbfc"][g] = p.o("vector", lambda e, gsl=gsl: e.tensor_copy(
                        out=hbf[:, gsl], in_=h32[:, gsl]), "vE")
                    yield

                ydump = []
                if debug:
                    p.w("sync", "vE", st4["yadd"][7])
                    ydump.append(gD.issue("sync", dbg["dbg_y"][c], ytok[:, :]))
                p.w("scalar", "vE", st4["yadd"][7])
                p.w("scalar", "tE", st4["states"][7])
                ssq = p.o("scalar", lambda e: e.activation(xdec[:, :], ytok[:, :],
                                                           AF.Square), "sE")
                p.w("vector", "sE", ssq)
                p.o("vector", lambda e: e.reduce_sum(yrms[:, 0:1], xdec[:, :],
                                                     axis=AX.X), "vE")
                p.w("vector", "vE")
                p.o("vector", lambda e: e.tensor_scalar(
                    yrms[:, 1:2], yrms[:, 0:1], 1.0 / DIN, 1e-6, OP.mult, OP.add),
                    "vE")
                p.w("scalar", "vE", p.cnt["vE"])
                p.o("scalar", lambda e: e.activation(yrms[:, 2:3], yrms[:, 1:2],
                                                     AF.Ln), "sE")
                p.w("scalar", "sE")
                p.o("scalar", lambda e: e.activation(yrms[:, 3:4], yrms[:, 2:3],
                                                     AF.Exp, scale=-0.5), "sE")
                p.w("scalar", "sE")
                syn = p.o("scalar", lambda e: e.activation(
                    xdec[:, :], ytok[:, :], AF.Copy, scale=yrms[:, 3:4]), "sE")
                tevb2 = {}
                for bi, b0 in enumerate(range(0, 32, 8)):
                    bank = 4 + bi
                    pTb = pb[bank][:, :].bitcast(BF16)
                    if bi == 0:
                        p.w("tensor", "sE", syn)
                        p.w("tensor", "vE", st4["hadd"][7])

                    def t_trb(e, b0=b0, pT=pTb):
                        last = None
                        for j in range(8):
                            last = e.transpose(pT[:, j * 128:(j + 1) * 128],
                                               xdec[:, (b0 + j) * 128:(b0 + j + 1) * 128],
                                               idb[:, :])
                        return last
                    ttb = p.o("tensor", t_trb, "tE")
                    ee = "scalar" if bi % 2 == 0 else "vector"
                    p.w(ee, "tE", ttb)
                    if bi < 2:
                        p.w(ee, "sE" if ee == "vector" else "vE",
                            syn if ee == "vector" else st4["hadd"][7])

                    def x_trev(e, b0=b0, pT=pTb, ee=ee):
                        last = None
                        for j in range(8):
                            d_ = xtok[:, (b0 + j) * 128:(b0 + j + 1) * 128]
                            if ee == "scalar":
                                last = e.activation(d_, pT[:, j * 128:(j + 1) * 128],
                                                    AF.Copy)
                            else:
                                last = e.tensor_copy(
                                    out=d_, in_=pT[:, j * 128:(j + 1) * 128])
                        return last
                    tevb2[bi] = ("sE" if ee == "scalar" else "vE",
                                 p.o(ee, x_trev, "sE" if ee == "scalar" else "vE"))
                p.w("vector", "sE", max(v for k, v in tevb2.values() if k == "sE"))
                p.w("vector", *st4["scl"][sc][1])
                for pr in ydump:
                    p.w("vector", *pr)
                p.w("vector", "sE", syn)
                p.w("vector", "vE")
                vyg = p.o("vector", lambda e, off=off: e.tensor_tensor(
                    ytok3, xtok[:, :].rearrange("p (m t) -> p m t", t=128),
                    gact[0][:, :].rearrange("p (m t) -> p m t", t=SC)[:, :,
                                                                     off:off + 128],
                    OP.mult), "vE")
                st4["prev_vyg"] = vyg
                p.w("sync", "vE", vyg)
                prs = [gW.issue("sync", yg_s[c].transpose([1, 0, 2]),
                                ytok[:, :].rearrange("p (m t) -> p m t", t=128))]
                if debug:
                    prs.append(gD.issue("sync", dbg["dbg_yg"][c], ytok[:, :]))
                st4["ygstore"].append(prs)
                st4["tlast"] = p.cnt["tE"]
                if c % 2 == 1:
                    st4["sccons"][sc] = [("tE", p.cnt["tE"]), ("vE", p.cnt["vE"]),
                                         ("sE", p.cnt["sE"])]

            ph4_prelude()
            gens = [ph4_chunk(c) for c in range(NCH)]
            todo = [s for g_ in gens for s in [g_]]

        NU = DFF // 128
        gi = 0
        for m in range(NU):
            ph3_unit(m)
            if n_phases >= 5:
                for _ in range(3):
                    if gi < len(todo):
                        try:
                            next(todo[gi])
                        except StopIteration:
                            gi += 1
        if n_phases >= 5:
            while gi < len(todo):
                try:
                    next(todo[gi])
                except StopIteration:
                    gi += 1
        gW.waitall("sync")
        gD.waitall("sync")

    if n_phases >= 6:
        p.barrier()
        ablock = arena[:, 0:131072].bitcast(BF16).rearrange("p (k t) -> p k t", t=1024)
        o1sem = []
        for tb2 in range(2):
            if tb2 == 1:
                p.w("sync", "vE", p.cnt["vE"])
                p.w("sync", "tE", p.cnt["tE"])
            asem = gS.issue("sync", ablock,
                            a_s[:, :, tb2 * 1024:(tb2 + 1) * 1024].transpose([1, 0, 2]))
            w2sem, t5, v5 = [], [], []
            for m in range(NK):
                if m >= 2:
                    p.w("gpsimd", "tE", t5[m - 2])
                elif tb2 == 1:
                    p.w("gpsimd", "tE", p.cnt["tE"] - NK + m if False else 0) or \
                        p.w("gpsimd", "tE")
                src_ap = w2t[:, m * 128:(m + 1) * 128].rearrange(
                    "(k p) c -> p k c", p=128)
                dst_ap = w2slab[m % 2][:, :].rearrange("p (k c) -> p k c", c=128)
                w2sem.append(gG.issue("gpsimd", dst_ap, src_ap))
                if m >= 1:
                    p.w("sync", "vE", v5[m - 1])
                rse = gS.issue("sync", rchunk[:, 0:1024],
                               r_s[m][:, tb2 * 1024:(tb2 + 1) * 1024])
                p.w("tensor", *w2sem[m])
                p.w("tensor", *asem)
                if m >= 1:
                    p.w("tensor", "vE", v5[m - 1])

                def t_mm(e, m=m):
                    last = None
                    for k in range(64):
                        for tb in range(2):
                            last = e.matmul(
                                pb[2 * (m % 2) + tb][:, :],
                                w2slab[m % 2][:, k * 128:(k + 1) * 128],
                                atile[k][:, tb * 512:(tb + 1) * 512],
                                start=(k == 0), stop=(k == 63))
                    return last
                t5.append(p.o("tensor", t_mm, "tE"))
                p.w("vector", "tE", t5[m])
                p.w("vector", *rse)
                if o1sem:
                    p.w("vector", *o1sem[-1])
                p.w("vector", "vE")

                def v_ev(e, m=m):
                    last = None
                    for tb in range(2):
                        last = e.tensor_tensor(
                            ochunk[:, tb * 512:(tb + 1) * 512],
                            pb[2 * (m % 2) + tb][:, :],
                            rchunk[:, tb * 512:(tb + 1) * 512], OP.add)
                    return last
                v5.append(p.o("vector", v_ev, "vE"))
                p.w("sync", "vE", v5[m])
                o1sem.append(gW.issue("sync",
                                      out1_s[m][:, tb2 * 1024:(tb2 + 1) * 1024],
                                      ochunk[:, 0:1024]))
        gW.waitall("sync")

    if n_phases >= 7:
        p.barrier()
        for kf in range(32):
            gS.issue("sync",
                     ygt[kf][:, :].rearrange("p (c t) -> p c t", t=128),
                     yg_s[:, kf].transpose([1, 0, 2]))
        wosem, t6, v6, f6, ostores = [], [], [], [], []
        for m in range(NK):
            if m >= 2:
                p.w("gpsimd", "vE", f6[m - 2])
                p.w("gpsimd", "tE", t6[m - 2])
            src_ap = wot[:, m * 128:(m + 1) * 128].rearrange("(k p) c -> p k c", p=128)
            dst_ap = woslab[m % 2][:, :].rearrange("p (k c) -> p k c", c=128)
            wosem.append(gG.issue("gpsimd", dst_ap, src_ap))
            p.w("vector", *wosem[m])
            if m >= 2:
                p.w("vector", "tE", t6[m - 2])

            def v_fold(e, m=m):
                return e.tensor_tensor(woslab[m % 2][:, :], woslab[m % 2][:, :],
                                       outnw_slab[:, :], OP.mult)
            f6.append(p.o("vector", v_fold, "vE"))
            p.w("tensor", "vE", f6[m])
            if m == 0:
                gS.waitall("tensor")
            if m >= 1:
                p.w("tensor", "vE", v6[m - 1])

            def t_mm(e, m=m):
                last = None
                for k in range(32):
                    for tb in range(4):
                        last = e.matmul(pb[tb][:, :],
                                        woslab[m % 2][:, k * 128:(k + 1) * 128],
                                        ygt[k][:, tb * 512:(tb + 1) * 512],
                                        start=(k == 0), stop=(k == 31))
                return last
            t6.append(p.o("tensor", t_mm, "tE"))
            if m >= 1:
                p.w("sync", "vE", v6[m - 1])
            ose = gS.issue("sync", rchunk[:, 0:1024], out1_s[m][:, 0:1024])
            ose2 = gS.issue("sync", rchunk[:, 1024:2048], out1_s[m][:, 1024:2048])
            p.w("vector", "tE", t6[m])
            p.w("vector", *ose)
            p.w("vector", *ose2)
            if ostores:
                p.w("vector", *ostores[-1])
            p.w("vector", "vE")

            def v_fin(e, m=m):
                last = None
                for tb in range(4):
                    last = e.tensor_tensor(ochunk[:, tb * 512:(tb + 1) * 512],
                                           pb[tb][:, :],
                                           rchunk[:, tb * 512:(tb + 1) * 512], OP.add)
                return last
            v6.append(p.o("vector", v_fin, "vE"))
            p.w("sync", "vE", v6[m])
            ostores.append(gW.issue("sync", outT[m * 128:(m + 1) * 128, :],
                                    ochunk[:, :]))
        gW.waitall("sync")
        gW.waitall("vector")

    with ExitStack() as st:
        sems = {n: st.enter_context(nc.semaphore(n)) for n in p.names}
        block = st.enter_context(nc.Block())
        for ename in ENGINES:
            def mk(en):
                def go(eng):
                    for it in p.q[en]:
                        if it[0] == "w":
                            eng.wait_ge(sems[it[1]], it[2])
                        else:
                            inst = it[1](eng)
                            if it[2] is not None:
                                assert inst is not None, f"op returned None on {en}"
                                inst.then_inc(sems[it[2]], it[3])
                return go
            getattr(block, ename)(mk(ename))
    return nc


def _prep_inputs(L0_attn_x, L0_attn_norm_w, L0_attn_in_proj_w, L0_attn_conv_w,
                 L0_attn_conv_b, L0_attn_conv_norm_w, L0_attn_A_log, L0_attn_dt_w,
                 L0_attn_dt_b, L0_attn_out_norm_w, L0_attn_out_proj_w,
                 L0_ffn_W1, L0_ffn_W2):
    f = np.float32
    x = np.asarray(L0_attn_x, f)
    w_in = np.asarray(L0_attn_in_proj_w, f)
    perm = np.concatenate([
        np.arange(0, DIN),
        np.arange(DIN + H, DIN + H + GN),
        np.arange(DIN + H + GN, DIN + H + 2 * GN),
        np.arange(DIN + H + 2 * GN, NPROJ),
        np.arange(DIN, DIN + H),
    ])
    shared = {
        "w_in": np.ascontiguousarray(w_in[perm].T),
        "w1t": np.ascontiguousarray(np.asarray(L0_ffn_W1, f).T),
        "w2t": np.ascontiguousarray(np.asarray(L0_ffn_W2, f).T),
        "wot": np.ascontiguousarray(np.asarray(L0_attn_out_proj_w, f).T),
        "convw": np.ascontiguousarray(np.asarray(L0_attn_conv_w, f).reshape(CONV, 4)),
        "convb": np.ascontiguousarray(np.asarray(L0_attn_conv_b, f)),
        "convnw": np.ascontiguousarray(np.asarray(L0_attn_conv_norm_w, f)),
        "normw": np.ascontiguousarray(np.asarray(L0_attn_norm_w, f)),
        "outnw": np.ascontiguousarray(np.asarray(L0_attn_out_norm_w, f)),
        "alog": np.ascontiguousarray(np.asarray(L0_attn_A_log, f)),
        "dtb": np.ascontiguousarray(np.asarray(L0_attn_dt_b, f)),
        "dtwt": np.ascontiguousarray(np.asarray(L0_attn_dt_w, f).T),
    }
    in_maps = []
    for c in range(8):
        m = dict(shared)
        m["xT"] = np.ascontiguousarray(x[c].T)
        in_maps.append(m)
    return in_maps


def kernel(**inputs):
    in_maps = _prep_inputs(**inputs)
    nc = build(N_PHASES, DEBUG)
    res = run_bass_kernel_spmd(nc, in_maps, core_ids=list(range(8)))
    out = np.stack([np.asarray(r["outT"]).T for r in res.results], axis=0)
    return np.ascontiguousarray(out.astype(np.float32))


# version 13
# speedup vs baseline: 1.3740x; 1.1145x over previous
import numpy as np
import ml_dtypes
from contextlib import ExitStack
import concourse.bass as bass
import concourse.mybir as mybir
from concourse.bass_utils import run_bass_kernel_spmd

F32 = mybir.dt.float32
BF16 = mybir.dt.bfloat16
U8 = mybir.dt.uint8
AF = mybir.ActivationFunctionType
OP = mybir.AluOpType
AX = mybir.AxisListType

D = 2048
L = 2048
DIN = 4096
H = 64
DH = 64
NST = 128
G = 8
GN = G * NST
CONV = DIN + 2 * GN
NPROJ = DIN + H + 2 * GN + DIN
DFF = 8192
NK = D // 128
NCH = 16
CK = 128
MAGIC = 12582912.0
BIGM = 30000.0

ENGINES = ("sync", "tensor", "vector", "scalar", "gpsimd")

N_PHASES = 7
DEBUG = False


class Sched:
    def __init__(self):
        self.q = {e: [] for e in ENGINES}
        self.cnt = {}
        self.names = []

    def sem(self, *names):
        for n in names:
            self.cnt[n] = 0
            self.names.append(n)

    def w(self, eng, s, v=None):
        v = self.cnt[s] if v is None else v
        if v > 0:
            self.q[eng].append(("w", s, v))

    def o(self, eng, fn, inc=None, n=1):
        self.q[eng].append(("o", fn, inc, n))
        if inc is not None:
            self.cnt[inc] += n
            return self.cnt[inc]
        return None

    def barrier(self):
        snap = [(s, self.cnt[s]) for s in self.names if self.cnt[s] > 0]
        for e in ENGINES:
            for s, v in snap:
                self.q[e].append(("w", s, v))


class DGroup:

    def __init__(self, p, base, n=6):
        self.p = p
        self.names = [f"{base}{i}" for i in range(n)]
        p.sem(*self.names)
        self.i = 0
        self.waited = {s: 0 for s in self.names}

    def issue(self, eng, out_ap, in_ap, **kw):
        s = self.names[self.i % len(self.names)]
        self.i += 1
        if self.p.cnt[s] > self.waited[s]:
            self.p.w(eng, s, self.p.cnt[s])
            self.waited[s] = self.p.cnt[s]
        self.p.o(eng,
                 lambda e, o=out_ap, i=in_ap, kw=kw: e.dma_start(out=o, in_=i, **kw),
                 s, 16)
        return s, self.p.cnt[s]

    def wait_last(self, eng):
        s = self.names[(self.i - 1) % len(self.names)]
        self.p.w(eng, s, self.p.cnt[s])

    def waitall(self, eng, issuer=False):
        for s in self.names:
            self.p.w(eng, s, self.p.cnt[s])
            if issuer:
                self.waited[s] = max(self.waited[s], self.p.cnt[s])


def _sz(dt):
    return {F32: 4, BF16: 2}[dt]


def build(n_phases=N_PHASES, debug=False):
    nc = bass.Bass("TRN2", target_bir_lowering=False)

    xT = nc.dram_tensor("xT", [D, L], F32, kind="ExternalInput").ap()
    w_in = nc.dram_tensor("w_in", [D, NPROJ], F32, kind="ExternalInput").ap()
    w1t = nc.dram_tensor("w1t", [D, DFF], F32, kind="ExternalInput").ap()
    w2t = nc.dram_tensor("w2t", [DFF, D], F32, kind="ExternalInput").ap()
    wot = nc.dram_tensor("wot", [DIN, D], F32, kind="ExternalInput").ap()
    convw = nc.dram_tensor("convw", [CONV, 4], F32, kind="ExternalInput").ap()
    convb = nc.dram_tensor("convb", [CONV], F32, kind="ExternalInput").ap()
    convnw = nc.dram_tensor("convnw", [CONV], F32, kind="ExternalInput").ap()
    normw = nc.dram_tensor("normw", [D], F32, kind="ExternalInput").ap()
    outnw = nc.dram_tensor("outnw", [DIN], F32, kind="ExternalInput").ap()
    alog = nc.dram_tensor("alog", [H], F32, kind="ExternalInput").ap()
    dtb = nc.dram_tensor("dtb", [H], F32, kind="ExternalInput").ap()
    dtwt = nc.dram_tensor("dtwt", [H, H], F32, kind="ExternalInput").ap()
    outT = nc.dram_tensor("outT", [D, L], F32, kind="ExternalOutput").ap()

    r_s = nc.dram_tensor("r_s", [NK, 128, L], F32, kind="Internal").ap()
    craw_s = nc.dram_tensor("craw_s", [CONV // 128, 128, L], BF16, kind="Internal").ap()
    gate_s = nc.dram_tensor("gate_s", [DIN // 128, 128, L], BF16, kind="Internal").ap()
    a_s = nc.dram_tensor("a_s", [DFF // 128, 128, L], BF16, kind="Internal").ap()
    yg_s = nc.dram_tensor("yg_s", [NCH, DIN // 128, 128, CK], BF16, kind="Internal").ap()
    out1_s = nc.dram_tensor("out1_s", [NK, 128, L], F32, kind="Internal").ap()
    w2bf_s = nc.dram_tensor("w2bf_s", [NK, 128, 64 * 128], BF16, kind="Internal").ap()
    wobf_s = nc.dram_tensor("wobf_s", [NK, 128, 32 * 128], BF16, kind="Internal").ap()

    dbg = {}
    if debug:
        dbg["dbg_xn"] = nc.dram_tensor("dbg_xn", [D, L], BF16, kind="ExternalOutput").ap()
        dbg["dbg_r"] = nc.dram_tensor("dbg_r", [D, L], F32, kind="ExternalOutput").ap()
        dbg["dbg_craw"] = nc.dram_tensor("dbg_craw", [48, 128, L], BF16, kind="ExternalOutput").ap()
        dbg["dbg_gate"] = nc.dram_tensor("dbg_gate", [32, 128, L], BF16, kind="ExternalOutput").ap()
        dbg["dbg_adt"] = nc.dram_tensor("dbg_adt", [64, L], F32, kind="ExternalOutput").ap()
        dbg["dbg_crms"] = nc.dram_tensor("dbg_crms", [128, L], BF16, kind="ExternalOutput").ap()
        dbg["dbg_a"] = nc.dram_tensor("dbg_a", [DFF // 128, 128, L], BF16, kind="ExternalOutput").ap()
        dbg["dbg_yg"] = nc.dram_tensor("dbg_yg", [NCH, 128, DIN], BF16, kind="ExternalOutput").ap()
        dbg["dbg_y"] = nc.dram_tensor("dbg_y", [NCH, 128, DIN], BF16, kind="ExternalOutput").ap()
        dbg["dbg_acs"] = nc.dram_tensor("dbg_acs", [64, L], F32, kind="ExternalOutput").ap()
        dbg["dbg_h"] = nc.dram_tensor("dbg_h", [128, H * DH], F32, kind="ExternalOutput").ap()
        dbg["dbg_lex"] = nc.dram_tensor("dbg_lex", [8, 128, 128], BF16, kind="ExternalOutput").ap()
        dbg["dbg_mcb"] = nc.dram_tensor("dbg_mcb", [8, 128, 128], BF16, kind="ExternalOutput").ap()
        dbg["dbg_xtok"] = nc.dram_tensor("dbg_xtok", [128, DIN], BF16, kind="ExternalOutput").ap()
        dbg["dbg_cbs"] = nc.dram_tensor("dbg_cbs", [128, 128], BF16, kind="ExternalOutput").ap()

    ones_col = nc.alloc_sbuf_tensor("ones_col", [128, 1], BF16)
    ones_row = nc.alloc_sbuf_tensor("ones_row", [1, 128], F32)
    onesb = nc.alloc_sbuf_tensor("onesb", [128, 128], BF16)
    ones64 = nc.alloc_sbuf_tensor("ones64", [64, 128], BF16)
    ones64f = nc.alloc_sbuf_tensor("ones64f", [64, 128], F32)
    id64 = nc.alloc_sbuf_tensor("id64", [64, 64], BF16)
    nid64 = nc.alloc_sbuf_tensor("nid64", [64, 64], BF16)
    idb = nc.alloc_sbuf_tensor("idb", [128, 128], BF16)
    c1m = nc.alloc_sbuf_tensor("c1m", [128, 128], BF16)
    c2m = nc.alloc_sbuf_tensor("c2m", [128, 128], BF16)
    cst = nc.alloc_sbuf_tensor("cst", [128, 4], F32)
    normw_kp = nc.alloc_sbuf_tensor("normw_kp", [128, NK], F32)
    convw_kp = nc.alloc_sbuf_tensor("convw_kp", [128, 192], F32)
    convb_kp = nc.alloc_sbuf_tensor("convb_kp", [128, 48], F32)
    convnw_kp = nc.alloc_sbuf_tensor("convnw_kp", [128, 48], F32)
    outnw_kp = nc.alloc_sbuf_tensor("outnw_kp", [128, 32], F32)
    norm_slab = nc.alloc_sbuf_tensor("norm_slab", [128, D], BF16)
    outnw_slab = nc.alloc_sbuf_tensor("outnw_slab", [128, DIN], BF16)
    alog_sb = nc.alloc_sbuf_tensor("alog_sb", [64, 1], F32)
    dtb_sb = nc.alloc_sbuf_tensor("dtb_sb", [64, 1], F32)
    dtw_sb = nc.alloc_sbuf_tensor("dtw_sb", [64, 64], F32)
    negea = nc.alloc_sbuf_tensor("negea", [64, 1], F32)

    AB = 196096
    arena = nc.alloc_sbuf_tensor("arena", [128, AB], U8)

    def V(off, elems, dt, p=128):
        assert off % 64 == 0 and off + elems * _sz(dt) <= AB, (off, elems, dt)
        return arena[0:p, off:off + elems * _sz(dt)].bitcast(dt)

    SC = 256
    cact = [V(0, 48 * SC, BF16)]
    gact = [V(24576, 32 * SC, BF16)]
    xtok = V(40960, DIN, BF16)
    xdec = V(49152, DIN, BF16)
    btok = V(57344, GN, BF16)
    ytok = V(59392, DIN, BF16)
    lexp = V(67584, 512, BF16)
    mcbq = [V(68608, 512, BF16), V(69632, 512, BF16)]
    cbs = [V(70656, 128, BF16), V(70912, 128, BF16)]
    eas = V(71168, 512, BF16)
    cdl_t = V(72192, 128, BF16, p=64)
    ysc = V(72448, 512, BF16)
    yrms = V(73472, 16, F32)
    awk = V(73536, 64, F32, p=64)
    htmp = V(73792, 512, F32)
    crmsb = V(75840, L, BF16)
    hi_t = V(79936, CK, BF16, p=64)
    lo_t = V(80192, CK, BF16, p=64)
    exp_t = V(80448, CK, BF16, p=64)
    dec_t = V(80704, CK, BF16, p=64)
    acs = V(80960, L, F32, p=64)
    h32 = V(89152, H * DH, F32)
    hbf = V(105536, H * DH, BF16)
    a_dt = V(89152, L, F32, p=64)
    wk1 = V(97344, 512, F32, p=64)
    wk2 = V(99392, 512, F32, p=64)
    wk3 = V(101440, 512, F32, p=64)
    xn = [V(113728 + k * 4096, D, BF16) for k in range(NK)]
    w1b = [V(179264, D, BF16), V(183360, D, BF16)]
    qf = V(187456, 512, F32)
    qt = V(189504, 512, F32)
    astage = [V(191552, L, BF16)]
    exp_tok = V(195648, 64, BF16)
    cdx = V(195776, 64, BF16)
    dec_tok = V(195904, 64, BF16)
    s_b = V(0, L, F32)
    ms_b = V(8192, L, F32)
    xstage = [V(16384, L, F32), V(24576, L, F32)]
    rstage = [V(32768, L, F32), V(40960, L, F32)]
    xbf_t = [V(49152, L, BF16), V(53248, L, BF16)]
    sq_t = [V(57344, L, BF16), V(61440, L, BF16)]
    rowm = V(65536, L, F32, p=1)
    rowe = V(73728, L, F32, p=1)
    rows_ = V(81920, L, F32, p=1)
    rowt = V(90112, L, F32, p=1)
    rmsf_b = V(0, L, F32)
    wcast = [V(8192, D, BF16), V(12288, D, BF16)]
    wf = [V(16384, D, BF16), V(20480, D, BF16)]
    cslab = V(24576, 2051, BF16)
    ct0 = V(28736, L, BF16)
    ct1 = V(32832, L, BF16)
    ct2 = V(36928, L, BF16)
    csq = V(41024, L, BF16)
    crawt = V(45120, L, BF16)
    gtmp = V(28736, L, F32)
    rowcv = V(49216, L, F32, p=1)
    rowcv2 = V(57408, L, F32, p=1)
    atile = [V(k * 2048, 1024, BF16) for k in range(64)]
    ygt = [V(k * 4096, L, BF16) for k in range(32)]
    w2slab = [V(131072, 64 * 128, BF16), V(147456, 64 * 128, BF16)]
    woslab = [V(131072, 32 * 128, BF16), V(139264, 32 * 128, BF16)]
    ochunk = V(163840, 2048, F32)
    rchunk = V(172032, 2048, F32)

    p = Sched()
    p.sem("tE", "vE", "sE", "gE")
    gS = DGroup(p, "dS")
    gG = DGroup(p, "dG")
    gR = DGroup(p, "dR")
    gD = DGroup(p, "dD")
    gSc = DGroup(p, "dC")
    gRc = DGroup(p, "dT2")

    pc_jobs = [("w2", m) for m in range(NK)] + [("wo", m) for m in range(NK)]
    pc_state = {"i": 0, "w2": [], "wo": []}

    def precast_piece(n=1):
        for _ in range(n):
            if pc_state["i"] >= len(pc_jobs):
                return
            kind, m = pc_jobs[pc_state["i"]]
            pc_state["i"] += 1
            if kind == "w2":
                src = w2t[:, m * 128:(m + 1) * 128].rearrange(
                    "(k p) c -> p k c", p=128)
                dst = w2bf_s[m].rearrange("p (k c) -> p k c", c=128)
            else:
                src = wot[:, m * 128:(m + 1) * 128].rearrange(
                    "(k p) c -> p k c", p=128)
                dst = wobf_s[m].rearrange("p (k c) -> p k c", c=128)
            pc_state[kind].append(gG.issue("gpsimd", dst, src))

    if n_phases >= 1:
        gS.issue("sync", normw_kp[:, :], normw.rearrange("(k p) -> p k", p=128),
                 allow_slow_non_contiguous=True)
        gS.issue("sync", convw_kp[:, :].rearrange("p (k j) -> p k j", j=4),
                 convw.rearrange("(k p) j -> p k j", p=128),
                 allow_slow_non_contiguous=True)
        gS.issue("sync", convb_kp[:, :], convb.rearrange("(k p) -> p k", p=128),
                 allow_slow_non_contiguous=True)
        gS.issue("sync", convnw_kp[:, :], convnw.rearrange("(k p) -> p k", p=128),
                 allow_slow_non_contiguous=True)
        gS.issue("sync", outnw_kp[:, :], outnw.rearrange("(k p) -> p k", p=128),
                 allow_slow_non_contiguous=True)
        gS.issue("sync", alog_sb[0:64, :], alog.rearrange("(h o) -> h o", o=1))
        gS.issue("sync", dtb_sb[0:64, :], dtb.rearrange("(h o) -> h o", o=1))
        gS.issue("sync", dtw_sb[0:64, :], dtwt)
        precast_piece(2)

        def g_consts_a(e):
            e.memset(ones_col[:, :], 1.0)
            e.memset(onesb[:, :], 1.0)
            e.memset(ones64[0:64, :], 1.0)
            e.memset(ones64f[0:64, :], 1.0)
            e.memset(ones_row[:, :], 1.0)
            e.memset(cst[:, 0:1], 1e-5)
            e.memset(cst[:, 1:2], 1e-6)
            e.memset(cst[:, 2:3], 1.0)
            e.memset(id64[0:64, :], 1.0)
            e.memset(idb[:, :], 1.0)
            e.memset(nid64[0:64, :], -1.0)
            e.memset(c1m[:, :], 1.0)
            return e.memset(c2m[:, :], -BIGM)
        p.o("gpsimd", g_consts_a, "gE")
        p.w("gpsimd", "gE")

        def g_consts_b(e):
            e.affine_select(out=id64[0:64, :], in_=id64[0:64, :], pattern=[[-1, 64]],
                            compare_op=OP.is_equal, fill=0.0, base=0, channel_multiplier=1)
            e.affine_select(out=idb[:, :], in_=idb[:, :], pattern=[[-1, 128]],
                            compare_op=OP.is_equal, fill=0.0, base=0, channel_multiplier=1)
            e.affine_select(out=nid64[0:64, :], in_=nid64[0:64, :], pattern=[[-1, 64]],
                            compare_op=OP.is_equal, fill=0.0, base=0, channel_multiplier=1)
            e.affine_select(out=c1m[:, :], in_=c1m[:, :], pattern=[[1, 128]],
                            compare_op=OP.is_ge, fill=0.0, base=-1, channel_multiplier=-1)
            return e.affine_select(out=c2m[:, :], in_=c2m[:, :], pattern=[[-1, 128]],
                                   compare_op=OP.is_ge, fill=0.0, base=0, channel_multiplier=1)
        p.o("gpsimd", g_consts_b, "gE")

        gS.waitall("scalar")
        p.o("scalar", lambda e: e.activation(negea[0:64, :], alog_sb[0:64, :], AF.Exp),
            "sE")
        gS.waitall("vector")
        p.w("vector", "gE")
        p.w("vector", "sE")

        def v_slabs(e):
            for k in range(NK):
                e.tensor_scalar(norm_slab[:, k * 128:(k + 1) * 128], onesb[:, :],
                                normw_kp[:, k:k + 1], None, OP.mult)
            for k in range(32):
                e.tensor_scalar(outnw_slab[:, k * 128:(k + 1) * 128], onesb[:, :],
                                outnw_kp[:, k:k + 1], None, OP.mult)
            return e.tensor_scalar(negea[0:64, :], negea[0:64, :], -1.0, None, OP.mult)
        p.o("vector", v_slabs, "vE")

    pb = [nc.alloc_psum_tensor(f"pb{i}", [128, 512], F32) for i in range(8)]

    if n_phases >= 2:
        p.w("tensor", "gE")
        sc_cnt = []
        for k in range(NK):
            precast_piece(1)
            if k % 2 == 0:
                if k >= 2:
                    p.w("sync", "sE", sc_cnt[k - 2])
                sv = gS.issue("sync", xstage[k % 2][:, :],
                              xT[k * 128:(k + 1) * 128, :])
            else:
                if k >= 2:
                    p.w("scalar", "sE", sc_cnt[k - 2])
                sv = gSc.issue("scalar", xstage[k % 2][:, :],
                               xT[k * 128:(k + 1) * 128, :])
            p.w("scalar", *sv)
            if k >= 2:
                p.w("scalar", "tE", p.cnt["tE"] - 1)

            def s_cast(e, k=k):
                e.activation(xbf_t[k % 2][:, :], xstage[k % 2][:, :], AF.Copy)
                return e.activation(sq_t[k % 2][:, :], xstage[k % 2][:, :], AF.Square)
            sc_cnt.append(p.o("scalar", s_cast, "sE"))
            p.w("tensor", "sE")

            def t_stats(e, k=k):
                for tb in range(4):
                    e.matmul(pb[tb][0:1, :], ones_col[:, :],
                             xbf_t[k % 2][:, tb * 512:(tb + 1) * 512],
                             start=(k == 0), stop=(k == NK - 1))
                last = None
                for tb in range(4):
                    last = e.matmul(pb[4 + tb][0:1, :], ones_col[:, :],
                                    sq_t[k % 2][:, tb * 512:(tb + 1) * 512],
                                    start=(k == 0), stop=(k == NK - 1))
                return last
            p.o("tensor", t_stats, "tE")

        def vch(fn):
            p.w("vector", "vE")
            p.o("vector", fn, "vE")

        def sch(fn):
            p.w("scalar", "vE")
            p.o("scalar", fn, "sE")
            p.w("vector", "sE")

        p.w("vector", "tE")

        def v_fin1(e):
            last = None
            for tb in range(4):
                e.tensor_scalar(rowm[0:1, tb * 512:(tb + 1) * 512], pb[tb][0:1, :],
                                1.0 / D, None, OP.mult)
                last = e.tensor_scalar(rowe[0:1, tb * 512:(tb + 1) * 512],
                                       pb[4 + tb][0:1, :], 1.0 / D, None, OP.mult)
            return last
        p.o("vector", v_fin1, "vE")
        vch(lambda e: e.tensor_tensor(rowt[0:1, :], rowm[0:1, :], rowm[0:1, :], OP.mult))
        vch(lambda e: e.tensor_tensor(rowe[0:1, :], rowe[0:1, :], rowt[0:1, :], OP.subtract))
        sch(lambda e: e.activation(rowt[0:1, :], rowe[0:1, :], AF.Sqrt, bias=cst[0:1, 0:1]))
        vch(lambda e: e.reciprocal(rows_[0:1, :], rowt[0:1, :]))
        vch(lambda e: e.tensor_tensor(rowt[0:1, :], rowe[0:1, :], rows_[0:1, :], OP.mult))
        vch(lambda e: e.tensor_tensor(rowt[0:1, :], rowt[0:1, :], rows_[0:1, :], OP.mult))
        sch(lambda e: e.activation(rowt[0:1, :], rowt[0:1, :], AF.Sqrt, bias=cst[0:1, 1:2]))
        vch(lambda e: e.reciprocal(rowe[0:1, :], rowt[0:1, :]))
        vch(lambda e: e.tensor_tensor(rowm[0:1, :], rowm[0:1, :], rows_[0:1, :], OP.mult))

        for row, dst in ((rows_, s_b), (rowm, ms_b)):
            p.w("tensor", "vE")

            def t_bc(e, row=row):
                last = None
                for tb in range(4):
                    last = e.matmul(pb[tb][:, :], ones_row[:, :],
                                    row[0:1, tb * 512:(tb + 1) * 512],
                                    start=True, stop=True)
                return last
            p.o("tensor", t_bc, "tE")
            p.w("vector", "tE")

            def v_bc(e, dst=dst):
                last = None
                for tb in range(4):
                    last = e.tensor_copy(out=dst[:, tb * 512:(tb + 1) * 512],
                                         in_=pb[tb][:, :])
                return last
            p.o("vector", v_bc, "vE")

        xcons = []
        rsems = []
        p.w("sync", "vE")
        p.w("sync", "sE")
        for k in range(NK):
            precast_piece(1)
            if k % 2 == 0:
                if k >= 2:
                    p.w("sync", "vE", xcons[k - 2])
                sv = gS.issue("sync", xstage[k % 2][:, :],
                              xT[k * 128:(k + 1) * 128, :])
            else:
                if k >= 2:
                    p.w("scalar", "vE", xcons[k - 2])
                sv = gSc.issue("scalar", xstage[k % 2][:, :],
                               xT[k * 128:(k + 1) * 128, :])
            p.w("vector", *sv)
            if k >= 2:
                for pr in rsems[k - 2]:
                    p.w("vector", *pr)

            def v_ln1(e, k=k):
                return e.tensor_tensor(rstage[k % 2][:, :], xstage[k % 2][:, :],
                                       s_b[:, :], OP.mult)
            p.o("vector", v_ln1, "vE")
            p.w("vector", "vE")

            def v_ln2a(e, k=k):
                return e.tensor_tensor(xn[k][:, :], rstage[k % 2][:, :], ms_b[:, :],
                                       OP.subtract)
            p.o("vector", v_ln2a, "vE")
            p.w("vector", "vE")

            def v_ln2b(e, k=k):
                return e.tensor_tensor(rstage[k % 2][:, :], rstage[k % 2][:, :],
                                       xstage[k % 2][:, :], OP.add)
            xcons.append(p.o("vector", v_ln2b, "vE"))
            p.w("vector", "vE")

            def v_ln3(e, k=k):
                return e.tensor_tensor(rstage[k % 2][:, :], rstage[k % 2][:, :],
                                       ms_b[:, :], OP.subtract)
            p.o("vector", v_ln3, "vE")
            prs = []
            if debug:
                p.w("sync", "vE")
                gD.issue("sync", dbg["dbg_xn"][k * 128:(k + 1) * 128, :], xn[k][:, :])
                prs.append(gR.issue("sync", dbg["dbg_r"][k * 128:(k + 1) * 128, :],
                                    rstage[k % 2][:, :]))
            if k % 2 == 0:
                p.w("sync", "vE")
                prs.append(gR.issue("sync", r_s[k], rstage[k % 2][:, :]))
            else:
                p.w("scalar", "vE")
                prs.append(gRc.issue("scalar", r_s[k], rstage[k % 2][:, :]))
            rsems.append(prs)
        gR.waitall("sync")
        gRc.waitall("scalar")
        gD.waitall("sync")

    if n_phases >= 3:
        NM = 81
        p.w("vector", "sE")
        p.w("vector", "tE")
        p.w("gpsimd", "vE")
        gR.waitall("vector")
        gD.waitall("vector")
        gW = DGroup(p, "dW")

        p.w("tensor", "vE")

        def t_bcf(e):
            last = None
            for tb in range(4):
                last = e.matmul(pb[tb][:, :], ones_row[:, :],
                                rowe[0:1, tb * 512:(tb + 1) * 512],
                                start=True, stop=True)
            return last
        p.o("tensor", t_bcf, "tE")
        p.w("vector", "tE")

        def v_bcf(e):
            last = None
            for tb in range(4):
                last = e.tensor_copy(out=rmsf_b[:, tb * 512:(tb + 1) * 512],
                                     in_=pb[tb][:, :])
            return last
        p.o("vector", v_bcf, "vE")
        p.w("vector", "vE")
        p.o("vector", lambda e: e.memset(cslab[:, 0:3], 0.0), "vE")

        wsem, vfold, tmm, vevac = [], [], [], []
        sevac = {}
        statc = {}
        statt = {}
        cstore, gstore = [], []
        nslab64 = norm_slab[:, :].rearrange("p (k c) -> p k c", c=128)[:, :, 0:64]

        def emit_cast_fold(mm):
            c0 = mm * 128
            w = 64 if mm == NM - 1 else 128
            if mm >= 2:
                p.w("gpsimd", "vE", vfold[mm - 2])
            src_ap = w_in[:, c0:c0 + w].rearrange("(k p) c -> p k c", p=128)
            dst_ap = wcast[mm % 2][:, 0:NK * w].rearrange("p (k c) -> p k c", c=w)
            wsem.append(gG.issue("gpsimd", dst_ap, src_ap))

            p.w("vector", *wsem[mm])
            if mm >= 2:
                p.w("vector", "tE", tmm[mm - 2])

            def v_fold(e, mm=mm, w=w):
                if w == 128:
                    return e.tensor_tensor(wf[mm % 2][:, :], wcast[mm % 2][:, :],
                                           norm_slab[:, :], OP.mult)
                return e.tensor_tensor(
                    wf[mm % 2][:, 0:NK * w].rearrange("p (k c) -> p k c", c=w),
                    wcast[mm % 2][:, 0:NK * w].rearrange("p (k c) -> p k c", c=w),
                    nslab64, OP.mult)
            vfold.append(p.o("vector", v_fold, "vE"))

        emit_cast_fold(0)
        for m in range(NM):
            w = 64 if m == NM - 1 else 128
            if m + 1 < NM:
                emit_cast_fold(m + 1)
            p.w("tensor", "vE", vfold[m])
            if m >= 1:
                p.w("tensor", "vE", vevac[m - 1])
                if m - 1 in sevac:
                    p.w("tensor", "sE", sevac[m - 1])

            def t_mm(e, m=m, w=w):
                last = None
                for k in range(NK):
                    for tb in range(4):
                        last = e.matmul(pb[tb][0:w, :],
                                        wf[m % 2][:, k * w:(k + 1) * w],
                                        xn[k][:, tb * 512:(tb + 1) * 512],
                                        start=(k == 0), stop=(k == NK - 1))
                return last
            tmm.append(p.o("tensor", t_mm, "tE"))
            if 1 <= m <= 48:
                ms = m - 1
                p.w("tensor", "sE", statc[ms])

                def t_stat(e, ms=ms):
                    last = None
                    for tb in range(4):
                        last = e.matmul(pb[4 + tb][0:1, :], ones_col[:, :],
                                        csq[:, tb * 512:(tb + 1) * 512],
                                        start=(ms == 0), stop=(ms == 47))
                    return last
                statt[ms] = p.o("tensor", t_stat, "tE")

            p.w("vector", "tE", tmm[m])
            if m < 48:
                p.w("scalar", "tE", tmm[m])
                p.w("scalar", "vE", p.cnt["vE"])

                def s_ev(e):
                    e.activation(cslab[:, 3:3 + 512], pb[0][:, :], AF.Copy)
                    return e.activation(cslab[:, 3 + 512:3 + 1024], pb[1][:, :],
                                        AF.Copy)
                sevac[m] = p.o("scalar", s_ev, "sE")

                def v_ev(e, m=m):
                    last = None
                    for tb in (2, 3):
                        last = e.tensor_tensor(
                            cslab[:, 3 + tb * 512:3 + (tb + 1) * 512],
                            pb[tb][:, :], rmsf_b[:, tb * 512:(tb + 1) * 512], OP.mult)
                    return last
                p.o("vector", v_ev, "vE")
                p.w("vector", "sE", sevac[m])
                vevac.append(p.o("vector", lambda e: e.tensor_tensor(
                    cslab[:, 3:3 + 1024], cslab[:, 3:3 + 1024],
                    rmsf_b[:, 0:1024], OP.mult), "vE"))

                p.w("vector", "vE")
                p.o("vector", lambda e, m=m: e.tensor_scalar(
                    ct0[:, :], cslab[:, 0:L], convw_kp[:, 4 * m:4 * m + 1],
                    None, OP.mult), "vE")
                p.w("vector", "vE")
                if m >= 1:
                    for pr in cstore[m - 1]:
                        p.w("vector", *pr)
                p.o("vector", lambda e, m=m: e.scalar_tensor_tensor(
                    ct1[:, :], cslab[:, 1:1 + L], convw_kp[:, 4 * m + 1:4 * m + 2],
                    ct0[:, :], OP.mult, OP.add), "vE")
                p.w("vector", "vE")
                p.o("vector", lambda e, m=m: e.scalar_tensor_tensor(
                    ct2[:, :], cslab[:, 2:2 + L], convw_kp[:, 4 * m + 2:4 * m + 3],
                    ct1[:, :], OP.mult, OP.add), "vE")
                p.w("vector", "vE")
                if m >= 1:
                    p.w("vector", "sE", statc[m - 1])
                p.o("vector", lambda e, m=m: e.scalar_tensor_tensor(
                    ct0[:, :], cslab[:, 3:3 + L], convw_kp[:, 4 * m + 3:4 * m + 4],
                    ct2[:, :], OP.mult, OP.add), "vE")
                p.w("vector", "vE")
                p.o("vector", lambda e, m=m: e.tensor_scalar(
                    crawt[:, :], ct0[:, :], convb_kp[:, m:m + 1], None, OP.add), "vE")

                p.w("scalar", "vE")
                if m >= 1:
                    p.w("scalar", "tE", statt[m - 1])
                statc[m] = p.o("scalar", lambda e: e.activation(
                    csq[:, :], crawt[:, :], AF.Square), "sE")
                p.w("vector", "vE")
                p.o("vector", lambda e, m=m: e.tensor_scalar(
                    ct1[:, :], crawt[:, :], convnw_kp[:, m:m + 1], None, OP.mult), "vE")
                p.w("sync", "vE")
                prs = [gW.issue("sync", craw_s[m], ct1[:, :])]
                if debug:
                    prs.append(gD.issue("sync", dbg["dbg_craw"][m], ct1[:, :]))
                cstore.append(prs)
            elif m < 80:
                mg = m - 48
                p.w("vector", "vE", p.cnt["vE"])
                if mg >= 1:
                    p.w("vector", "sE", gsil[mg - 1])
                else:
                    for pr in cstore[-1]:
                        p.w("vector", *pr)

                p.w("scalar", "tE", tmm[m])
                p.w("scalar", "vE", p.cnt["vE"])
                if mg >= 1:
                    p.w("scalar", "sE", gsil[mg - 1])

                def s_evg(e):
                    e.activation(gtmp[:, 0:512], pb[0][:, :], AF.Copy)
                    return e.activation(gtmp[:, 512:1024], pb[1][:, :], AF.Copy)
                sevac[m] = p.o("scalar", s_evg, "sE")

                def v_evg(e):
                    last = None
                    for tb in (2, 3):
                        last = e.tensor_tensor(
                            gtmp[:, tb * 512:(tb + 1) * 512],
                            pb[tb][:, :], rmsf_b[:, tb * 512:(tb + 1) * 512], OP.mult)
                    return last
                p.o("vector", v_evg, "vE")
                p.w("vector", "sE", sevac[m])
                vevac.append(p.o("vector", lambda e: e.tensor_tensor(
                    gtmp[:, 0:1024], gtmp[:, 0:1024],
                    rmsf_b[:, 0:1024], OP.mult), "vE"))
                p.w("scalar", "vE")
                p.w("scalar", "sE", p.cnt["sE"])
                if mg >= 1:
                    for pr in gstore[mg - 1]:
                        p.w("scalar", *pr)
                elif cstore:
                    for pr in cstore[-1]:
                        p.w("scalar", *pr)
                if mg == 0:
                    gsil = []
                gsil.append(p.o("scalar",
                                lambda e: e.activation(crawt[:, :], gtmp[:, :], AF.Silu),
                                "sE"))
                p.w("sync", "sE")
                prs = [gW.issue("sync", gate_s[mg], crawt[:, :])]
                if debug:
                    prs.append(gD.issue("sync", dbg["dbg_gate"][mg], crawt[:, :]))
                gstore.append(prs)
            else:
                def v_eva(e):
                    last = None
                    for tb in range(4):
                        last = e.tensor_tensor(
                            a_dt[0:64, tb * 512:(tb + 1) * 512],
                            pb[tb][0:64, :], rmsf_b[0:64, tb * 512:(tb + 1) * 512],
                            OP.mult)
                    return last
                vevac.append(p.o("vector", v_eva, "vE"))
                if debug:
                    p.w("sync", "vE")
                    gD.issue("sync", dbg["dbg_adt"], a_dt[0:64, :])

        p.w("vector", "tE")

        def v_cst1(e):
            last = None
            for tb in range(4):
                last = e.tensor_scalar(rowcv[0:1, tb * 512:(tb + 1) * 512],
                                       pb[4 + tb][0:1, :], 1.0 / CONV, None, OP.mult)
            return last
        p.o("vector", v_cst1, "vE")
        p.w("scalar", "vE")
        p.o("scalar", lambda e: e.activation(rowcv[0:1, :], rowcv[0:1, :], AF.Sqrt,
                                             bias=cst[0:1, 1:2]), "sE")
        p.w("vector", "sE")
        p.o("vector", lambda e: e.reciprocal(rowcv2[0:1, :], rowcv[0:1, :]), "vE")
        p.w("tensor", "vE")

        def t_bcc(e):
            last = None
            for tb in range(4):
                last = e.matmul(pb[4 + tb][:, :], ones_row[:, :],
                                rowcv2[0:1, tb * 512:(tb + 1) * 512],
                                start=True, stop=True)
            return last
        p.o("tensor", t_bcc, "tE")
        p.w("vector", "tE")

        def v_bcc(e):
            last = None
            for tb in range(4):
                last = e.tensor_copy(out=crmsb[:, tb * 512:(tb + 1) * 512],
                                     in_=pb[4 + tb][:, :])
            return last
        p.o("vector", v_bcc, "vE")
        if debug:
            p.w("sync", "vE")
            gD.issue("sync", dbg["dbg_crms"], crmsb[:, :])
        gW.waitall("sync")
        gD.waitall("sync")

    if n_phases >= 4:
        p.w("tensor", "vE")
        p.w("vector", "tE")
        p.w("vector", "sE")
        p.w("scalar", "vE")
        p.w("scalar", "tE")
        gW.waitall("vector")
        gW.waitall("scalar")
        gG.waitall("gpsimd", issuer=True)
        p.w("gpsimd", "vE")
        p.w("gpsimd", "tE")
        gR.waitall("vector")
        gR.waitall("scalar")
        gR.waitall("tensor")
        gD.waitall("vector")
        gD.waitall("scalar")
        p.w("sync", "vE")
        p.w("sync", "tE")
        p.w("sync", "sE")
        gT = DGroup(p, "dX")

        st3 = {"w1sem": [], "tmm": [], "vq": [], "astore": [], "agel": [],
               "qfv": []}

        def ph3_unit(m):
            if m >= 2:
                p.w("gpsimd", "tE", st3["tmm"][m - 2])
            src_ap = w1t[:, m * 128:(m + 1) * 128].rearrange("(k p) c -> p k c", p=128)
            dst_ap = w1b[m % 2][:, :].rearrange("p (k c) -> p k c", c=128)
            st3["w1sem"].append(gG.issue("gpsimd", dst_ap, src_ap))
            p.w("tensor", *st3["w1sem"][m])
            ag = None
            for tb in range(4):
                idx = m * 4 + tb
                if idx >= 2:
                    p.w("tensor", "vE", st3["qfv"][idx - 2])

                def t_mm(e, m=m, tb=tb, idx=idx):
                    last = None
                    for k in range(NK):
                        last = e.matmul(pb[idx % 2][:, :],
                                        w1b[m % 2][:, k * 128:(k + 1) * 128],
                                        xn[k][:, tb * 512:(tb + 1) * 512],
                                        start=(k == 0), stop=(k == NK - 1))
                    return last
                tmb = p.o("tensor", t_mm, "tE")
                p.w("vector", "tE", tmb)
                p.w("vector", "vE")
                if ag is not None:
                    p.w("vector", "sE", ag)
                elif st3["agel"]:
                    p.w("vector", "sE", st3["agel"][-1])
                st3["qfv"].append(p.o("vector", lambda e, idx=idx: e.tensor_scalar(
                    qf[:, :], pb[idx % 2][:, :], 10.0, -32768.49,
                    OP.mult, OP.max), "vE"))
                p.w("vector", "vE")
                p.o("vector", lambda e: e.tensor_scalar(
                    qt[:, :], qf[:, :], 32767.49, MAGIC, OP.min, OP.add), "vE")
                p.w("vector", "vE")
                vq = p.o("vector", lambda e: e.tensor_scalar(
                    qf[:, :], qt[:, :], -MAGIC, None, OP.add), "vE")
                p.w("scalar", "vE", vq)
                if tb == 0 and m >= 1:
                    for pr in st3["astore"][m - 1]:
                        p.w("scalar", *pr)
                ag = p.o("scalar", lambda e, tb=tb: e.activation(
                    astage[0][:, tb * 512:(tb + 1) * 512], qf[:, :],
                    AF.Gelu_apprx_tanh, scale=0.1), "sE")
            st3["tmm"].append(tmb)
            st3["vq"].append(vq)
            st3["agel"].append(ag)
            p.w("sync", "sE", ag)
            st3["astore"].append([gW.issue("sync", a_s[m], astage[0][:, :])])

        if n_phases >= 5:
            st4 = {"sccons": {}, "scl": {}, "silc": {},
                   "hbfc": [0] * 8, "xdec": [0] * 8, "states": [0] * 8,
                   "yadd": [0] * 8, "ysc": [0] * 8, "htmp": [0] * 8,
                   "hadd": [0] * 8, "cbev": 0, "mcb01": 0, "mcb01_2": 0,
                   "ydq": 0, "lex01": 0, "ygstore": [],
                   "prev_vyg": 0, "tlast": 0, "vacs": 0, "easv": 0}
            ytok3 = ytok[:, :].rearrange("p (m t) -> p m t", t=128)

            def ph4_prelude():
                relc = {}
                for cb in range(4):
                    bank = pb[6 + (cb % 2)]
                    if cb >= 2:
                        p.w("tensor", "sE", relc[cb - 2])

                    def t_dt(e, cb=cb, bank=bank):
                        return e.matmul(bank[0:64, :], dtw_sb[0:64, :],
                                        a_dt[0:64, cb * 512:(cb + 1) * 512],
                                        start=True, stop=True)
                    tdt = p.o("tensor", t_dt, "tE")
                    p.w("scalar", "tE", tdt)
                    p.w("scalar", "vE", p.cnt["vE"])
                    p.o("scalar", lambda e, bank=bank: e.activation(
                        wk1[0:64, :], bank[0:64, :], AF.Abs, bias=dtb_sb[0:64, :]),
                        "sE")
                    p.w("scalar", "sE")
                    p.o("scalar", lambda e: e.activation(
                        wk2[0:64, :], wk1[0:64, :], AF.Exp, scale=-1.0), "sE")
                    p.w("scalar", "sE")
                    p.o("scalar", lambda e: e.activation(
                        wk3[0:64, :], wk2[0:64, :], AF.Ln, bias=cst[0:64, 2:3]), "sE")
                    p.w("scalar", "sE")
                    relc[cb] = p.o("scalar", lambda e, bank=bank: e.activation(
                        wk1[0:64, :], bank[0:64, :], AF.Relu, bias=dtb_sb[0:64, :]),
                        "sE")
                    p.w("vector", "sE", relc[cb])
                    p.o("vector", lambda e: e.tensor_tensor(
                        wk2[0:64, :], wk3[0:64, :], wk1[0:64, :], OP.add), "vE")
                    p.w("vector", "vE")
                    p.o("vector", lambda e, cb=cb: e.tensor_scalar(
                        a_dt[0:64, cb * 512:(cb + 1) * 512], wk2[0:64, :],
                        negea[0:64, 0:1], None, OP.mult), "vE")
                p.w("vector", "vE")
                for c in range(NCH):
                    p.o("vector", lambda e, c=c: e.tensor_tensor_scan(
                        acs[0:64, c * CK:(c + 1) * CK], ones64f[0:64, :],
                        a_dt[0:64, c * CK:(c + 1) * CK], 0.0, OP.mult, OP.add), "vE")
                st4["vacs"] = p.cnt["vE"]
                p.w("vector", "vE")
                p.w("vector", "sE")
                p.o("vector", lambda e: e.memset(h32[:, :], 0.0), "vE")
                p.w("vector", "vE")
                p.o("vector", lambda e: e.memset(hbf[:, :], 0.0), "vE")
                st4["hbfc"] = [p.cnt["vE"]] * 8
                if debug:
                    p.w("sync", "vE")
                    gD.issue("sync", dbg["dbg_acs"], acs[0:64, :])

            def ph4_chunk(c):
                sc, off = c // 2, (c % 2) * CK
                yield
                if c % 2 == 0:
                    if sc >= 1:
                        for s_, v_ in st4["sccons"][sc - 1]:
                            p.w("sync", s_, v_)
                    NSUB = 4
                    TW = 48 // NSUB
                    subs = []
                    for i in range(NSUB):
                        subs.append(gS.issue(
                            "sync",
                            cact[0][:, i * TW * SC:(i + 1) * TW * SC].rearrange(
                                "p (m t) -> p m t", t=SC),
                            craw_s[i * TW:(i + 1) * TW, :,
                                   sc * SC:(sc + 1) * SC].transpose([1, 0, 2])))
                    st4["scl"][sc] = [None,
                                      gS.issue("sync",
                                               gact[0][:, :].rearrange(
                                                   "p (m t) -> p m t", t=SC),
                                               gate_s[:, :, sc * SC:(sc + 1) * SC]
                                               .transpose([1, 0, 2]))]
                    sils = []
                    for i in range(NSUB):
                        p.w("vector", *subs[i])
                        p.w("vector", "vE")
                        vcv = p.o("vector", lambda e, sc=sc, i=i, TW=TW:
                                  e.tensor_tensor(
                            cact[0][:, i * TW * SC:(i + 1) * TW * SC].rearrange(
                                "p (m t) -> p m t", t=SC),
                            cact[0][:, i * TW * SC:(i + 1) * TW * SC].rearrange(
                                "p (m t) -> p m t", t=SC),
                            crmsb[:, sc * SC:(sc + 1) * SC].unsqueeze(1)
                            .broadcast_to([128, TW, SC]), OP.mult), "vE")
                        p.w("scalar", "vE", vcv)
                        sils.append(p.o("scalar", lambda e, i=i, TW=TW: e.activation(
                            cact[0][:, i * TW * SC:(i + 1) * TW * SC],
                            cact[0][:, i * TW * SC:(i + 1) * TW * SC], AF.Silu),
                            "sE"))
                    st4["silc"][sc] = sils

                def csl(m, width=CK):
                    return cact[0][:, m * SC + off:m * SC + off + width]

                acs_c = acs[0:64, c * CK:(c + 1) * CK]
                p.w("scalar", "tE", st4["tlast"])
                p.w("scalar", "sE")
                p.o("scalar", lambda e, acs_c=acs_c: e.activation(
                    exp_t[0:64, :], acs_c, AF.Exp), "sE")
                p.o("scalar", lambda e, acs_c=acs_c, c=c: e.activation(
                    dec_t[0:64, :], acs_c, AF.Exp, scale=-1.0,
                    bias=acs[0:64, c * CK + CK - 1:c * CK + CK]), "sE")
                sa = p.o("scalar", lambda e, c=c: e.activation(
                    awk[0:64, 0:1], acs[0:64, c * CK + CK - 1:c * CK + CK], AF.Exp),
                    "sE")
                p.w("vector", "tE", st4["tlast"])
                p.w("vector", "vE")
                p.o("vector", lambda e, acs_c=acs_c: e.tensor_copy(
                    out=hi_t[0:64, :], in_=acs_c), "vE")
                p.w("vector", "vE")
                p.o("vector", lambda e, acs_c=acs_c: e.tensor_tensor(
                    lo_t[0:64, :], acs_c, hi_t[0:64, :], OP.subtract), "vE")
                p.w("vector", "sE", sa)
                vcdl = p.o("vector", lambda e: e.tensor_scalar(
                    cdl_t[0:64, :], ones64[0:64, :], awk[0:64, 0:1], None, OP.mult),
                    "vE")
                satl = p.cnt["sE"]
                vtl = p.cnt["vE"]

                silv = st4["silc"][sc]
                bmap = {0: 0, 1: 1, 2: 2, 3: 2, 4: 3}
                if c >= 1:
                    p.w("tensor", "vE", st4["prev_vyg"])
                    p.w("tensor", "sE", st4["lex01"])
                    for pr in st4["ygstore"][c - 1]:
                        p.w("vector", *pr)
                srcs = ([csl(m) for m in range(32)] + [csl(32 + g) for g in range(8)]
                        + [exp_t[0:64, :], cdl_t[0:64, :], dec_t[0:64, :]])
                dsts = ([xtok[:, m * 128:(m + 1) * 128] for m in range(32)] +
                        [btok[:, g * 128:(g + 1) * 128] for g in range(8)] +
                        [exp_tok[:, :], cdx[:, :], dec_tok[:, :]])
                tevb = {}
                for bi, b0 in enumerate(range(0, 43, 8)):
                    nb = min(8, 43 - b0)
                    bank = 4 + (bi % 4)
                    pTb = pb[bank][:, :].bitcast(BF16)
                    if bi < 5:
                        p.w("tensor", "sE", silv[bmap[bi]])
                    else:
                        p.w("tensor", "sE", satl)
                        p.w("tensor", "vE", vtl)
                    if bi >= 4:
                        p.w("tensor", *tevb[bi - 4])

                    def t_trb(e, b0=b0, nb=nb, srcs=srcs, pT=pTb):
                        last = None
                        for j in range(nb):
                            s_ = srcs[b0 + j]
                            if s_.shape[0] == 64:
                                last = e.transpose(pT[:, j * 128:j * 128 + 64],
                                                   s_, id64[0:64, :])
                            else:
                                last = e.transpose(pT[:, j * 128:(j + 1) * 128],
                                                   s_, idb[:, :])
                        return last
                    ttb = p.o("tensor", t_trb, "tE")
                    ee = "scalar" if bi % 2 == 0 else "vector"
                    p.w(ee, "tE", ttb)
                    if bi < 2 and c >= 1:
                        p.w(ee, "vE", st4["prev_vyg"])

                    def x_trev(e, b0=b0, nb=nb, dsts=dsts, pT=pTb, ee=ee):
                        last = None
                        for j in range(nb):
                            d_ = dsts[b0 + j]
                            w_ = d_.shape[-1]
                            if ee == "scalar":
                                last = e.activation(d_, pT[:, j * 128:j * 128 + w_],
                                                    AF.Copy)
                            else:
                                last = e.tensor_copy(
                                    out=d_, in_=pT[:, j * 128:j * 128 + w_])
                        return last
                    tevb[bi] = ("sE" if ee == "scalar" else "vE",
                                p.o(ee, x_trev, "sE" if ee == "scalar" else "vE"))
                sevf = max(v for k, v in tevb.values() if k == "sE")
                vevf = max(v for k, v in tevb.values() if k == "vE")
                p.w("tensor", "sE", sevf)
                p.w("tensor", "vE", vevf)
                p.w("vector", "sE", sevf)
                p.w("vector", "vE", vevf)
                p.w("scalar", "vE", vevf)
                yield

                for i in range(9):
                    mcb_prev = st4["mcb01"]
                    if i < 8:
                        g = i
                        gsl = slice(g * 512, (g + 1) * 512)
                        bAP, cAP = csl(32 + g), csl(40 + g)
                        p.w("tensor", "sE", st4["cbev"])

                        def t_cb(e, bAP=bAP, cAP=cAP):
                            return e.matmul(pb[3][:, 0:128], bAP, cAP,
                                            start=True, stop=True)
                        tcb = p.o("tensor", t_cb, "tE")
                        p.w("vector", "vE")
                        decr = dec_tok[:, 8 * g:8 * g + 8].unsqueeze(2) \
                            .broadcast_to([128, 8, 64])
                        st4["xdec"][g] = p.o("vector", lambda e, gsl=gsl, decr=decr:
                                             e.tensor_tensor(
                            xdec[:, gsl].rearrange("p (j d) -> p j d", d=64),
                            xtok[:, gsl].rearrange("p (j d) -> p j d", d=64),
                            decr, OP.mult), "vE")
                        hi = hi_t[0:64, :]
                        lo = lo_t[0:64, :]

                        def t_lm(e, g=g, q=0, hi=hi, lo=lo, bank=4):
                            for j in range(4):
                                h = 8 * g + 4 * q + j
                                selh = id64[0:64, h:h + 1].broadcast_to([64, 128])
                                nselh = nid64[0:64, h:h + 1].broadcast_to([64, 128])
                                sl = pb[bank][:, j * 128:(j + 1) * 128]
                                e.matmul(sl, selh, hi, start=(j == 0), stop=False)
                                e.matmul(sl, selh, lo, start=False, stop=False)
                                e.matmul(sl, hi, nselh, start=False, stop=False)
                                e.matmul(sl, lo, nselh, start=False, stop=False)
                            c2r = c2m[:, :].unsqueeze(1).broadcast_to([128, 4, 128])
                            return e.matmul(
                                pb[bank][:, 0:512].rearrange("p (j t) -> p j t",
                                                             t=128),
                                c1m[:, :], c2r, start=False, stop=True)

                        p.w("tensor", "sE", st4["lex01"])
                        tl0 = p.o("tensor", lambda e, f=t_lm: f(e, q=0, bank=4),
                                  "tE")
                        tl1 = p.o("tensor", lambda e, f=t_lm: f(e, q=1, bank=7),
                                  "tE")
                        p.w("scalar", "tE", tcb)
                        p.w("scalar", "vE", st4["mcb01_2"])
                        st4["cbev"] = p.o("scalar", lambda e, g=g: e.activation(
                            cbs[g % 2][:, :], pb[3][:, 0:128], AF.Copy), "sE")
                        p.w("scalar", "tE", tl1)
                        p.w("scalar", "vE", mcb_prev)

                        def s_lex(e):
                            e.activation(lexp[:, :], pb[4][:, 0:512], AF.Exp)
                            return e.activation(eas[:, :], pb[7][:, 0:512], AF.Exp)
                        st4["lex01"] = p.o("scalar", s_lex, "sE")
                    if i >= 1:
                        g2 = i - 1
                        gsl2 = slice(g2 * 512, (g2 + 1) * 512)
                        cAP2 = csl(40 + g2)
                        p.w("tensor", "vE", mcb_prev)
                        p.w("tensor", "vE", st4["yadd"][g2 - 1])

                        def t_yd(e, g2=g2):
                            last = None
                            for h8 in range(8):
                                last = e.matmul(
                                    pb[5][:, h8 * 64:(h8 + 1) * 64],
                                    mcbq[h8 // 4][:, (h8 % 4) * 128:
                                                  (h8 % 4 + 1) * 128],
                                    xtok[:, g2 * 512 + h8 * 64:
                                         g2 * 512 + (h8 + 1) * 64],
                                    start=True, stop=True)
                            return last
                        st4["ydq"] = p.o("tensor", t_yd, "tE")
                        p.w("tensor", "vE", st4["xdec"][g2])
                        p.w("tensor", "vE", st4["hadd"][g2 - 1])

                        def t_st(e, g2=g2, gsl2=gsl2):
                            return e.matmul(pb[2][:, :],
                                            btok[:, g2 * 128:(g2 + 1) * 128],
                                            xdec[:, gsl2], start=True, stop=True)
                        st4["states"][g2] = p.o("tensor", t_st, "tE")
                        p.w("tensor", "vE", st4["ysc"][g2 - 1])
                        p.w("tensor", "vE", st4["hbfc"][g2])

                        def t_yc(e, cAP2=cAP2, gsl2=gsl2):
                            return e.matmul(pb[6][:, :], cAP2, hbf[:, gsl2],
                                            start=True, stop=True)
                        tyc = p.o("tensor", t_yc, "tE")
                    if i < 8:
                        p.w("vector", "sE", st4["lex01"])
                        p.w("vector", "tE", st4["ydq"])
                        cbr = cbs[g % 2][:, :].unsqueeze(1) \
                            .broadcast_to([128, 4, 128])

                        def v_mcb(e, cbr=cbr):
                            e.tensor_tensor(
                                mcbq[0][:, :].rearrange("p (j t) -> p j t", t=128),
                                lexp[:, :].rearrange("p (j t) -> p j t", t=128),
                                cbr, OP.mult)
                            return e.tensor_tensor(
                                mcbq[1][:, :].rearrange("p (j t) -> p j t", t=128),
                                eas[:, :].rearrange("p (j t) -> p j t", t=128),
                                cbr, OP.mult)
                        st4["mcb01_2"] = mcb_prev
                        st4["mcb01"] = p.o("vector", v_mcb, "vE")
                    if i >= 1:
                        p.w("vector", "vE")
                        cdr = cdx[:, 8 * g2:8 * g2 + 8].unsqueeze(2).broadcast_to(
                            [128, 8, 64])
                        st4["htmp"][g2] = p.o("vector", lambda e, gsl2=gsl2,
                                              cdr=cdr: e.tensor_tensor(
                            htmp[:, :].rearrange("p (j d) -> p j d", d=64),
                            h32[:, gsl2].rearrange("p (j d) -> p j d", d=64),
                            cdr, OP.mult), "vE")
                        p.w("vector", "tE", tyc)
                        p.w("vector", "vE")
                        exr = exp_tok[:, 8 * g2:8 * g2 + 8].unsqueeze(2) \
                            .broadcast_to([128, 8, 64])
                        st4["ysc"][g2] = p.o("vector", lambda e, exr=exr:
                                             e.tensor_tensor(
                            ysc[:, :].rearrange("p (j d) -> p j d", d=64),
                            pb[6][:, :].rearrange("p (j d) -> p j d", d=64),
                            exr, OP.mult), "vE")
                        p.w("vector", "tE", st4["ydq"])
                        p.w("vector", "vE")
                        st4["yadd"][g2] = p.o("vector", lambda e, gsl2=gsl2:
                                              e.tensor_tensor(
                            ytok[:, gsl2], pb[5][:, :], ysc[:, :], OP.add), "vE")
                        p.w("vector", "tE", st4["states"][g2])
                        p.w("vector", "vE")
                        st4["hadd"][g2] = p.o("vector", lambda e, gsl2=gsl2:
                                              e.tensor_tensor(
                            h32[:, gsl2], htmp[:, :], pb[2][:, :], OP.add), "vE")
                        p.w("vector", "vE")
                        st4["hbfc"][g2] = p.o("vector", lambda e, gsl2=gsl2:
                                              e.tensor_copy(
                            out=hbf[:, gsl2], in_=h32[:, gsl2]), "vE")
                    yield

                ydump = []
                if debug:
                    p.w("sync", "vE", st4["yadd"][7])
                    ydump.append(gD.issue("sync", dbg["dbg_y"][c], ytok[:, :]))
                p.w("scalar", "vE", st4["yadd"][7])
                p.w("scalar", "tE", st4["states"][7])
                ssq = p.o("scalar", lambda e: e.activation(xdec[:, :], ytok[:, :],
                                                           AF.Square), "sE")
                p.w("vector", "sE", ssq)
                p.o("vector", lambda e: e.reduce_sum(yrms[:, 0:1], xdec[:, :],
                                                     axis=AX.X), "vE")
                p.w("vector", "vE")
                p.o("vector", lambda e: e.tensor_scalar(
                    yrms[:, 1:2], yrms[:, 0:1], 1.0 / DIN, 1e-6, OP.mult, OP.add),
                    "vE")
                p.w("scalar", "vE", p.cnt["vE"])
                p.o("scalar", lambda e: e.activation(yrms[:, 2:3], yrms[:, 1:2],
                                                     AF.Ln), "sE")
                p.w("scalar", "sE")
                p.o("scalar", lambda e: e.activation(yrms[:, 3:4], yrms[:, 2:3],
                                                     AF.Exp, scale=-0.5), "sE")
                p.w("scalar", "sE")
                syn = p.o("scalar", lambda e: e.activation(
                    xdec[:, :], ytok[:, :], AF.Copy, scale=yrms[:, 3:4]), "sE")
                tevb2 = {}
                for bi, b0 in enumerate(range(0, 32, 8)):
                    bank = 4 + bi
                    pTb = pb[bank][:, :].bitcast(BF16)
                    if bi == 0:
                        p.w("tensor", "sE", syn)
                        p.w("tensor", "vE", st4["hadd"][7])

                    def t_trb(e, b0=b0, pT=pTb):
                        last = None
                        for j in range(8):
                            last = e.transpose(pT[:, j * 128:(j + 1) * 128],
                                               xdec[:, (b0 + j) * 128:(b0 + j + 1) * 128],
                                               idb[:, :])
                        return last
                    ttb = p.o("tensor", t_trb, "tE")
                    ee = "scalar" if bi % 2 == 0 else "vector"
                    p.w(ee, "tE", ttb)
                    if bi < 2:
                        p.w(ee, "sE" if ee == "vector" else "vE",
                            syn if ee == "vector" else st4["hadd"][7])

                    def x_trev(e, b0=b0, pT=pTb, ee=ee):
                        last = None
                        for j in range(8):
                            d_ = xtok[:, (b0 + j) * 128:(b0 + j + 1) * 128]
                            if ee == "scalar":
                                last = e.activation(d_, pT[:, j * 128:(j + 1) * 128],
                                                    AF.Copy)
                            else:
                                last = e.tensor_copy(
                                    out=d_, in_=pT[:, j * 128:(j + 1) * 128])
                        return last
                    tevb2[bi] = ("sE" if ee == "scalar" else "vE",
                                 p.o(ee, x_trev, "sE" if ee == "scalar" else "vE"))
                p.w("vector", "sE", max(v for k, v in tevb2.values() if k == "sE"))
                p.w("vector", *st4["scl"][sc][1])
                for pr in ydump:
                    p.w("vector", *pr)
                p.w("vector", "sE", syn)
                p.w("vector", "vE")
                vyg = p.o("vector", lambda e, off=off: e.tensor_tensor(
                    ytok3, xtok[:, :].rearrange("p (m t) -> p m t", t=128),
                    gact[0][:, :].rearrange("p (m t) -> p m t", t=SC)[:, :,
                                                                     off:off + 128],
                    OP.mult), "vE")
                st4["prev_vyg"] = vyg
                p.w("sync", "vE", vyg)
                prs = [gW.issue("sync", yg_s[c].transpose([1, 0, 2]),
                                ytok[:, :].rearrange("p (m t) -> p m t", t=128))]
                if debug:
                    prs.append(gD.issue("sync", dbg["dbg_yg"][c], ytok[:, :]))
                st4["ygstore"].append(prs)
                st4["tlast"] = p.cnt["tE"]
                if c % 2 == 1:
                    st4["sccons"][sc] = [("tE", p.cnt["tE"]), ("vE", p.cnt["vE"]),
                                         ("sE", p.cnt["sE"])]

            ph4_prelude()
            gens = [ph4_chunk(c) for c in range(NCH)]
            todo = [s for g_ in gens for s in [g_]]

        NU = DFF // 128
        gi = 0
        for m in range(NU):
            ph3_unit(m)
            if n_phases >= 5:
                for _ in range(3):
                    if gi < len(todo):
                        try:
                            next(todo[gi])
                        except StopIteration:
                            gi += 1
        if n_phases >= 5:
            while gi < len(todo):
                try:
                    next(todo[gi])
                except StopIteration:
                    gi += 1
        gW.waitall("sync")
        gD.waitall("sync")

    if n_phases >= 6:
        p.barrier()
        for pr in pc_state["w2"]:
            p.w("sync", *pr)
            p.w("scalar", *pr)
        ablock = arena[:, 0:131072].bitcast(BF16).rearrange("p (k t) -> p k t", t=1024)
        o1sem = []
        for tb2 in range(2):
            if tb2 == 1:
                p.w("sync", "vE", p.cnt["vE"])
                p.w("sync", "tE", p.cnt["tE"])
                p.w("scalar", "vE", p.cnt["vE"])
                p.w("scalar", "tE", p.cnt["tE"])
            asem = [gS.issue("sync", ablock[:, 0:32, :],
                             a_s[0:32, :, tb2 * 1024:(tb2 + 1) * 1024]
                             .transpose([1, 0, 2])),
                    gSc.issue("scalar", ablock[:, 32:64, :],
                              a_s[32:64, :, tb2 * 1024:(tb2 + 1) * 1024]
                              .transpose([1, 0, 2]))]
            w2sem, t5, v5 = [], [], []
            for m in range(NK):
                eng5 = "sync" if m % 2 == 0 else "scalar"
                if m >= 2:
                    p.w(eng5, "tE", t5[m - 2])
                w2sem.append((gS if m % 2 == 0 else gSc).issue(
                    eng5, w2slab[m % 2][:, :], w2bf_s[m][:, :]))
                if m >= 1:
                    p.w("sync", "vE", v5[m - 1])
                rse = gS.issue("sync", rchunk[:, 0:1024],
                               r_s[m][:, tb2 * 1024:(tb2 + 1) * 1024])
                p.w("tensor", *w2sem[m])
                for pr in asem:
                    p.w("tensor", *pr)
                if m >= 1:
                    p.w("tensor", "vE", v5[m - 1])

                def t_mm(e, m=m):
                    last = None
                    for k in range(64):
                        for tb in range(2):
                            last = e.matmul(
                                pb[2 * (m % 2) + tb][:, :],
                                w2slab[m % 2][:, k * 128:(k + 1) * 128],
                                atile[k][:, tb * 512:(tb + 1) * 512],
                                start=(k == 0), stop=(k == 63))
                    return last
                t5.append(p.o("tensor", t_mm, "tE"))
                p.w("vector", "tE", t5[m])
                p.w("vector", *rse)
                if o1sem:
                    p.w("vector", *o1sem[-1])
                p.w("vector", "vE")

                def v_ev(e, m=m):
                    last = None
                    for tb in range(2):
                        last = e.tensor_tensor(
                            ochunk[:, tb * 512:(tb + 1) * 512],
                            pb[2 * (m % 2) + tb][:, :],
                            rchunk[:, tb * 512:(tb + 1) * 512], OP.add)
                    return last
                v5.append(p.o("vector", v_ev, "vE"))
                p.w("sync", "vE", v5[m])
                o1sem.append(gW.issue("sync",
                                      out1_s[m][:, tb2 * 1024:(tb2 + 1) * 1024],
                                      ochunk[:, 0:1024]))
        gW.waitall("sync")

    if n_phases >= 7:
        p.barrier()
        for pr in pc_state["wo"]:
            p.w("sync", *pr)
            p.w("scalar", *pr)
        for kf in range(32):
            (gS if kf % 2 == 0 else gSc).issue(
                "sync" if kf % 2 == 0 else "scalar",
                ygt[kf][:, :].rearrange("p (c t) -> p c t", t=128),
                yg_s[:, kf].transpose([1, 0, 2]))
        wosem, t6, v6, f6, ostores = [], [], [], [], []
        for m in range(NK):
            eng6 = "sync" if m % 2 == 0 else "scalar"
            if m >= 2:
                p.w(eng6, "vE", f6[m - 2])
                p.w(eng6, "tE", t6[m - 2])
            wosem.append((gS if m % 2 == 0 else gSc).issue(
                eng6, woslab[m % 2][:, :], wobf_s[m][:, :]))
            p.w("vector", *wosem[m])
            if m >= 2:
                p.w("vector", "tE", t6[m - 2])

            def v_fold(e, m=m):
                return e.tensor_tensor(woslab[m % 2][:, :], woslab[m % 2][:, :],
                                       outnw_slab[:, :], OP.mult)
            f6.append(p.o("vector", v_fold, "vE"))
            p.w("tensor", "vE", f6[m])
            if m == 0:
                gS.waitall("tensor")
                gSc.waitall("tensor")
            if m >= 1:
                p.w("tensor", "vE", v6[m - 1])

            def t_mm(e, m=m):
                last = None
                for k in range(32):
                    for tb in range(4):
                        last = e.matmul(pb[tb][:, :],
                                        woslab[m % 2][:, k * 128:(k + 1) * 128],
                                        ygt[k][:, tb * 512:(tb + 1) * 512],
                                        start=(k == 0), stop=(k == 31))
                return last
            t6.append(p.o("tensor", t_mm, "tE"))
            if m >= 1:
                p.w("sync", "vE", v6[m - 1])
            ose = gS.issue("sync", rchunk[:, 0:1024], out1_s[m][:, 0:1024])
            ose2 = gS.issue("sync", rchunk[:, 1024:2048], out1_s[m][:, 1024:2048])
            p.w("vector", "tE", t6[m])
            p.w("vector", *ose)
            p.w("vector", *ose2)
            if ostores:
                p.w("vector", *ostores[-1])
            p.w("vector", "vE")

            def v_fin(e, m=m):
                last = None
                for tb in range(4):
                    last = e.tensor_tensor(ochunk[:, tb * 512:(tb + 1) * 512],
                                           pb[tb][:, :],
                                           rchunk[:, tb * 512:(tb + 1) * 512], OP.add)
                return last
            v6.append(p.o("vector", v_fin, "vE"))
            p.w("sync", "vE", v6[m])
            ostores.append(gW.issue("sync", outT[m * 128:(m + 1) * 128, :],
                                    ochunk[:, :]))
        gW.waitall("sync")
        gW.waitall("vector")

    with ExitStack() as st:
        sems = {n: st.enter_context(nc.semaphore(n)) for n in p.names}
        block = st.enter_context(nc.Block())
        for ename in ENGINES:
            def mk(en):
                def go(eng):
                    for it in p.q[en]:
                        if it[0] == "w":
                            eng.wait_ge(sems[it[1]], it[2])
                        else:
                            inst = it[1](eng)
                            if it[2] is not None:
                                assert inst is not None, f"op returned None on {en}"
                                inst.then_inc(sems[it[2]], it[3])
                return go
            getattr(block, ename)(mk(ename))
    return nc


def _prep_inputs(L0_attn_x, L0_attn_norm_w, L0_attn_in_proj_w, L0_attn_conv_w,
                 L0_attn_conv_b, L0_attn_conv_norm_w, L0_attn_A_log, L0_attn_dt_w,
                 L0_attn_dt_b, L0_attn_out_norm_w, L0_attn_out_proj_w,
                 L0_ffn_W1, L0_ffn_W2):
    f = np.float32
    x = np.asarray(L0_attn_x, f)
    w_in = np.asarray(L0_attn_in_proj_w, f)
    perm = np.concatenate([
        np.arange(0, DIN),
        np.arange(DIN + H, DIN + H + GN),
        np.arange(DIN + H + GN, DIN + H + 2 * GN),
        np.arange(DIN + H + 2 * GN, NPROJ),
        np.arange(DIN, DIN + H),
    ])
    shared = {
        "w_in": np.ascontiguousarray(w_in[perm].T),
        "w1t": np.ascontiguousarray(np.asarray(L0_ffn_W1, f).T),
        "w2t": np.ascontiguousarray(np.asarray(L0_ffn_W2, f).T),
        "wot": np.ascontiguousarray(np.asarray(L0_attn_out_proj_w, f).T),
        "convw": np.ascontiguousarray(np.asarray(L0_attn_conv_w, f).reshape(CONV, 4)),
        "convb": np.ascontiguousarray(np.asarray(L0_attn_conv_b, f)),
        "convnw": np.ascontiguousarray(np.asarray(L0_attn_conv_norm_w, f)),
        "normw": np.ascontiguousarray(np.asarray(L0_attn_norm_w, f)),
        "outnw": np.ascontiguousarray(np.asarray(L0_attn_out_norm_w, f)),
        "alog": np.ascontiguousarray(np.asarray(L0_attn_A_log, f)),
        "dtb": np.ascontiguousarray(np.asarray(L0_attn_dt_b, f)),
        "dtwt": np.ascontiguousarray(np.asarray(L0_attn_dt_w, f).T),
    }
    in_maps = []
    for c in range(8):
        m = dict(shared)
        m["xT"] = np.ascontiguousarray(x[c].T)
        in_maps.append(m)
    return in_maps


def kernel(**inputs):
    in_maps = _prep_inputs(**inputs)
    nc = build(N_PHASES, DEBUG)
    res = run_bass_kernel_spmd(nc, in_maps, core_ids=list(range(8)))
    out = np.stack([np.asarray(r["outT"]).T for r in res.results], axis=0)
    return np.ascontiguousarray(out.astype(np.float32))


# version 14
# speedup vs baseline: 1.4881x; 1.0830x over previous
import numpy as np
import ml_dtypes
from contextlib import ExitStack
import concourse.bass as bass
import concourse.mybir as mybir
from concourse.bass_utils import run_bass_kernel_spmd

F32 = mybir.dt.float32
BF16 = mybir.dt.bfloat16
U8 = mybir.dt.uint8
AF = mybir.ActivationFunctionType
OP = mybir.AluOpType
AX = mybir.AxisListType

D = 2048
L = 2048
DIN = 4096
H = 64
DH = 64
NST = 128
G = 8
GN = G * NST
CONV = DIN + 2 * GN
NPROJ = DIN + H + 2 * GN + DIN
DFF = 8192
NK = D // 128
NCH = 16
CK = 128
MAGIC = 12582912.0
BIGM = 30000.0

ENGINES = ("sync", "tensor", "vector", "scalar", "gpsimd")

N_PHASES = 7
DEBUG = False


class Sched:
    def __init__(self):
        self.q = {e: [] for e in ENGINES}
        self.cnt = {}
        self.names = []

    def sem(self, *names):
        for n in names:
            self.cnt[n] = 0
            self.names.append(n)

    def w(self, eng, s, v=None):
        v = self.cnt[s] if v is None else v
        if v > 0:
            self.q[eng].append(("w", s, v))

    def o(self, eng, fn, inc=None, n=1):
        self.q[eng].append(("o", fn, inc, n))
        if inc is not None:
            self.cnt[inc] += n
            return self.cnt[inc]
        return None

    def barrier(self):
        snap = [(s, self.cnt[s]) for s in self.names if self.cnt[s] > 0]
        for e in ENGINES:
            for s, v in snap:
                self.q[e].append(("w", s, v))


class DGroup:

    def __init__(self, p, base, n=6):
        self.p = p
        self.names = [f"{base}{i}" for i in range(n)]
        p.sem(*self.names)
        self.i = 0
        self.waited = {s: 0 for s in self.names}

    def issue(self, eng, out_ap, in_ap, **kw):
        s = self.names[self.i % len(self.names)]
        self.i += 1
        if self.p.cnt[s] > self.waited[s]:
            self.p.w(eng, s, self.p.cnt[s])
            self.waited[s] = self.p.cnt[s]
        self.p.o(eng,
                 lambda e, o=out_ap, i=in_ap, kw=kw: e.dma_start(out=o, in_=i, **kw),
                 s, 16)
        return s, self.p.cnt[s]

    def wait_last(self, eng):
        s = self.names[(self.i - 1) % len(self.names)]
        self.p.w(eng, s, self.p.cnt[s])

    def waitall(self, eng, issuer=False):
        for s in self.names:
            self.p.w(eng, s, self.p.cnt[s])
            if issuer:
                self.waited[s] = max(self.waited[s], self.p.cnt[s])


def _sz(dt):
    return {F32: 4, BF16: 2}[dt]


def build(n_phases=N_PHASES, debug=False):
    nc = bass.Bass("TRN2", target_bir_lowering=False)

    xT = nc.dram_tensor("xT", [D, L], F32, kind="ExternalInput").ap()
    w_in = nc.dram_tensor("w_in", [D, NPROJ], F32, kind="ExternalInput").ap()
    w1t = nc.dram_tensor("w1t", [D, DFF], F32, kind="ExternalInput").ap()
    w2t = nc.dram_tensor("w2t", [DFF, D], F32, kind="ExternalInput").ap()
    wot = nc.dram_tensor("wot", [DIN, D], F32, kind="ExternalInput").ap()
    convw = nc.dram_tensor("convw", [CONV, 4], F32, kind="ExternalInput").ap()
    convb = nc.dram_tensor("convb", [CONV], F32, kind="ExternalInput").ap()
    convnw = nc.dram_tensor("convnw", [CONV], F32, kind="ExternalInput").ap()
    normw = nc.dram_tensor("normw", [D], F32, kind="ExternalInput").ap()
    outnw = nc.dram_tensor("outnw", [DIN], F32, kind="ExternalInput").ap()
    alog = nc.dram_tensor("alog", [H], F32, kind="ExternalInput").ap()
    dtb = nc.dram_tensor("dtb", [H], F32, kind="ExternalInput").ap()
    dtwt = nc.dram_tensor("dtwt", [H, H], F32, kind="ExternalInput").ap()
    outT = nc.dram_tensor("outT", [D, L], F32, kind="ExternalOutput").ap()

    r_s = nc.dram_tensor("r_s", [NK, 128, L], F32, kind="Internal").ap()
    craw_s = nc.dram_tensor("craw_s", [CONV // 128, 128, L], BF16, kind="Internal").ap()
    gate_s = nc.dram_tensor("gate_s", [DIN // 128, 128, L], BF16, kind="Internal").ap()
    a_s = nc.dram_tensor("a_s", [DFF // 128, 128, L], BF16, kind="Internal").ap()
    yg_s = nc.dram_tensor("yg_s", [NCH, DIN // 128, 128, CK], BF16, kind="Internal").ap()
    out1_s = nc.dram_tensor("out1_s", [NK, 128, L], F32, kind="Internal").ap()

    dbg = {}
    if debug:
        dbg["dbg_xn"] = nc.dram_tensor("dbg_xn", [D, L], BF16, kind="ExternalOutput").ap()
        dbg["dbg_r"] = nc.dram_tensor("dbg_r", [D, L], F32, kind="ExternalOutput").ap()
        dbg["dbg_craw"] = nc.dram_tensor("dbg_craw", [48, 128, L], BF16, kind="ExternalOutput").ap()
        dbg["dbg_gate"] = nc.dram_tensor("dbg_gate", [32, 128, L], BF16, kind="ExternalOutput").ap()
        dbg["dbg_adt"] = nc.dram_tensor("dbg_adt", [64, L], F32, kind="ExternalOutput").ap()
        dbg["dbg_crms"] = nc.dram_tensor("dbg_crms", [128, L], BF16, kind="ExternalOutput").ap()
        dbg["dbg_a"] = nc.dram_tensor("dbg_a", [DFF // 128, 128, L], BF16, kind="ExternalOutput").ap()
        dbg["dbg_yg"] = nc.dram_tensor("dbg_yg", [NCH, 128, DIN], BF16, kind="ExternalOutput").ap()
        dbg["dbg_y"] = nc.dram_tensor("dbg_y", [NCH, 128, DIN], BF16, kind="ExternalOutput").ap()
        dbg["dbg_acs"] = nc.dram_tensor("dbg_acs", [64, L], F32, kind="ExternalOutput").ap()
        dbg["dbg_h"] = nc.dram_tensor("dbg_h", [128, H * DH], F32, kind="ExternalOutput").ap()
        dbg["dbg_lex"] = nc.dram_tensor("dbg_lex", [8, 128, 128], BF16, kind="ExternalOutput").ap()
        dbg["dbg_mcb"] = nc.dram_tensor("dbg_mcb", [8, 128, 128], BF16, kind="ExternalOutput").ap()
        dbg["dbg_xtok"] = nc.dram_tensor("dbg_xtok", [128, DIN], BF16, kind="ExternalOutput").ap()
        dbg["dbg_cbs"] = nc.dram_tensor("dbg_cbs", [128, 128], BF16, kind="ExternalOutput").ap()

    ones_col = nc.alloc_sbuf_tensor("ones_col", [128, 1], BF16)
    ones_row = nc.alloc_sbuf_tensor("ones_row", [1, 128], F32)
    onesb = nc.alloc_sbuf_tensor("onesb", [128, 128], BF16)
    ones64 = nc.alloc_sbuf_tensor("ones64", [64, 128], BF16)
    ones64f = nc.alloc_sbuf_tensor("ones64f", [64, 128], F32)
    id64 = nc.alloc_sbuf_tensor("id64", [64, 64], BF16)
    nid64 = nc.alloc_sbuf_tensor("nid64", [64, 64], BF16)
    idb = nc.alloc_sbuf_tensor("idb", [128, 128], BF16)
    c1m = nc.alloc_sbuf_tensor("c1m", [128, 128], BF16)
    c2m = nc.alloc_sbuf_tensor("c2m", [128, 128], BF16)
    cst = nc.alloc_sbuf_tensor("cst", [128, 4], F32)
    normw_kp = nc.alloc_sbuf_tensor("normw_kp", [128, NK], F32)
    convw_kp = nc.alloc_sbuf_tensor("convw_kp", [128, 192], F32)
    convb_kp = nc.alloc_sbuf_tensor("convb_kp", [128, 48], F32)
    convnw_kp = nc.alloc_sbuf_tensor("convnw_kp", [128, 48], F32)
    outnw_kp = nc.alloc_sbuf_tensor("outnw_kp", [128, 32], F32)
    norm_slab = nc.alloc_sbuf_tensor("norm_slab", [128, D], BF16)
    outnw_slab = nc.alloc_sbuf_tensor("outnw_slab", [128, DIN], BF16)
    alog_sb = nc.alloc_sbuf_tensor("alog_sb", [64, 1], F32)
    dtb_sb = nc.alloc_sbuf_tensor("dtb_sb", [64, 1], F32)
    dtw_sb = nc.alloc_sbuf_tensor("dtw_sb", [64, 64], F32)
    negea = nc.alloc_sbuf_tensor("negea", [64, 1], F32)

    AB = 196096
    arena = nc.alloc_sbuf_tensor("arena", [128, AB], U8)

    def V(off, elems, dt, p=128):
        assert off % 64 == 0 and off + elems * _sz(dt) <= AB, (off, elems, dt)
        return arena[0:p, off:off + elems * _sz(dt)].bitcast(dt)

    SC = 256
    cact = [V(0, 48 * SC, BF16)]
    gact = [V(24576, 32 * SC, BF16)]
    xtok = V(40960, DIN, BF16)
    xdec = V(49152, DIN, BF16)
    btok = V(57344, GN, BF16)
    ytok = V(59392, DIN, BF16)
    lexp = V(67584, 512, BF16)
    mcbq = [V(68608, 512, BF16), V(69632, 512, BF16)]
    cbs = [V(70656, 128, BF16), V(70912, 128, BF16)]
    eas = V(71168, 512, BF16)
    cdl_t = V(72192, 128, BF16, p=64)
    ysc = V(72448, 512, BF16)
    yrms = V(73472, 16, F32)
    awk = V(73536, 64, F32, p=64)
    htmp = V(73792, 512, F32)
    crmsb = V(75840, L, BF16)
    hi_t = V(79936, CK, BF16, p=64)
    lo_t = V(80192, CK, BF16, p=64)
    exp_t = V(80448, CK, BF16, p=64)
    dec_t = V(80704, CK, BF16, p=64)
    acs = V(80960, L, F32, p=64)
    h32 = V(89152, H * DH, F32)
    hbf = V(105536, H * DH, BF16)
    a_dt = V(89152, L, F32, p=64)
    wk1 = V(97344, 512, F32, p=64)
    wk2 = V(99392, 512, F32, p=64)
    wk3 = V(101440, 512, F32, p=64)
    xn = [V(113728 + k * 4096, D, BF16) for k in range(NK)]
    w1b = [V(179264, D, BF16), V(183360, D, BF16)]
    qf = V(187456, 512, F32)
    qt = V(189504, 512, F32)
    astage = [V(191552, L, BF16)]
    exp_tok = V(195648, 64, BF16)
    cdx = V(195776, 64, BF16)
    dec_tok = V(195904, 64, BF16)
    s_b = V(0, L, F32)
    ms_b = V(8192, L, F32)
    xstage = [V(16384, L, F32), V(24576, L, F32)]
    rstage = [V(32768, L, F32), V(40960, L, F32)]
    xbf_t = [V(49152, L, BF16), V(53248, L, BF16)]
    sq_t = [V(57344, L, BF16), V(61440, L, BF16)]
    rowm = V(65536, L, F32, p=1)
    rowe = V(73728, L, F32, p=1)
    rows_ = V(81920, L, F32, p=1)
    rowt = V(90112, L, F32, p=1)
    rmsf_b = V(0, L, F32)
    wcast = [V(8192, D, BF16), V(12288, D, BF16)]
    wf = [V(16384, D, BF16), V(20480, D, BF16)]
    cslab = V(24576, 2051, BF16)
    ct0 = V(28736, L, BF16)
    ct1 = V(32832, L, BF16)
    ct2 = V(36928, L, BF16)
    csq = V(41024, L, BF16)
    crawt = V(45120, L, BF16)
    gtmp = V(28736, L, F32)
    rowcv = V(49216, L, F32, p=1)
    rowcv2 = V(57408, L, F32, p=1)
    atile = [V(k * 2048, 1024, BF16) for k in range(64)]
    ygt = [V(k * 4096, L, BF16) for k in range(32)]
    w2slab = [V(131072, 64 * 128, BF16), V(147456, 64 * 128, BF16)]
    woslab = [V(131072, 32 * 128, BF16), V(139264, 32 * 128, BF16)]
    ochunk = V(163840, 2048, F32)
    rchunk = V(172032, 2048, F32)

    p = Sched()
    p.sem("tE", "vE", "sE", "gE")
    gS = DGroup(p, "dS")
    gG = DGroup(p, "dG")
    gR = DGroup(p, "dR")
    gD = DGroup(p, "dD")
    gP = DGroup(p, "dC")

    if n_phases >= 1:
        gS.issue("sync", normw_kp[:, :], normw.rearrange("(k p) -> p k", p=128),
                 allow_slow_non_contiguous=True)
        gS.issue("sync", convw_kp[:, :].rearrange("p (k j) -> p k j", j=4),
                 convw.rearrange("(k p) j -> p k j", p=128),
                 allow_slow_non_contiguous=True)
        gS.issue("sync", convb_kp[:, :], convb.rearrange("(k p) -> p k", p=128),
                 allow_slow_non_contiguous=True)
        gS.issue("sync", convnw_kp[:, :], convnw.rearrange("(k p) -> p k", p=128),
                 allow_slow_non_contiguous=True)
        gS.issue("sync", outnw_kp[:, :], outnw.rearrange("(k p) -> p k", p=128),
                 allow_slow_non_contiguous=True)
        gS.issue("sync", alog_sb[0:64, :], alog.rearrange("(h o) -> h o", o=1))
        gS.issue("sync", dtb_sb[0:64, :], dtb.rearrange("(h o) -> h o", o=1))
        gS.issue("sync", dtw_sb[0:64, :], dtwt)

        def g_consts_a(e):
            e.memset(ones_col[:, :], 1.0)
            e.memset(onesb[:, :], 1.0)
            e.memset(ones64[0:64, :], 1.0)
            e.memset(ones64f[0:64, :], 1.0)
            e.memset(ones_row[:, :], 1.0)
            e.memset(cst[:, 0:1], 1e-5)
            e.memset(cst[:, 1:2], 1e-6)
            e.memset(cst[:, 2:3], 1.0)
            e.memset(id64[0:64, :], 1.0)
            e.memset(idb[:, :], 1.0)
            e.memset(nid64[0:64, :], -1.0)
            e.memset(c1m[:, :], 1.0)
            return e.memset(c2m[:, :], -BIGM)
        p.o("gpsimd", g_consts_a, "gE")
        p.w("gpsimd", "gE")

        def g_consts_b(e):
            e.affine_select(out=id64[0:64, :], in_=id64[0:64, :], pattern=[[-1, 64]],
                            compare_op=OP.is_equal, fill=0.0, base=0, channel_multiplier=1)
            e.affine_select(out=idb[:, :], in_=idb[:, :], pattern=[[-1, 128]],
                            compare_op=OP.is_equal, fill=0.0, base=0, channel_multiplier=1)
            e.affine_select(out=nid64[0:64, :], in_=nid64[0:64, :], pattern=[[-1, 64]],
                            compare_op=OP.is_equal, fill=0.0, base=0, channel_multiplier=1)
            e.affine_select(out=c1m[:, :], in_=c1m[:, :], pattern=[[1, 128]],
                            compare_op=OP.is_ge, fill=0.0, base=-1, channel_multiplier=-1)
            return e.affine_select(out=c2m[:, :], in_=c2m[:, :], pattern=[[-1, 128]],
                                   compare_op=OP.is_ge, fill=0.0, base=0, channel_multiplier=1)
        p.o("gpsimd", g_consts_b, "gE")

        gS.waitall("scalar")
        p.o("scalar", lambda e: e.activation(negea[0:64, :], alog_sb[0:64, :], AF.Exp),
            "sE")
        gS.waitall("vector")
        p.w("vector", "gE")
        p.w("vector", "sE")

        def v_slabs(e):
            for k in range(NK):
                e.tensor_scalar(norm_slab[:, k * 128:(k + 1) * 128], onesb[:, :],
                                normw_kp[:, k:k + 1], None, OP.mult)
            for k in range(32):
                e.tensor_scalar(outnw_slab[:, k * 128:(k + 1) * 128], onesb[:, :],
                                outnw_kp[:, k:k + 1], None, OP.mult)
            return e.tensor_scalar(negea[0:64, :], negea[0:64, :], -1.0, None, OP.mult)
        p.o("vector", v_slabs, "vE")

    pb = [nc.alloc_psum_tensor(f"pb{i}", [128, 512], F32) for i in range(8)]

    if n_phases >= 2:
        p.w("tensor", "gE")
        sc_cnt = []
        for k in range(NK):
            if k % 2 == 0:
                if k >= 2:
                    p.w("sync", "sE", sc_cnt[k - 2])
                sv = gS.issue("sync", xstage[k % 2][:, :],
                              xT[k * 128:(k + 1) * 128, :])
            else:
                if k >= 2:
                    p.w("gpsimd", "sE", sc_cnt[k - 2])
                sv = gP.issue("gpsimd", xstage[k % 2][:, :],
                              xT[k * 128:(k + 1) * 128, :])
            p.w("scalar", *sv)
            if k >= 2:
                p.w("scalar", "tE", p.cnt["tE"] - 1)

            def s_cast(e, k=k):
                e.activation(xbf_t[k % 2][:, :], xstage[k % 2][:, :], AF.Copy)
                return e.activation(sq_t[k % 2][:, :], xstage[k % 2][:, :], AF.Square)
            sc_cnt.append(p.o("scalar", s_cast, "sE"))
            p.w("tensor", "sE")

            def t_stats(e, k=k):
                for tb in range(4):
                    e.matmul(pb[tb][0:1, :], ones_col[:, :],
                             xbf_t[k % 2][:, tb * 512:(tb + 1) * 512],
                             start=(k == 0), stop=(k == NK - 1))
                last = None
                for tb in range(4):
                    last = e.matmul(pb[4 + tb][0:1, :], ones_col[:, :],
                                    sq_t[k % 2][:, tb * 512:(tb + 1) * 512],
                                    start=(k == 0), stop=(k == NK - 1))
                return last
            p.o("tensor", t_stats, "tE")

        def vch(fn):
            p.w("vector", "vE")
            p.o("vector", fn, "vE")

        def sch(fn):
            p.w("scalar", "vE")
            p.o("scalar", fn, "sE")
            p.w("vector", "sE")

        p.w("vector", "tE")

        def v_fin1(e):
            last = None
            for tb in range(4):
                e.tensor_scalar(rowm[0:1, tb * 512:(tb + 1) * 512], pb[tb][0:1, :],
                                1.0 / D, None, OP.mult)
                last = e.tensor_scalar(rowe[0:1, tb * 512:(tb + 1) * 512],
                                       pb[4 + tb][0:1, :], 1.0 / D, None, OP.mult)
            return last
        p.o("vector", v_fin1, "vE")
        vch(lambda e: e.tensor_tensor(rowt[0:1, :], rowm[0:1, :], rowm[0:1, :], OP.mult))
        vch(lambda e: e.tensor_tensor(rowe[0:1, :], rowe[0:1, :], rowt[0:1, :], OP.subtract))
        sch(lambda e: e.activation(rowt[0:1, :], rowe[0:1, :], AF.Sqrt, bias=cst[0:1, 0:1]))
        vch(lambda e: e.reciprocal(rows_[0:1, :], rowt[0:1, :]))
        vch(lambda e: e.tensor_tensor(rowt[0:1, :], rowe[0:1, :], rows_[0:1, :], OP.mult))
        vch(lambda e: e.tensor_tensor(rowt[0:1, :], rowt[0:1, :], rows_[0:1, :], OP.mult))
        sch(lambda e: e.activation(rowt[0:1, :], rowt[0:1, :], AF.Sqrt, bias=cst[0:1, 1:2]))
        vch(lambda e: e.reciprocal(rowe[0:1, :], rowt[0:1, :]))
        vch(lambda e: e.tensor_tensor(rowm[0:1, :], rowm[0:1, :], rows_[0:1, :], OP.mult))

        for row, dst in ((rows_, s_b), (rowm, ms_b)):
            p.w("tensor", "vE")

            def t_bc(e, row=row):
                last = None
                for tb in range(4):
                    last = e.matmul(pb[tb][:, :], ones_row[:, :],
                                    row[0:1, tb * 512:(tb + 1) * 512],
                                    start=True, stop=True)
                return last
            p.o("tensor", t_bc, "tE")
            p.w("vector", "tE")

            def v_bc(e, dst=dst):
                last = None
                for tb in range(4):
                    last = e.tensor_copy(out=dst[:, tb * 512:(tb + 1) * 512],
                                         in_=pb[tb][:, :])
                return last
            p.o("vector", v_bc, "vE")

        xcons = []
        rsems = []
        p.w("sync", "vE")
        p.w("sync", "sE")
        p.w("gpsimd", "vE")
        p.w("gpsimd", "sE")
        for k in range(NK):
            if k % 2 == 0:
                if k >= 2:
                    p.w("sync", "vE", xcons[k - 2])
                sv = gS.issue("sync", xstage[k % 2][:, :],
                              xT[k * 128:(k + 1) * 128, :])
            else:
                if k >= 2:
                    p.w("gpsimd", "vE", xcons[k - 2])
                sv = gP.issue("gpsimd", xstage[k % 2][:, :],
                              xT[k * 128:(k + 1) * 128, :])
            p.w("vector", *sv)
            if k >= 2:
                for pr in rsems[k - 2]:
                    p.w("vector", *pr)

            def v_ln1(e, k=k):
                return e.tensor_tensor(rstage[k % 2][:, :], xstage[k % 2][:, :],
                                       s_b[:, :], OP.mult)
            p.o("vector", v_ln1, "vE")
            p.w("vector", "vE")

            def v_ln2a(e, k=k):
                return e.tensor_tensor(xn[k][:, :], rstage[k % 2][:, :], ms_b[:, :],
                                       OP.subtract)
            p.o("vector", v_ln2a, "vE")
            p.w("vector", "vE")

            def v_ln2b(e, k=k):
                return e.tensor_tensor(rstage[k % 2][:, :], rstage[k % 2][:, :],
                                       xstage[k % 2][:, :], OP.add)
            xcons.append(p.o("vector", v_ln2b, "vE"))
            p.w("vector", "vE")

            def v_ln3(e, k=k):
                return e.tensor_tensor(rstage[k % 2][:, :], rstage[k % 2][:, :],
                                       ms_b[:, :], OP.subtract)
            p.o("vector", v_ln3, "vE")
            prs = []
            if debug:
                p.w("sync", "vE")
                gD.issue("sync", dbg["dbg_xn"][k * 128:(k + 1) * 128, :], xn[k][:, :])
                prs.append(gR.issue("sync", dbg["dbg_r"][k * 128:(k + 1) * 128, :],
                                    rstage[k % 2][:, :]))
            if k % 2 == 0:
                p.w("sync", "vE")
                prs.append(gR.issue("sync", r_s[k], rstage[k % 2][:, :]))
            else:
                p.w("gpsimd", "vE")
                prs.append(gP.issue("gpsimd", r_s[k], rstage[k % 2][:, :]))
            rsems.append(prs)
        gR.waitall("sync")
        gP.waitall("gpsimd")
        gD.waitall("sync")

    if n_phases >= 3:
        NM = 81
        p.w("vector", "sE")
        p.w("vector", "tE")
        p.w("gpsimd", "vE")
        gR.waitall("vector")
        gD.waitall("vector")
        gW = DGroup(p, "dW")

        p.w("tensor", "vE")

        def t_bcf(e):
            last = None
            for tb in range(4):
                last = e.matmul(pb[tb][:, :], ones_row[:, :],
                                rowe[0:1, tb * 512:(tb + 1) * 512],
                                start=True, stop=True)
            return last
        p.o("tensor", t_bcf, "tE")
        p.w("vector", "tE")

        def v_bcf(e):
            last = None
            for tb in range(4):
                last = e.tensor_copy(out=rmsf_b[:, tb * 512:(tb + 1) * 512],
                                     in_=pb[tb][:, :])
            return last
        p.o("vector", v_bcf, "vE")
        p.w("vector", "vE")
        p.o("vector", lambda e: e.memset(cslab[:, 0:3], 0.0), "vE")

        wsem, vfold, tmm, vevac = [], [], [], []
        statc = {}
        statt = {}
        cstore, gstore = [], []
        nslab64 = norm_slab[:, :].rearrange("p (k c) -> p k c", c=128)[:, :, 0:64]

        def emit_cast_fold(mm):
            c0 = mm * 128
            w = 64 if mm == NM - 1 else 128
            if mm >= 2:
                p.w("gpsimd", "vE", vfold[mm - 2])
            src_ap = w_in[:, c0:c0 + w].rearrange("(k p) c -> p k c", p=128)
            dst_ap = wcast[mm % 2][:, 0:NK * w].rearrange("p (k c) -> p k c", c=w)
            wsem.append(gG.issue("gpsimd", dst_ap, src_ap))

            p.w("vector", *wsem[mm])
            if mm >= 2:
                p.w("vector", "tE", tmm[mm - 2])

            def v_fold(e, mm=mm, w=w):
                if w == 128:
                    return e.tensor_tensor(wf[mm % 2][:, :], wcast[mm % 2][:, :],
                                           norm_slab[:, :], OP.mult)
                return e.tensor_tensor(
                    wf[mm % 2][:, 0:NK * w].rearrange("p (k c) -> p k c", c=w),
                    wcast[mm % 2][:, 0:NK * w].rearrange("p (k c) -> p k c", c=w),
                    nslab64, OP.mult)
            vfold.append(p.o("vector", v_fold, "vE"))

        emit_cast_fold(0)
        for m in range(NM):
            w = 64 if m == NM - 1 else 128
            if m + 1 < NM:
                emit_cast_fold(m + 1)
            p.w("tensor", "vE", vfold[m])
            if m >= 1:
                p.w("tensor", "vE", vevac[m - 1])

            def t_mm(e, m=m, w=w):
                last = None
                for k in range(NK):
                    for tb in range(4):
                        last = e.matmul(pb[tb][0:w, :],
                                        wf[m % 2][:, k * w:(k + 1) * w],
                                        xn[k][:, tb * 512:(tb + 1) * 512],
                                        start=(k == 0), stop=(k == NK - 1))
                return last
            tmm.append(p.o("tensor", t_mm, "tE"))
            if 1 <= m <= 48:
                ms = m - 1
                p.w("tensor", "sE", statc[ms])

                def t_stat(e, ms=ms):
                    last = None
                    for tb in range(4):
                        last = e.matmul(pb[4 + tb][0:1, :], ones_col[:, :],
                                        csq[:, tb * 512:(tb + 1) * 512],
                                        start=(ms == 0), stop=(ms == 47))
                    return last
                statt[ms] = p.o("tensor", t_stat, "tE")

            p.w("vector", "tE", tmm[m])
            if m < 48:
                def v_ev(e, m=m):
                    last = None
                    for tb in range(4):
                        last = e.tensor_tensor(
                            cslab[:, 3 + tb * 512:3 + (tb + 1) * 512],
                            pb[tb][:, :], rmsf_b[:, tb * 512:(tb + 1) * 512], OP.mult)
                    return last
                vevac.append(p.o("vector", v_ev, "vE"))

                p.w("vector", "vE")
                p.o("vector", lambda e, m=m: e.tensor_scalar(
                    ct0[:, :], cslab[:, 0:L], convw_kp[:, 4 * m:4 * m + 1],
                    None, OP.mult), "vE")
                p.w("vector", "vE")
                if m >= 1:
                    for pr in cstore[m - 1]:
                        p.w("vector", *pr)
                p.o("vector", lambda e, m=m: e.scalar_tensor_tensor(
                    ct1[:, :], cslab[:, 1:1 + L], convw_kp[:, 4 * m + 1:4 * m + 2],
                    ct0[:, :], OP.mult, OP.add), "vE")
                p.w("vector", "vE")
                p.o("vector", lambda e, m=m: e.scalar_tensor_tensor(
                    ct2[:, :], cslab[:, 2:2 + L], convw_kp[:, 4 * m + 2:4 * m + 3],
                    ct1[:, :], OP.mult, OP.add), "vE")
                p.w("vector", "vE")
                if m >= 1:
                    p.w("vector", "sE", statc[m - 1])
                p.o("vector", lambda e, m=m: e.scalar_tensor_tensor(
                    ct0[:, :], cslab[:, 3:3 + L], convw_kp[:, 4 * m + 3:4 * m + 4],
                    ct2[:, :], OP.mult, OP.add), "vE")
                p.w("vector", "vE")
                p.o("vector", lambda e, m=m: e.tensor_scalar(
                    crawt[:, :], ct0[:, :], convb_kp[:, m:m + 1], None, OP.add), "vE")

                p.w("scalar", "vE")
                if m >= 1:
                    p.w("scalar", "tE", statt[m - 1])
                statc[m] = p.o("scalar", lambda e: e.activation(
                    csq[:, :], crawt[:, :], AF.Square), "sE")
                p.w("vector", "vE")
                p.o("vector", lambda e, m=m: e.tensor_scalar(
                    ct1[:, :], crawt[:, :], convnw_kp[:, m:m + 1], None, OP.mult), "vE")
                p.w("sync", "vE")
                prs = [gW.issue("sync", craw_s[m], ct1[:, :])]
                if debug:
                    prs.append(gD.issue("sync", dbg["dbg_craw"][m], ct1[:, :]))
                cstore.append(prs)
            elif m < 80:
                mg = m - 48
                p.w("vector", "vE", p.cnt["vE"])
                if mg >= 1:
                    p.w("vector", "sE", gsil[mg - 1])
                else:
                    for pr in cstore[-1]:
                        p.w("vector", *pr)

                def v_evg(e):
                    last = None
                    for tb in range(4):
                        last = e.tensor_tensor(
                            gtmp[:, tb * 512:(tb + 1) * 512],
                            pb[tb][:, :], rmsf_b[:, tb * 512:(tb + 1) * 512], OP.mult)
                    return last
                vevac.append(p.o("vector", v_evg, "vE"))
                p.w("scalar", "vE")
                p.w("scalar", "sE", p.cnt["sE"])
                if mg >= 1:
                    for pr in gstore[mg - 1]:
                        p.w("scalar", *pr)
                elif cstore:
                    for pr in cstore[-1]:
                        p.w("scalar", *pr)
                if mg == 0:
                    gsil = []
                gsil.append(p.o("scalar",
                                lambda e: e.activation(crawt[:, :], gtmp[:, :], AF.Silu),
                                "sE"))
                p.w("sync", "sE")
                prs = [gW.issue("sync", gate_s[mg], crawt[:, :])]
                if debug:
                    prs.append(gD.issue("sync", dbg["dbg_gate"][mg], crawt[:, :]))
                gstore.append(prs)
            else:
                def v_eva(e):
                    last = None
                    for tb in range(4):
                        last = e.tensor_tensor(
                            a_dt[0:64, tb * 512:(tb + 1) * 512],
                            pb[tb][0:64, :], rmsf_b[0:64, tb * 512:(tb + 1) * 512],
                            OP.mult)
                    return last
                vevac.append(p.o("vector", v_eva, "vE"))
                if debug:
                    p.w("sync", "vE")
                    gD.issue("sync", dbg["dbg_adt"], a_dt[0:64, :])

        p.w("vector", "tE")

        def v_cst1(e):
            last = None
            for tb in range(4):
                last = e.tensor_scalar(rowcv[0:1, tb * 512:(tb + 1) * 512],
                                       pb[4 + tb][0:1, :], 1.0 / CONV, None, OP.mult)
            return last
        p.o("vector", v_cst1, "vE")
        p.w("scalar", "vE")
        p.o("scalar", lambda e: e.activation(rowcv[0:1, :], rowcv[0:1, :], AF.Sqrt,
                                             bias=cst[0:1, 1:2]), "sE")
        p.w("vector", "sE")
        p.o("vector", lambda e: e.reciprocal(rowcv2[0:1, :], rowcv[0:1, :]), "vE")
        p.w("tensor", "vE")

        def t_bcc(e):
            last = None
            for tb in range(4):
                last = e.matmul(pb[4 + tb][:, :], ones_row[:, :],
                                rowcv2[0:1, tb * 512:(tb + 1) * 512],
                                start=True, stop=True)
            return last
        p.o("tensor", t_bcc, "tE")
        p.w("vector", "tE")

        def v_bcc(e):
            last = None
            for tb in range(4):
                last = e.tensor_copy(out=crmsb[:, tb * 512:(tb + 1) * 512],
                                     in_=pb[4 + tb][:, :])
            return last
        p.o("vector", v_bcc, "vE")
        if debug:
            p.w("sync", "vE")
            gD.issue("sync", dbg["dbg_crms"], crmsb[:, :])
        gW.waitall("sync")
        gD.waitall("sync")

    if n_phases >= 4:
        p.w("tensor", "vE")
        p.w("vector", "tE")
        p.w("vector", "sE")
        p.w("scalar", "vE")
        p.w("scalar", "tE")
        gW.waitall("vector")
        gW.waitall("scalar")
        gG.waitall("gpsimd", issuer=True)
        p.w("gpsimd", "vE")
        p.w("gpsimd", "tE")
        gR.waitall("vector")
        gR.waitall("scalar")
        gR.waitall("tensor")
        gD.waitall("vector")
        gD.waitall("scalar")
        p.w("sync", "vE")
        p.w("sync", "tE")
        p.w("sync", "sE")
        gT = DGroup(p, "dX")

        st3 = {"w1sem": [], "tmm": [], "vq": [], "astore": [], "agel": [],
               "qfv": []}

        def ph3_unit(m):
            if m >= 2:
                p.w("gpsimd", "tE", st3["tmm"][m - 2])
            src_ap = w1t[:, m * 128:(m + 1) * 128].rearrange("(k p) c -> p k c", p=128)
            dst_ap = w1b[m % 2][:, :].rearrange("p (k c) -> p k c", c=128)
            st3["w1sem"].append(gG.issue("gpsimd", dst_ap, src_ap))
            p.w("tensor", *st3["w1sem"][m])
            ag = None
            for tb in range(4):
                idx = m * 4 + tb
                if idx >= 2:
                    p.w("tensor", "vE", st3["qfv"][idx - 2])

                def t_mm(e, m=m, tb=tb, idx=idx):
                    last = None
                    for k in range(NK):
                        last = e.matmul(pb[idx % 2][:, :],
                                        w1b[m % 2][:, k * 128:(k + 1) * 128],
                                        xn[k][:, tb * 512:(tb + 1) * 512],
                                        start=(k == 0), stop=(k == NK - 1))
                    return last
                tmb = p.o("tensor", t_mm, "tE")
                p.w("vector", "tE", tmb)
                p.w("vector", "vE")
                if ag is not None:
                    p.w("vector", "sE", ag)
                elif st3["agel"]:
                    p.w("vector", "sE", st3["agel"][-1])
                st3["qfv"].append(p.o("vector", lambda e, idx=idx: e.tensor_scalar(
                    qf[:, :], pb[idx % 2][:, :], 10.0, -32768.49,
                    OP.mult, OP.max), "vE"))
                p.w("vector", "vE")
                p.o("vector", lambda e: e.tensor_scalar(
                    qt[:, :], qf[:, :], 32767.49, MAGIC, OP.min, OP.add), "vE")
                p.w("vector", "vE")
                vq = p.o("vector", lambda e: e.tensor_scalar(
                    qf[:, :], qt[:, :], -MAGIC, None, OP.add), "vE")
                p.w("scalar", "vE", vq)
                if tb == 0 and m >= 1:
                    for pr in st3["astore"][m - 1]:
                        p.w("scalar", *pr)
                ag = p.o("scalar", lambda e, tb=tb: e.activation(
                    astage[0][:, tb * 512:(tb + 1) * 512], qf[:, :],
                    AF.Gelu_apprx_tanh, scale=0.1), "sE")
            st3["tmm"].append(tmb)
            st3["vq"].append(vq)
            st3["agel"].append(ag)
            p.w("sync", "sE", ag)
            st3["astore"].append([gW.issue("sync", a_s[m], astage[0][:, :])])

        if n_phases >= 5:
            st4 = {"sccons": {}, "scl": {}, "silc": {},
                   "hbfc": [0] * 8, "xdec": [0] * 8, "states": [0] * 8,
                   "yadd": [0] * 8, "ysc": [0] * 8, "htmp": [0] * 8,
                   "hadd": [0] * 8, "cbev": 0, "mcb01": 0, "mcb01_2": 0,
                   "ydq": 0, "lex01": 0, "ygstore": [],
                   "prev_vyg": 0, "tlast": 0, "vacs": 0, "easv": 0}
            ytok3 = ytok[:, :].rearrange("p (m t) -> p m t", t=128)

            def ph4_prelude():
                relc = {}
                for cb in range(4):
                    bank = pb[6 + (cb % 2)]
                    if cb >= 2:
                        p.w("tensor", "sE", relc[cb - 2])

                    def t_dt(e, cb=cb, bank=bank):
                        return e.matmul(bank[0:64, :], dtw_sb[0:64, :],
                                        a_dt[0:64, cb * 512:(cb + 1) * 512],
                                        start=True, stop=True)
                    tdt = p.o("tensor", t_dt, "tE")
                    p.w("scalar", "tE", tdt)
                    p.w("scalar", "vE", p.cnt["vE"])
                    p.o("scalar", lambda e, bank=bank: e.activation(
                        wk1[0:64, :], bank[0:64, :], AF.Abs, bias=dtb_sb[0:64, :]),
                        "sE")
                    p.w("scalar", "sE")
                    p.o("scalar", lambda e: e.activation(
                        wk2[0:64, :], wk1[0:64, :], AF.Exp, scale=-1.0), "sE")
                    p.w("scalar", "sE")
                    p.o("scalar", lambda e: e.activation(
                        wk3[0:64, :], wk2[0:64, :], AF.Ln, bias=cst[0:64, 2:3]), "sE")
                    p.w("scalar", "sE")
                    relc[cb] = p.o("scalar", lambda e, bank=bank: e.activation(
                        wk1[0:64, :], bank[0:64, :], AF.Relu, bias=dtb_sb[0:64, :]),
                        "sE")
                    p.w("vector", "sE", relc[cb])
                    p.o("vector", lambda e: e.tensor_tensor(
                        wk2[0:64, :], wk3[0:64, :], wk1[0:64, :], OP.add), "vE")
                    p.w("vector", "vE")
                    p.o("vector", lambda e, cb=cb: e.tensor_scalar(
                        a_dt[0:64, cb * 512:(cb + 1) * 512], wk2[0:64, :],
                        negea[0:64, 0:1], None, OP.mult), "vE")
                p.w("vector", "vE")
                for c in range(NCH):
                    p.o("vector", lambda e, c=c: e.tensor_tensor_scan(
                        acs[0:64, c * CK:(c + 1) * CK], ones64f[0:64, :],
                        a_dt[0:64, c * CK:(c + 1) * CK], 0.0, OP.mult, OP.add), "vE")
                st4["vacs"] = p.cnt["vE"]
                p.w("vector", "vE")
                p.w("vector", "sE")
                p.o("vector", lambda e: e.memset(h32[:, :], 0.0), "vE")
                p.w("vector", "vE")
                p.o("vector", lambda e: e.memset(hbf[:, :], 0.0), "vE")
                st4["hbfc"] = [p.cnt["vE"]] * 8
                if debug:
                    p.w("sync", "vE")
                    gD.issue("sync", dbg["dbg_acs"], acs[0:64, :])

            def ph4_chunk(c):
                sc, off = c // 2, (c % 2) * CK
                yield
                if c % 2 == 0:
                    if sc >= 1:
                        for s_, v_ in st4["sccons"][sc - 1]:
                            p.w("sync", s_, v_)
                    NSUB = 4
                    TW = 48 // NSUB
                    subs = []
                    for i in range(NSUB):
                        subs.append(gS.issue(
                            "sync",
                            cact[0][:, i * TW * SC:(i + 1) * TW * SC].rearrange(
                                "p (m t) -> p m t", t=SC),
                            craw_s[i * TW:(i + 1) * TW, :,
                                   sc * SC:(sc + 1) * SC].transpose([1, 0, 2])))
                    st4["scl"][sc] = [None,
                                      gS.issue("sync",
                                               gact[0][:, :].rearrange(
                                                   "p (m t) -> p m t", t=SC),
                                               gate_s[:, :, sc * SC:(sc + 1) * SC]
                                               .transpose([1, 0, 2]))]
                    sils = []
                    for i in range(NSUB):
                        p.w("vector", *subs[i])
                        p.w("vector", "vE")
                        vcv = p.o("vector", lambda e, sc=sc, i=i, TW=TW:
                                  e.tensor_tensor(
                            cact[0][:, i * TW * SC:(i + 1) * TW * SC].rearrange(
                                "p (m t) -> p m t", t=SC),
                            cact[0][:, i * TW * SC:(i + 1) * TW * SC].rearrange(
                                "p (m t) -> p m t", t=SC),
                            crmsb[:, sc * SC:(sc + 1) * SC].unsqueeze(1)
                            .broadcast_to([128, TW, SC]), OP.mult), "vE")
                        p.w("scalar", "vE", vcv)
                        sils.append(p.o("scalar", lambda e, i=i, TW=TW: e.activation(
                            cact[0][:, i * TW * SC:(i + 1) * TW * SC],
                            cact[0][:, i * TW * SC:(i + 1) * TW * SC], AF.Silu),
                            "sE"))
                    st4["silc"][sc] = sils

                def csl(m, width=CK):
                    return cact[0][:, m * SC + off:m * SC + off + width]

                acs_c = acs[0:64, c * CK:(c + 1) * CK]
                p.w("scalar", "tE", st4["tlast"])
                p.w("scalar", "sE")
                p.o("scalar", lambda e, acs_c=acs_c: e.activation(
                    exp_t[0:64, :], acs_c, AF.Exp), "sE")
                p.o("scalar", lambda e, acs_c=acs_c, c=c: e.activation(
                    dec_t[0:64, :], acs_c, AF.Exp, scale=-1.0,
                    bias=acs[0:64, c * CK + CK - 1:c * CK + CK]), "sE")
                sa = p.o("scalar", lambda e, c=c: e.activation(
                    awk[0:64, 0:1], acs[0:64, c * CK + CK - 1:c * CK + CK], AF.Exp),
                    "sE")
                p.w("vector", "tE", st4["tlast"])
                p.w("vector", "vE")
                p.o("vector", lambda e, acs_c=acs_c: e.tensor_copy(
                    out=hi_t[0:64, :], in_=acs_c), "vE")
                p.w("vector", "vE")
                p.o("vector", lambda e, acs_c=acs_c: e.tensor_tensor(
                    lo_t[0:64, :], acs_c, hi_t[0:64, :], OP.subtract), "vE")
                p.w("vector", "sE", sa)
                vcdl = p.o("vector", lambda e: e.tensor_scalar(
                    cdl_t[0:64, :], ones64[0:64, :], awk[0:64, 0:1], None, OP.mult),
                    "vE")
                satl = p.cnt["sE"]
                vtl = p.cnt["vE"]

                silv = st4["silc"][sc]
                bmap = {0: 0, 1: 1, 2: 2, 3: 2, 4: 3}
                if c >= 1:
                    p.w("tensor", "vE", st4["prev_vyg"])
                    p.w("tensor", "sE", st4["lex01"])
                    for pr in st4["ygstore"][c - 1]:
                        p.w("vector", *pr)
                srcs = ([csl(m) for m in range(32)] + [csl(32 + g) for g in range(8)]
                        + [exp_t[0:64, :], cdl_t[0:64, :], dec_t[0:64, :]])
                dsts = ([xtok[:, m * 128:(m + 1) * 128] for m in range(32)] +
                        [btok[:, g * 128:(g + 1) * 128] for g in range(8)] +
                        [exp_tok[:, :], cdx[:, :], dec_tok[:, :]])
                tevb = {}
                for bi, b0 in enumerate(range(0, 43, 8)):
                    nb = min(8, 43 - b0)
                    bank = 4 + (bi % 4)
                    pTb = pb[bank][:, :].bitcast(BF16)
                    if bi < 5:
                        p.w("tensor", "sE", silv[bmap[bi]])
                    else:
                        p.w("tensor", "sE", satl)
                        p.w("tensor", "vE", vtl)
                    if bi >= 4:
                        p.w("tensor", *tevb[bi - 4])

                    def t_trb(e, b0=b0, nb=nb, srcs=srcs, pT=pTb):
                        last = None
                        for j in range(nb):
                            s_ = srcs[b0 + j]
                            if s_.shape[0] == 64:
                                last = e.transpose(pT[:, j * 128:j * 128 + 64],
                                                   s_, id64[0:64, :])
                            else:
                                last = e.transpose(pT[:, j * 128:(j + 1) * 128],
                                                   s_, idb[:, :])
                        return last
                    ttb = p.o("tensor", t_trb, "tE")
                    ee = "scalar" if bi % 2 == 0 else "vector"
                    p.w(ee, "tE", ttb)
                    if bi < 2 and c >= 1:
                        p.w(ee, "vE", st4["prev_vyg"])

                    def x_trev(e, b0=b0, nb=nb, dsts=dsts, pT=pTb, ee=ee):
                        last = None
                        for j in range(nb):
                            d_ = dsts[b0 + j]
                            w_ = d_.shape[-1]
                            if ee == "scalar":
                                last = e.activation(d_, pT[:, j * 128:j * 128 + w_],
                                                    AF.Copy)
                            else:
                                last = e.tensor_copy(
                                    out=d_, in_=pT[:, j * 128:j * 128 + w_])
                        return last
                    tevb[bi] = ("sE" if ee == "scalar" else "vE",
                                p.o(ee, x_trev, "sE" if ee == "scalar" else "vE"))
                sevf = max(v for k, v in tevb.values() if k == "sE")
                vevf = max(v for k, v in tevb.values() if k == "vE")
                p.w("tensor", "sE", sevf)
                p.w("tensor", "vE", vevf)
                p.w("vector", "sE", sevf)
                p.w("vector", "vE", vevf)
                p.w("scalar", "vE", vevf)
                yield

                for i in range(9):
                    mcb_prev = st4["mcb01"]
                    if i < 8:
                        g = i
                        gsl = slice(g * 512, (g + 1) * 512)
                        bAP, cAP = csl(32 + g), csl(40 + g)
                        p.w("tensor", "sE", st4["cbev"])

                        def t_cb(e, bAP=bAP, cAP=cAP):
                            return e.matmul(pb[3][:, 0:128], bAP, cAP,
                                            start=True, stop=True)
                        tcb = p.o("tensor", t_cb, "tE")
                        p.w("vector", "vE")
                        decr = dec_tok[:, 8 * g:8 * g + 8].unsqueeze(2) \
                            .broadcast_to([128, 8, 64])
                        st4["xdec"][g] = p.o("vector", lambda e, gsl=gsl, decr=decr:
                                             e.tensor_tensor(
                            xdec[:, gsl].rearrange("p (j d) -> p j d", d=64),
                            xtok[:, gsl].rearrange("p (j d) -> p j d", d=64),
                            decr, OP.mult), "vE")
                        hi = hi_t[0:64, :]
                        lo = lo_t[0:64, :]

                        def t_lm(e, g=g, q=0, hi=hi, lo=lo, bank=4):
                            for j in range(4):
                                h = 8 * g + 4 * q + j
                                selh = id64[0:64, h:h + 1].broadcast_to([64, 128])
                                nselh = nid64[0:64, h:h + 1].broadcast_to([64, 128])
                                sl = pb[bank][:, j * 128:(j + 1) * 128]
                                e.matmul(sl, selh, hi, start=(j == 0), stop=False)
                                e.matmul(sl, selh, lo, start=False, stop=False)
                                e.matmul(sl, hi, nselh, start=False, stop=False)
                                e.matmul(sl, lo, nselh, start=False, stop=False)
                            c2r = c2m[:, :].unsqueeze(1).broadcast_to([128, 4, 128])
                            return e.matmul(
                                pb[bank][:, 0:512].rearrange("p (j t) -> p j t",
                                                             t=128),
                                c1m[:, :], c2r, start=False, stop=True)

                        p.w("tensor", "sE", st4["lex01"])
                        tl0 = p.o("tensor", lambda e, f=t_lm: f(e, q=0, bank=4),
                                  "tE")
                        tl1 = p.o("tensor", lambda e, f=t_lm: f(e, q=1, bank=7),
                                  "tE")
                        p.w("scalar", "tE", tcb)
                        p.w("scalar", "vE", st4["mcb01_2"])
                        st4["cbev"] = p.o("scalar", lambda e, g=g: e.activation(
                            cbs[g % 2][:, :], pb[3][:, 0:128], AF.Copy), "sE")
                        p.w("scalar", "tE", tl1)
                        p.w("scalar", "vE", mcb_prev)

                        def s_lex(e):
                            e.activation(lexp[:, :], pb[4][:, 0:512], AF.Exp)
                            return e.activation(eas[:, :], pb[7][:, 0:512], AF.Exp)
                        st4["lex01"] = p.o("scalar", s_lex, "sE")
                    if i >= 1:
                        g2 = i - 1
                        gsl2 = slice(g2 * 512, (g2 + 1) * 512)
                        cAP2 = csl(40 + g2)
                        p.w("tensor", "vE", mcb_prev)
                        p.w("tensor", "vE", st4["yadd"][g2 - 1])

                        def t_yd(e, g2=g2):
                            last = None
                            for h8 in range(8):
                                last = e.matmul(
                                    pb[5][:, h8 * 64:(h8 + 1) * 64],
                                    mcbq[h8 // 4][:, (h8 % 4) * 128:
                                                  (h8 % 4 + 1) * 128],
                                    xtok[:, g2 * 512 + h8 * 64:
                                         g2 * 512 + (h8 + 1) * 64],
                                    start=True, stop=True)
                            return last
                        st4["ydq"] = p.o("tensor", t_yd, "tE")
                        p.w("tensor", "vE", st4["xdec"][g2])
                        p.w("tensor", "vE", st4["hadd"][g2 - 1])

                        def t_st(e, g2=g2, gsl2=gsl2):
                            return e.matmul(pb[2][:, :],
                                            btok[:, g2 * 128:(g2 + 1) * 128],
                                            xdec[:, gsl2], start=True, stop=True)
                        st4["states"][g2] = p.o("tensor", t_st, "tE")
                        p.w("tensor", "vE", st4["ysc"][g2 - 1])
                        p.w("tensor", "vE", st4["hbfc"][g2])

                        def t_yc(e, cAP2=cAP2, gsl2=gsl2):
                            return e.matmul(pb[6][:, :], cAP2, hbf[:, gsl2],
                                            start=True, stop=True)
                        tyc = p.o("tensor", t_yc, "tE")
                    if i < 8:
                        p.w("vector", "sE", st4["lex01"])
                        p.w("vector", "tE", st4["ydq"])
                        cbr = cbs[g % 2][:, :].unsqueeze(1) \
                            .broadcast_to([128, 4, 128])

                        def v_mcb(e, cbr=cbr):
                            e.tensor_tensor(
                                mcbq[0][:, :].rearrange("p (j t) -> p j t", t=128),
                                lexp[:, :].rearrange("p (j t) -> p j t", t=128),
                                cbr, OP.mult)
                            return e.tensor_tensor(
                                mcbq[1][:, :].rearrange("p (j t) -> p j t", t=128),
                                eas[:, :].rearrange("p (j t) -> p j t", t=128),
                                cbr, OP.mult)
                        st4["mcb01_2"] = mcb_prev
                        st4["mcb01"] = p.o("vector", v_mcb, "vE")
                    if i >= 1:
                        p.w("vector", "vE")
                        cdr = cdx[:, 8 * g2:8 * g2 + 8].unsqueeze(2).broadcast_to(
                            [128, 8, 64])
                        st4["htmp"][g2] = p.o("vector", lambda e, gsl2=gsl2,
                                              cdr=cdr: e.tensor_tensor(
                            htmp[:, :].rearrange("p (j d) -> p j d", d=64),
                            h32[:, gsl2].rearrange("p (j d) -> p j d", d=64),
                            cdr, OP.mult), "vE")
                        p.w("vector", "tE", tyc)
                        p.w("vector", "vE")
                        exr = exp_tok[:, 8 * g2:8 * g2 + 8].unsqueeze(2) \
                            .broadcast_to([128, 8, 64])
                        st4["ysc"][g2] = p.o("vector", lambda e, exr=exr:
                                             e.tensor_tensor(
                            ysc[:, :].rearrange("p (j d) -> p j d", d=64),
                            pb[6][:, :].rearrange("p (j d) -> p j d", d=64),
                            exr, OP.mult), "vE")
                        p.w("vector", "tE", st4["ydq"])
                        p.w("vector", "vE")
                        st4["yadd"][g2] = p.o("vector", lambda e, gsl2=gsl2:
                                              e.tensor_tensor(
                            ytok[:, gsl2], pb[5][:, :], ysc[:, :], OP.add), "vE")
                        p.w("vector", "tE", st4["states"][g2])
                        p.w("vector", "vE")
                        st4["hadd"][g2] = p.o("vector", lambda e, gsl2=gsl2:
                                              e.tensor_tensor(
                            h32[:, gsl2], htmp[:, :], pb[2][:, :], OP.add), "vE")
                        p.w("vector", "vE")
                        st4["hbfc"][g2] = p.o("vector", lambda e, gsl2=gsl2:
                                              e.tensor_copy(
                            out=hbf[:, gsl2], in_=h32[:, gsl2]), "vE")
                    if i == 5:
                        p.w("scalar", "vE", st4["yadd"][3])
                        p.w("scalar", "tE", st4["states"][3])
                        sqh0 = p.o("scalar", lambda e: e.activation(
                            xdec[:, 0:2048], ytok[:, 0:2048], AF.Square), "sE")
                        p.w("vector", "sE", sqh0)
                        p.w("vector", "vE")
                        p.o("vector", lambda e: e.reduce_sum(
                            yrms[:, 4:5], xdec[:, 0:2048], axis=AX.X), "vE")
                    yield

                ydump = []
                if debug:
                    p.w("sync", "vE", st4["yadd"][7])
                    ydump.append(gD.issue("sync", dbg["dbg_y"][c], ytok[:, :]))
                p.w("scalar", "vE", st4["yadd"][7])
                p.w("scalar", "tE", st4["states"][7])
                ssq = p.o("scalar", lambda e: e.activation(
                    xdec[:, 2048:4096], ytok[:, 2048:4096], AF.Square), "sE")
                p.w("vector", "sE", ssq)
                p.o("vector", lambda e: e.reduce_sum(
                    yrms[:, 5:6], xdec[:, 2048:4096], axis=AX.X), "vE")
                p.w("vector", "vE")
                p.o("vector", lambda e: e.tensor_tensor(
                    yrms[:, 0:1], yrms[:, 4:5], yrms[:, 5:6], OP.add), "vE")
                p.w("vector", "vE")
                p.o("vector", lambda e: e.tensor_scalar(
                    yrms[:, 1:2], yrms[:, 0:1], 1.0 / DIN, 1e-6, OP.mult, OP.add),
                    "vE")
                p.w("scalar", "vE", p.cnt["vE"])
                p.o("scalar", lambda e: e.activation(yrms[:, 2:3], yrms[:, 1:2],
                                                     AF.Ln), "sE")
                p.w("scalar", "sE")
                p.o("scalar", lambda e: e.activation(yrms[:, 3:4], yrms[:, 2:3],
                                                     AF.Exp, scale=-0.5), "sE")
                p.w("scalar", "sE")
                syn = p.o("scalar", lambda e: e.activation(
                    xdec[:, :], ytok[:, :], AF.Copy, scale=yrms[:, 3:4]), "sE")
                tevb2 = {}
                for bi, b0 in enumerate(range(0, 32, 8)):
                    bank = 4 + bi
                    pTb = pb[bank][:, :].bitcast(BF16)
                    if bi == 0:
                        p.w("tensor", "sE", syn)
                        p.w("tensor", "vE", st4["hadd"][7])

                    def t_trb(e, b0=b0, pT=pTb):
                        last = None
                        for j in range(8):
                            last = e.transpose(pT[:, j * 128:(j + 1) * 128],
                                               xdec[:, (b0 + j) * 128:(b0 + j + 1) * 128],
                                               idb[:, :])
                        return last
                    ttb = p.o("tensor", t_trb, "tE")
                    ee = "scalar" if bi % 2 == 0 else "vector"
                    p.w(ee, "tE", ttb)
                    if bi < 2:
                        p.w(ee, "sE" if ee == "vector" else "vE",
                            syn if ee == "vector" else st4["hadd"][7])

                    def x_trev(e, b0=b0, pT=pTb, ee=ee):
                        last = None
                        for j in range(8):
                            d_ = xtok[:, (b0 + j) * 128:(b0 + j + 1) * 128]
                            if ee == "scalar":
                                last = e.activation(d_, pT[:, j * 128:(j + 1) * 128],
                                                    AF.Copy)
                            else:
                                last = e.tensor_copy(
                                    out=d_, in_=pT[:, j * 128:(j + 1) * 128])
                        return last
                    tevb2[bi] = ("sE" if ee == "scalar" else "vE",
                                 p.o(ee, x_trev, "sE" if ee == "scalar" else "vE"))
                p.w("vector", "sE", max(v for k, v in tevb2.values() if k == "sE"))
                p.w("vector", *st4["scl"][sc][1])
                for pr in ydump:
                    p.w("vector", *pr)
                p.w("vector", "sE", syn)
                p.w("vector", "vE")
                vyg = p.o("vector", lambda e, off=off: e.tensor_tensor(
                    ytok3, xtok[:, :].rearrange("p (m t) -> p m t", t=128),
                    gact[0][:, :].rearrange("p (m t) -> p m t", t=SC)[:, :,
                                                                     off:off + 128],
                    OP.mult), "vE")
                st4["prev_vyg"] = vyg
                p.w("sync", "vE", vyg)
                prs = [gW.issue("sync", yg_s[c].transpose([1, 0, 2]),
                                ytok[:, :].rearrange("p (m t) -> p m t", t=128))]
                if debug:
                    prs.append(gD.issue("sync", dbg["dbg_yg"][c], ytok[:, :]))
                st4["ygstore"].append(prs)
                st4["tlast"] = p.cnt["tE"]
                if c % 2 == 1:
                    st4["sccons"][sc] = [("tE", p.cnt["tE"]), ("vE", p.cnt["vE"]),
                                         ("sE", p.cnt["sE"])]

            ph4_prelude()
            gens = [ph4_chunk(c) for c in range(NCH)]
            todo = [s for g_ in gens for s in [g_]]

        NU = DFF // 128
        gi = 0
        for m in range(NU):
            ph3_unit(m)
            if n_phases >= 5:
                for _ in range(3):
                    if gi < len(todo):
                        try:
                            next(todo[gi])
                        except StopIteration:
                            gi += 1
        if n_phases >= 5:
            while gi < len(todo):
                try:
                    next(todo[gi])
                except StopIteration:
                    gi += 1
        gW.waitall("sync")
        gD.waitall("sync")

    if n_phases >= 6:
        p.barrier()
        ablock = arena[:, 0:131072].bitcast(BF16).rearrange("p (k t) -> p k t", t=1024)
        o1sem = []
        for tb2 in range(2):
            if tb2 == 1:
                p.w("sync", "vE", p.cnt["vE"])
                p.w("sync", "tE", p.cnt["tE"])
                p.w("gpsimd", "tE", p.cnt["tE"])
            w2sem, t5, v5 = [], [], []

            def w2cast(m):
                src_ap = w2t[:, m * 128:(m + 1) * 128].rearrange(
                    "(k p) c -> p k c", p=128)
                dst_ap = w2slab[m % 2][:, :].rearrange("p (k c) -> p k c", c=128)
                w2sem.append(gG.issue("gpsimd", dst_ap, src_ap))
            w2cast(0)
            w2cast(1)
            asem = [gS.issue("sync", ablock[:, 0:32, :],
                             a_s[0:32, :, tb2 * 1024:(tb2 + 1) * 1024]
                             .transpose([1, 0, 2])),
                    gP.issue("gpsimd", ablock[:, 32:64, :],
                             a_s[32:64, :, tb2 * 1024:(tb2 + 1) * 1024]
                             .transpose([1, 0, 2]))]
            for m in range(NK):
                if m >= 2:
                    p.w("gpsimd", "tE", t5[m - 2])
                    w2cast(m)
                if m >= 1:
                    p.w("sync", "vE", v5[m - 1])
                rse = gS.issue("sync", rchunk[:, 0:1024],
                               r_s[m][:, tb2 * 1024:(tb2 + 1) * 1024])
                p.w("tensor", *w2sem[m])
                if m >= 1:
                    p.w("tensor", "vE", v5[m - 1])

                def t_mm(e, m=m, k0=0, k1=64):
                    last = None
                    for k in range(k0, k1):
                        for tb in range(2):
                            last = e.matmul(
                                pb[2 * (m % 2) + tb][:, :],
                                w2slab[m % 2][:, k * 128:(k + 1) * 128],
                                atile[k][:, tb * 512:(tb + 1) * 512],
                                start=(k == 0), stop=(k == 63))
                    return last
                if m == 0:
                    p.w("tensor", *asem[0])
                    p.o("tensor", lambda e, f=t_mm: f(e, k0=0, k1=32), "tE")
                    p.w("tensor", *asem[1])
                    t5.append(p.o("tensor", lambda e, f=t_mm: f(e, k0=32, k1=64),
                                  "tE"))
                else:
                    t5.append(p.o("tensor", t_mm, "tE"))
                p.w("vector", "tE", t5[m])
                p.w("vector", *rse)
                if o1sem:
                    p.w("vector", *o1sem[-1])
                p.w("vector", "vE")

                def v_ev(e, m=m):
                    last = None
                    for tb in range(2):
                        last = e.tensor_tensor(
                            ochunk[:, tb * 512:(tb + 1) * 512],
                            pb[2 * (m % 2) + tb][:, :],
                            rchunk[:, tb * 512:(tb + 1) * 512], OP.add)
                    return last
                v5.append(p.o("vector", v_ev, "vE"))
                p.w("sync", "vE", v5[m])
                o1sem.append(gW.issue("sync",
                                      out1_s[m][:, tb2 * 1024:(tb2 + 1) * 1024],
                                      ochunk[:, 0:1024]))
        gW.waitall("sync")

    if n_phases >= 7:
        p.barrier()
        wosem, t6, v6, f6, ostores = [], [], [], [], []

        def wocast(m):
            src_ap = wot[:, m * 128:(m + 1) * 128].rearrange(
                "(k p) c -> p k c", p=128)
            dst_ap = woslab[m % 2][:, :].rearrange("p (k c) -> p k c", c=128)
            wosem.append(gG.issue("gpsimd", dst_ap, src_ap))
        wocast(0)
        wocast(1)
        ygsems = []
        for kf in range(32):
            ygsems.append((gS if kf % 2 == 0 else gP).issue(
                "sync" if kf % 2 == 0 else "gpsimd",
                ygt[kf][:, :].rearrange("p (c t) -> p c t", t=128),
                yg_s[:, kf].transpose([1, 0, 2])))
        for m in range(NK):
            if m >= 2:
                p.w("gpsimd", "vE", f6[m - 2])
                p.w("gpsimd", "tE", t6[m - 2])
                wocast(m)
            p.w("vector", *wosem[m])
            if m >= 2:
                p.w("vector", "tE", t6[m - 2])

            def v_fold(e, m=m):
                return e.tensor_tensor(woslab[m % 2][:, :], woslab[m % 2][:, :],
                                       outnw_slab[:, :], OP.mult)
            f6.append(p.o("vector", v_fold, "vE"))
            p.w("tensor", "vE", f6[m])
            if m == 0:
                for pr in ygsems:
                    p.w("tensor", *pr)
            if m >= 1:
                p.w("tensor", "vE", v6[m - 1])

            def t_mm(e, m=m):
                last = None
                for k in range(32):
                    for tb in range(4):
                        last = e.matmul(pb[tb][:, :],
                                        woslab[m % 2][:, k * 128:(k + 1) * 128],
                                        ygt[k][:, tb * 512:(tb + 1) * 512],
                                        start=(k == 0), stop=(k == 31))
                return last
            t6.append(p.o("tensor", t_mm, "tE"))
            if m >= 1:
                p.w("sync", "vE", v6[m - 1])
            ose = gS.issue("sync", rchunk[:, 0:1024], out1_s[m][:, 0:1024])
            ose2 = gS.issue("sync", rchunk[:, 1024:2048], out1_s[m][:, 1024:2048])
            p.w("vector", "tE", t6[m])
            p.w("vector", *ose)
            p.w("vector", *ose2)
            if ostores:
                p.w("vector", *ostores[-1])
            p.w("vector", "vE")

            def v_fin(e, m=m):
                last = None
                for tb in range(4):
                    last = e.tensor_tensor(ochunk[:, tb * 512:(tb + 1) * 512],
                                           pb[tb][:, :],
                                           rchunk[:, tb * 512:(tb + 1) * 512], OP.add)
                return last
            v6.append(p.o("vector", v_fin, "vE"))
            p.w("sync", "vE", v6[m])
            ostores.append(gW.issue("sync", outT[m * 128:(m + 1) * 128, :],
                                    ochunk[:, :]))
        gW.waitall("sync")
        gW.waitall("vector")

    with ExitStack() as st:
        sems = {n: st.enter_context(nc.semaphore(n)) for n in p.names}
        block = st.enter_context(nc.Block())
        for ename in ENGINES:
            def mk(en):
                def go(eng):
                    for it in p.q[en]:
                        if it[0] == "w":
                            eng.wait_ge(sems[it[1]], it[2])
                        else:
                            inst = it[1](eng)
                            if it[2] is not None:
                                assert inst is not None, f"op returned None on {en}"
                                inst.then_inc(sems[it[2]], it[3])
                return go
            getattr(block, ename)(mk(ename))
    return nc


def _prep_inputs(L0_attn_x, L0_attn_norm_w, L0_attn_in_proj_w, L0_attn_conv_w,
                 L0_attn_conv_b, L0_attn_conv_norm_w, L0_attn_A_log, L0_attn_dt_w,
                 L0_attn_dt_b, L0_attn_out_norm_w, L0_attn_out_proj_w,
                 L0_ffn_W1, L0_ffn_W2):
    f = np.float32
    x = np.asarray(L0_attn_x, f)
    w_in = np.asarray(L0_attn_in_proj_w, f)
    perm = np.concatenate([
        np.arange(0, DIN),
        np.arange(DIN + H, DIN + H + GN),
        np.arange(DIN + H + GN, DIN + H + 2 * GN),
        np.arange(DIN + H + 2 * GN, NPROJ),
        np.arange(DIN, DIN + H),
    ])
    shared = {
        "w_in": np.ascontiguousarray(w_in[perm].T),
        "w1t": np.ascontiguousarray(np.asarray(L0_ffn_W1, f).T),
        "w2t": np.ascontiguousarray(np.asarray(L0_ffn_W2, f).T),
        "wot": np.ascontiguousarray(np.asarray(L0_attn_out_proj_w, f).T),
        "convw": np.ascontiguousarray(np.asarray(L0_attn_conv_w, f).reshape(CONV, 4)),
        "convb": np.ascontiguousarray(np.asarray(L0_attn_conv_b, f)),
        "convnw": np.ascontiguousarray(np.asarray(L0_attn_conv_norm_w, f)),
        "normw": np.ascontiguousarray(np.asarray(L0_attn_norm_w, f)),
        "outnw": np.ascontiguousarray(np.asarray(L0_attn_out_norm_w, f)),
        "alog": np.ascontiguousarray(np.asarray(L0_attn_A_log, f)),
        "dtb": np.ascontiguousarray(np.asarray(L0_attn_dt_b, f)),
        "dtwt": np.ascontiguousarray(np.asarray(L0_attn_dt_w, f).T),
    }
    in_maps = []
    for c in range(8):
        m = dict(shared)
        m["xT"] = np.ascontiguousarray(x[c].T)
        in_maps.append(m)
    return in_maps


def kernel(**inputs):
    in_maps = _prep_inputs(**inputs)
    nc = build(N_PHASES, DEBUG)
    res = run_bass_kernel_spmd(nc, in_maps, core_ids=list(range(8)))
    out = np.stack([np.asarray(r["outT"]).T for r in res.results], axis=0)
    return np.ascontiguousarray(out.astype(np.float32))
